# version 09
# baseline (speedup 1.0000x reference)
import contextlib
import numpy as np
import ml_dtypes
import concourse.bass as bass
import concourse.mybir as mybir
from concourse.bass_utils import run_bass_kernel_spmd

F32 = mybir.dt.float32
BF16 = mybir.dt.bfloat16
FP8 = mybir.dt.float8e4
AF = mybir.ActivationFunctionType
ALU = mybir.AluOpType
BF = ml_dtypes.bfloat16

T = 2048
DM = 2048
DI = 4096
NCH = 32


class Ctr:
    def __init__(self, handle):
        self.h = handle
        self.n = 0


def build_program(upto=99, p1mode=1, p1n=67, dbg=False):
    nc = bass.Bass("TRN2", target_bir_lowering=False, detect_race_conditions=False)

    d_u = nc.dram_tensor("u", [T, DM], F32, kind="ExternalInput")
    d_win = nc.dram_tensor("w_in", [DM, 8512], F32, kind="ExternalInput")
    d_wout = nc.dram_tensor("w_out", [DI, DM], F32, kind="ExternalInput")
    d_wffn = nc.dram_tensor("w_ffn", [DM, DM], F32, kind="ExternalInput")
    d_cw = nc.dram_tensor("cw", [128, 136], F32, kind="ExternalInput")
    d_cb = nc.dram_tensor("cb", [128, 34], F32, kind="ExternalInput")
    d_dtb = nc.dram_tensor("dtb", [64, 1], F32, kind="ExternalInput")
    d_alog = nc.dram_tensor("alog", [64, 1], F32, kind="ExternalInput")
    d_ddiag = nc.dram_tensor("ddiag", [64, DI], BF16, kind="ExternalInput")
    d_nw = nc.dram_tensor("nw", [128, 32], F32, kind="ExternalInput")
    d_identb = nc.dram_tensor("identb", [128, 128], BF16, kind="ExternalInput")
    d_identf = nc.dram_tensor("identf", [128, 128], F32, kind="ExternalInput")
    d_ut = nc.dram_tensor("ut", [64, 64], F32, kind="ExternalInput")
    d_ones = nc.dram_tensor("ones128", [128, 1], BF16, kind="ExternalInput")
    d_onesr = nc.dram_tensor("onesrow", [1, 128], BF16, kind="ExternalInput")
    d_onesf = nc.dram_tensor("onesf", [64, 128], F32, kind="ExternalInput")
    d_out = nc.dram_tensor("out", [T, DM], F32, kind="ExternalOutput")

    KIND = "ExternalOutput" if dbg else "Internal"
    d_ubf = nc.dram_tensor("ubf_scr", [T, DM], BF16)
    d_z = nc.dram_tensor("z_scr", [DI, T], BF16, kind=KIND)
    d_xc = nc.dram_tensor("xc_scr", [DI, T], BF16, kind=KIND)
    d_y = nc.dram_tensor("y_scr", [DI, T], BF16, kind=KIND)
    d_rr = nc.dram_tensor("rr_scr", [T], F32, kind=KIND)
    d_x2 = nc.dram_tensor("x2_scr", [T, DM], BF16, kind=KIND)
    d_x2q = nc.dram_tensor("x2q_scr", [T, DM], BF16, kind=KIND)
    d_wdq = nc.dram_tensor("wdq_scr", [DM, DM], BF16, kind=KIND)
    d_wqT = nc.dram_tensor("wqT_scr", [DM, DM], BF16)
    d_arep = nc.dram_tensor("arep_scr", [2, DI], F32)
    d_ea = nc.dram_tensor("ea_scr", [2, DI], F32)
    d_cbm = nc.dram_tensor("cbm_scr", [2, DI], F32)

    st = contextlib.ExitStack()
    with st:
        ps_a = st.enter_context(nc.psum_tensor("ps_a", [128, 2048], F32))
        ps_b = st.enter_context(nc.psum_tensor("ps_b", [128, 2048], F32))

        S = {}
        for name in ["sy", "gp", "tm", "vv", "ss", "ph", "psf", "yfree",
                     "stfree", "b6", "b7", "spl"]:
            S[name] = Ctr(st.enter_context(nc.semaphore(name)))
        sy, gp, tm, vv, ss = S["sy"], S["gp"], S["tm"], S["vv"], S["ss"]
        ph, psf, yfree = S["ph"], S["psf"], S["yfree"]
        stfree, b6, b7, spl = S["stfree"], S["b6"], S["b7"], S["spl"]

        sb_identb = st.enter_context(nc.sbuf_tensor("c_identb", [128, 128], BF16))
        sb_identf = st.enter_context(nc.sbuf_tensor("c_identf", [128, 128], F32))
        sb_ut = st.enter_context(nc.sbuf_tensor("c_ut", [64, 64], F32))
        sb_ones = st.enter_context(nc.sbuf_tensor("c_ones", [128, 1], BF16))
        sb_onesr = st.enter_context(nc.sbuf_tensor("c_onesr", [1, 128], BF16))
        sb_onesf = st.enter_context(nc.sbuf_tensor("c_onesf", [64, 128], F32))
        sb_nw = st.enter_context(nc.sbuf_tensor("c_nw", [128, 32], F32))
        sb_dtb = st.enter_context(nc.sbuf_tensor("c_dtb", [64, 1], F32))
        sb_alog = st.enter_context(nc.sbuf_tensor("c_alog", [64, 1], F32))
        sb_acol = st.enter_context(nc.sbuf_tensor("c_acol", [64, 1], F32))
        sb_cw = st.enter_context(nc.sbuf_tensor("c_cw", [128, 136], F32))
        sb_cb = st.enter_context(nc.sbuf_tensor("c_cb", [128, 34], F32))
        sb_eps = st.enter_context(nc.sbuf_tensor("c_eps", [128, 1], F32))
        sb_onef = st.enter_context(nc.sbuf_tensor("c_onef", [128, 1], F32))

        prog = {k: [] for k in ["sync", "tensor", "vector", "scalar", "gpsimd"]}
        MUTE = [False]

        def em(eng, fn):
            if not MUTE[0]:
                prog[eng].append(fn)

        def barrier():
            target = ph.n + 5
            for eng in ["sync", "tensor", "vector", "scalar", "gpsimd"]:
                def _f(e, _t=target):
                    e.drain()
                    e.nop().then_inc(ph.h, 1)
                    e.wait_ge(ph.h, _t)
                em(eng, _f)
            ph.n = target

        rec = {}

        p12 = contextlib.ExitStack()
        sb_dt = p12.enter_context(nc.sbuf_tensor("dt_cm", [64, T], F32))
        sb_lndt_tok = p12.enter_context(nc.sbuf_tensor("lndt_tok", [64, 2048], F32))
        sb_adt_tok = p12.enter_context(nc.sbuf_tensor("adt_tok", [64, 2048], F32))
        sb_B = p12.enter_context(nc.sbuf_tensor("B_cm", [128, T], BF16))
        sb_C = p12.enter_context(nc.sbuf_tensor("C_cm", [128, T], BF16))

        p1 = contextlib.ExitStack()
        sb_xT = p1.enter_context(nc.sbuf_tensor("xT", [128, 16 * T], BF16))
        sb_u0 = [p1.enter_context(nc.sbuf_tensor(f"u0_{i}", [128, DM], BF16))
                 for i in range(2)]
        sb_wsl = [p1.enter_context(nc.sbuf_tensor(f"wsl{i}", [128, 2048], BF16))
                  for i in range(2)]
        sb_acc = [p1.enter_context(nc.sbuf_tensor(f"acc{i}", [128, T], F32))
                  for i in range(2)]
        sb_cv = [p1.enter_context(nc.sbuf_tensor(f"cv{i}", [128, T], F32))
                 for i in range(2)]
        sb_sp = [p1.enter_context(nc.sbuf_tensor(f"spl{i}", [128, T], BF16))
                 for i in range(2)]
        sb_lncm = p1.enter_context(nc.sbuf_tensor("lndt_cm", [64, T], F32))
        sb_adcm = p1.enter_context(nc.sbuf_tensor("adt_cm", [64, T], F32))

        def _f(sync):
            for dst, src in [(sb_identb, d_identb), (sb_identf, d_identf),
                             (sb_ut, d_ut), (sb_ones, d_ones), (sb_onesr, d_onesr), (sb_onesf, d_onesf),
                             (sb_nw, d_nw), (sb_dtb, d_dtb),
                             (sb_alog, d_alog), (sb_cw, d_cw), (sb_cb, d_cb)]:
                sync.dma_start(dst[:], src.ap()).then_inc(sy.h, 16)
        em("sync", _f)
        sy.n += 11 * 16
        consts_at = sy.n

        gp0 = gp.n
        for i in range(16):
            def _f(g, i=i, gp0=gp0):
                if i >= 2:
                    g.wait_ge(gp.h, gp0 + (i - 2) * 32 + 32)
                g.dma_start(sb_u0[i % 2][:], d_u.ap()[i * 128:(i + 1) * 128, :]).then_inc(gp.h, 16)
                g.wait_ge(gp.h, gp0 + i * 32 + 16)
                g.dma_start(d_ubf.ap()[i * 128:(i + 1) * 128, :], sb_u0[i % 2][:]).then_inc(gp.h, 16)
            em("gpsimd", _f)
            gp.n += 32
        ubf_at = gp.n

        for dci in range(16):
            def _f(sync, dci=dci, ubf_at=ubf_at):
                if dci == 0:
                    sync.wait_ge(gp.h, ubf_at)
                sync.dma_start_transpose(
                    sb_xT[:, dci * T:(dci + 1) * T],
                    d_ubf.ap()[:, dci * 128:(dci + 1) * 128]).then_inc(sy.h, 16)
            em("sync", _f)
            sy.n += 16

        def _f(sync, sy_at=sy.n):
            sync.wait_ge(sy.h, sy_at)
        em("sync", _f)

        def _f(sc, consts_at=consts_at):
            sc.wait_ge(sy.h, consts_at)
            sc.activation(sb_acol[:], sb_alog[:], AF.Exp).then_inc(ss.h, 1)
        em("scalar", _f)
        ss.n += 1

        def _f(v):
            v.memset(sb_eps[:], 1e-5)
            v.memset(sb_onef[:], 1.0).then_inc(vv.h, 1)
        em("vector", _f)
        vv.n += 1

        barrier()
        MUTE[0] = upto < 1

        def _wld(g, c):
            w = 64 if c == 66 else 128
            src = bass.AP(d_win, c * 128, [[8512, 128], [128 * 8512, 16], [1, w]])
            g.dma_start(sb_wsl[c % 2][:, 0:16 * w].rearrange("p (d w) -> p d w", d=16),
                        src).then_inc(gp.h, 16)

        def _f(g):
            _wld(g, 0)
        em("gpsimd", _f)
        gp.n += 16
        wld_base = gp.n
        psf_base1 = psf.n
        last_spill = {0: 0, 1: 0}
        last_silu = {0: 0, 1: 0}

        for c in range(min(67, p1n)):
            w = 64 if c == 66 else 128
            ps = ps_a if c % 2 == 0 else ps_b
            if c + 1 < min(67, p1n):
                def _f(g, c=c, tm_at=tm.n):
                    g.wait_ge(tm.h, tm_at)
                    _wld(g, c + 1)
                em("gpsimd", _f)
                gp.n += 16
            wld_at = wld_base + c * 16

            def _f(t, c=c, w=w, ps=ps, wld_at=wld_at, psf_at=psf_base1 + max(0, c - 1)):
                t.wait_ge(gp.h, wld_at)
                t.wait_ge(psf.h, psf_at)
                if p1mode != 3:
                    mm = None
                    for d in range(16):
                        lhsT = sb_wsl[c % 2][:, d * w:(d + 1) * w]
                        for nb in range(4):
                            mm = t.matmul(ps[0:w, nb * 512:(nb + 1) * 512], lhsT,
                                          sb_xT[:, d * T + nb * 512:d * T + (nb + 1) * 512],
                                          start=(d == 0), stop=(d == 15))
                    mm.then_inc(tm.h, 1)
                else:
                    t.nop().then_inc(tm.h, 1)
            em("tensor", _f)
            tm.n += 1
            mm_done = tm.n

            if c < 32 or (p1mode == 0 and c < 66) or (p1mode in (2, 3) and c < 67):
                def _f(sc, c=c, ps=ps, mm_done=mm_done, spl_at=last_spill[c % 2]):
                    sc.wait_ge(tm.h, mm_done)
                    sc.wait_ge(spl.h, spl_at)
                    if p1mode in (2, 3):
                        sc.memzero(sb_sp[c % 2][:, 0:8]).then_inc(ss.h, 1)
                    else:
                        sc.activation(sb_sp[c % 2][:], ps[:], AF.Copy).then_inc(ss.h, 1)
                    sc.drain()
                    sc.nop().then_inc(psf.h, 1)
                em("scalar", _f)
                ss.n += 1
                psf.n += 1
                s_done = ss.n

                def _f(g, c=c, s_done=s_done):
                    g.wait_ge(ss.h, s_done)
                    if p1mode in (2, 3):
                        g.dma_start(d_z.ap()[0:128, 0:8], sb_sp[c % 2][:, 0:8]).then_inc(spl.h, 16)
                        return
                    dst = d_z if c < 32 else d_xc
                    coff = c if c < 32 else min(c - 32, 31)
                    g.dma_start(dst.ap()[coff * 128:(coff + 1) * 128, :],
                                sb_sp[c % 2][:]).then_inc(spl.h, 16)
                em("gpsimd", _f)
                spl.n += 16
                last_spill[c % 2] = spl.n
            elif c < 66:
                j2 = c - 32

                def _f(sc, c=c, ps=ps, mm_done=mm_done, acc_at=rec.get(("cvv", c - 2), 0)):
                    sc.wait_ge(tm.h, mm_done)
                    sc.wait_ge(vv.h, acc_at)
                    sc.activation(sb_acc[c % 2][:], ps[:], AF.Copy).then_inc(ss.h, 1)
                    sc.drain()
                    sc.nop().then_inc(psf.h, 1)
                em("scalar", _f)
                ss.n += 1
                psf.n += 1
                ev_done = ss.n

                def _f(v, c=c, j2=j2, ev_done=ev_done, silu_at=last_silu[c % 2]):
                    v.wait_ge(ss.h, max(ev_done, silu_at))
                    acc = sb_acc[c % 2]
                    cv = sb_cv[c % 2]
                    v.tensor_scalar(cv[:], acc[:], sb_cw[:, j2 * 4 + 3:j2 * 4 + 4],
                                    sb_cb[:, j2:j2 + 1], op0=ALU.mult, op1=ALU.add)
                    vop = None
                    for wt, off in [(2, 1), (1, 2), (0, 3)]:
                        vop = v.scalar_tensor_tensor(cv[:, off:T], acc[:, 0:T - off],
                                                     sb_cw[:, j2 * 4 + wt:j2 * 4 + wt + 1],
                                                     cv[:, off:T], op0=ALU.mult, op1=ALU.add)
                    vop.then_inc(vv.h, 1)
                em("vector", _f)
                vv.n += 1
                rec[("cvv", c)] = vv.n
                v_done = vv.n

                if c < 64:
                    def _f(sc, c=c, v_done=v_done, spl_at=last_spill[c % 2]):
                        sc.wait_ge(vv.h, v_done)
                        sc.wait_ge(spl.h, spl_at)
                        sc.activation(sb_sp[c % 2][:], sb_cv[c % 2][:], AF.Silu).then_inc(ss.h, 1)
                    em("scalar", _f)
                    ss.n += 1
                    last_silu[c % 2] = ss.n
                    s_done = ss.n

                    def _f(g, c=c, s_done=s_done):
                        g.wait_ge(ss.h, s_done)
                        g.dma_start(d_xc.ap()[(c - 32) * 128:(c - 31) * 128, :],
                                    sb_sp[c % 2][:]).then_inc(spl.h, 16)
                    em("gpsimd", _f)
                    spl.n += 16
                    last_spill[c % 2] = spl.n
                else:
                    dst = sb_B if c == 64 else sb_C

                    def _f(sc, c=c, v_done=v_done, dst=dst):
                        sc.wait_ge(vv.h, v_done)
                        sc.activation(dst[:], sb_cv[c % 2][:], AF.Silu).then_inc(ss.h, 1)
                    em("scalar", _f)
                    ss.n += 1
                    last_silu[c % 2] = ss.n
            elif p1mode == 0:
                def _f(sc, ps=ps, mm_done=mm_done):
                    sc.wait_ge(tm.h, mm_done)
                    sc.activation(sb_dt[:], ps[0:64, :], AF.Copy).then_inc(ss.h, 1)
                    sc.nop().then_inc(psf.h, 1)
                em("scalar", _f)
                ss.n += 1
                psf.n += 1
            else:
                def _f(sc, ps=ps, mm_done=mm_done):
                    sc.wait_ge(tm.h, mm_done)
                    sc.activation(sb_lncm[:], ps[0:64, :], AF.Copy).then_inc(ss.h, 1)
                    sc.drain()
                    sc.nop().then_inc(psf.h, 1)
                em("scalar", _f)
                ss.n += 1
                psf.n += 1
                s_ev66 = ss.n

                def _f(v, s_ev66=s_ev66):
                    v.wait_ge(ss.h, s_ev66)
                    v.tensor_scalar(sb_lncm[:], sb_lncm[:], sb_dtb[:], 80.0,
                                    op0=ALU.add, op1=ALU.min).then_inc(vv.h, 1)
                em("vector", _f)
                vv.n += 1
                v_t1 = vv.n

                def _f(sc, v_t1=v_t1):
                    sc.wait_ge(vv.h, v_t1)
                    if p1mode == 5:
                        sc.activation(sb_lncm[:], sb_lncm[:], AF.Copy)
                        sc.activation(sb_dt[:], sb_lncm[:], AF.Copy).then_inc(ss.h, 1)
                    else:
                        sc.activation(sb_lncm[:], sb_lncm[:], AF.Exp)
                        sc.activation(sb_dt[:], sb_lncm[:], AF.Ln,
                                      bias=sb_onef[0:64, :]).then_inc(ss.h, 1)
                em("scalar", _f)
                ss.n += 1
                sp_done = ss.n

                def _f(v, sp_done=sp_done):
                    v.wait_ge(ss.h, sp_done)
                    v.tensor_scalar_max(sb_lncm[:], sb_dt[:], 1e-38)
                    v.tensor_scalar(sb_adcm[:], sb_dt[:], sb_acol[0:64, :], -1.0,
                                    op0=ALU.mult, op1=ALU.mult).then_inc(vv.h, 1)
                em("vector", _f)
                vv.n += 1
                v_done = vv.n

                def _f(sc, v_done=v_done):
                    sc.wait_ge(vv.h, v_done)
                    if p1mode == 5:
                        sc.activation(sb_lncm[:], sb_lncm[:], AF.Copy).then_inc(ss.h, 1)
                    else:
                        sc.activation(sb_lncm[:], sb_lncm[:], AF.Ln).then_inc(ss.h, 1)
                em("scalar", _f)
                ss.n += 1
                ln_done = ss.n

                def _f(t, ln_done=ln_done, psf_at=psf.n):
                    t.wait_ge(ss.h, ln_done)
                    t.wait_ge(psf.h, psf_at)
                    if p1mode != 4:
                        tr = None
                        for i in range(32):
                            t.transpose(ps_a[0:64, i * 64:(i + 1) * 64],
                                        sb_lncm[:, i * 64:(i + 1) * 64],
                                        sb_identf[0:64, 0:64])
                            tr = t.transpose(ps_b[0:64, i * 64:(i + 1) * 64],
                                             sb_adcm[:, i * 64:(i + 1) * 64],
                                             sb_identf[0:64, 0:64])
                        tr.then_inc(tm.h, 1)
                    else:
                        t.nop().then_inc(tm.h, 1)
                em("tensor", _f)
                tm.n += 1
                tr_done = tm.n

                def _f(sc, tr_done=tr_done):
                    sc.wait_ge(tm.h, tr_done)
                    if p1mode == 4:
                        sc.nop().then_inc(ss.h, 1)
                        return
                    sc.activation(sb_lndt_tok[:], ps_a[0:64, 0:2048], AF.Copy)
                    sc.activation(sb_adt_tok[:], ps_b[0:64, 0:2048], AF.Copy).then_inc(ss.h, 1)
                em("scalar", _f)
                ss.n += 1

        def _f(g, spl_at=spl.n):
            g.wait_ge(spl.h, spl_at)
        em("gpsimd", _f)
        barrier()
        p1.close()

        MUTE[0] = upto < 2

        p2 = contextlib.ExitStack()
        sb_ddiag = p2.enter_context(nc.sbuf_tensor("c_ddiag", [64, DI], BF16))
        sb_S = p2.enter_context(nc.sbuf_tensor("S_state", [128, DI], BF16))
        sb_xtok = [p2.enter_context(nc.sbuf_tensor(f"xtok{i}", [128, DI], BF16))
                   for i in range(2)]
        sb_xch = p2.enter_context(nc.sbuf_tensor("xch", [64, DI], BF16))
        sb_zg = [p2.enter_context(nc.sbuf_tensor(f"zg{i}", [128, 4096], BF16))
                 for i in range(2)]
        sb_yg = [p2.enter_context(nc.sbuf_tensor(f"yg{i}", [128, 4096], BF16))
                 for i in range(2)]
        sb_arep = [p2.enter_context(nc.sbuf_tensor(f"arep{i}", [64, DI], F32))
                   for i in range(2)]
        sb_cbr = p2.enter_context(nc.sbuf_tensor("cbr", [64, DI], BF16))
        sb_dac = [p2.enter_context(nc.sbuf_tensor(f"dac{i}", [64, 1], F32))
                  for i in range(2)]
        sb_dax = [p2.enter_context(nc.sbuf_tensor(f"dax{i}", [128, 64], F32))
                  for i in range(2)]
        sb_wtok = [p2.enter_context(nc.sbuf_tensor(f"wtok{i}", [64, 64], F32))
                   for i in range(2)]
        sb_ea = [p2.enter_context(nc.sbuf_tensor(f"ea{i}", [128, 2048], BF16))
                 for i in range(2)]
        sb_btok = [p2.enter_context(nc.sbuf_tensor(f"btok{i}", [64, 128], BF16))
                   for i in range(2)]
        sb_front = [p2.enter_context(nc.sbuf_tensor(f"front{i}", [64, 576], F32))
                    for i in range(2)]
        sb_dneg = p2.enter_context(nc.sbuf_tensor("dneg", [64, DI], BF16))
        sb_sct = sb_dneg
        sb_xdd = p2.enter_context(nc.sbuf_tensor("xdd", [64, DI], BF16))
        sb_yf = p2.enter_context(nc.sbuf_tensor("yf", [128, 2048], BF16))
        sb_ev = p2.enter_context(nc.sbuf_tensor("yev", [128, 2048], BF16))
        sb_stb = [p2.enter_context(nc.sbuf_tensor(f"stb{i}", [128, 512], F32))
                  for i in range(2)]
        sb_zs = p2.enter_context(nc.sbuf_tensor("zs", [128, 2048], BF16))
        sb_sq = sb_zs
        sb_ms = p2.enter_context(nc.sbuf_tensor("msrow", [1, 192], F32))
        sb_sev = p2.enter_context(nc.sbuf_tensor("sevrow", [1, 512], F32))
        sb_wrow = p2.enter_context(nc.sbuf_tensor("wrow", [128, 2048], BF16))
        sb_wst = p2.enter_context(nc.sbuf_tensor("wst", [128, 48], F32))
        sb_wq8p = p2.enter_context(nc.sbuf_tensor("wq8p", [128, 128], FP8))

        def _f(sync):
            sync.dma_start(sb_ddiag[:], d_ddiag.ap()).then_inc(sy.h, 16)
        em("sync", _f)
        sy.n += 16

        PS_ST = [ps_b[:, 0:512], ps_b[:, 512:1024]]
        PS_ACT = ps_b[0:64, 1024:1088]
        PS_ACM = ps_b[0:64, 1088:1152]
        PS_CBT = ps_b[64:128, 1024:1088]
        PS_BT = ps_b.bitcast(BF16)[64:128, 2304:2432]
        PS_DAX = ps_b[:, 1152:1216]
        PS_WT = ps_b[0:64, 1216:1280]
        PS_RMS = ps_b[0:1, 1536:2048]

        def _f(v):
            v.memset(sb_S[:], 0.0).then_inc(vv.h, 1)
        em("vector", _f)
        vv.n += 1
        s_ready = [vv.n]
        stfree_base = stfree.n
        b6_base = b6.n
        b7_base = b7.n
        yfree_base = yfree.n

        def front(k):
            par = k % 2
            fb = sb_front[par]
            A_TOK, A_CM = fb[:, 0:64], fb[:, 64:128]
            A2, DEC = fb[:, 128:192], fb[:, 192:256]
            WCM, EAs, CBTM = fb[:, 256:320], fb[:, 320:384], fb[:, 384:448]

            if k % 2 == 0:
                def _f(sc, k=k, xch_at=rec.get(("xch", k - 2), 0)):
                    sc.wait_ge(ss.h, xch_at)
                    sc.dma_start_transpose(
                        sb_xtok[(k // 2) % 2][:],
                        d_xc.ap()[:, k * 64:(k + 2) * 64]).then_inc(ss.h, 16)
                em("scalar", _f)
                ss.n += 16
                def _f(sc, k=k, vv_at=rec.get(("zs_done", k - 3), 0)):
                    sc.wait_ge(vv.h, vv_at)
                    for j in range(32):
                        sc.dma_start(
                            sb_zg[(k // 2) % 2][:, j * 128:(j + 1) * 128],
                            d_z.ap()[j * 128:(j + 1) * 128,
                                     k * 64:(k + 2) * 64]).then_inc(ss.h, 16)
                em("scalar", _f)
                ss.n += 32 * 16
            rec[("xtok", k)] = ss.n

            def _f(t, k=k, b6_at=b6_base + 3 * k):
                t.wait_ge(b6.h, b6_at)
                la = sb_adt_tok[:, k * 64:(k + 1) * 64]
                t.matmul(PS_ACT[:, :], sb_ut[:], la, start=True, stop=True)
                t.matmul(PS_ACM[:, :], la, sb_ut[:], start=True, stop=True)
                t.matmul(PS_CBT[:, :], sb_B[:, k * 64:(k + 1) * 64],
                         sb_C[:, k * 64:(k + 1) * 64], start=True, stop=True)
                t.transpose(PS_BT[:, 0:128], sb_B[:, k * 64:(k + 1) * 64],
                            sb_identb[:]).then_inc(tm.h, 1)
            em("tensor", _f)
            tm.n += 1
            t1 = tm.n

            def _f(sc, k=k, t1=t1, A_TOK=A_TOK, A_CM=A_CM, DEC=DEC, EAs=EAs, fb=fb):
                sc.wait_ge(tm.h, t1)
                sc.activation(A_TOK[:, :], PS_ACT[:, :], AF.Copy)
                sc.activation(A_CM[:, :], PS_ACM[:, :], AF.Copy)
                sc.activation(fb[:, 512:576], PS_CBT[:, :], AF.Copy)
                sc.activation(sb_btok[k % 2][:], PS_BT[:, 0:128], AF.Copy)
                sc.activation(DEC[:, :], A_CM[:, :], AF.Exp, scale=-1.0,
                              bias=A_CM[:, 63:64])
                sc.activation(EAs[:, :], A_CM[:, :], AF.Exp)
                sc.activation(sb_dac[k % 2][:], A_CM[:, 63:64], AF.Exp).then_inc(ss.h, 1)
                sc.drain()
                sc.nop().then_inc(b6.h, 1)
            em("scalar", _f)
            ss.n += 1
            b6.n += 1
            s1 = ss.n

            def _f(v, k=k, s1=s1, A_TOK=A_TOK, A2=A2, DEC=DEC, WCM=WCM, CBTM=CBTM, fb=fb):
                v.wait_ge(ss.h, s1)
                lnd = sb_lndt_tok[:, k * 64:(k + 1) * 64]
                v.tensor_sub(A2[:, :], A_TOK[:, :], lnd)
                v.tensor_mul(WCM[:, :], sb_dt[:, k * 64:(k + 1) * 64], DEC[:, :])
                v.tensor_scalar_mul(fb[:, 448:512], sb_identf[0:64, 0:64],
                                    sb_dac[k % 2][:])
                v.tensor_mul(CBTM[:, :], fb[:, 512:576], sb_ut[:]).then_inc(vv.h, 1)
                v.drain()
                v.nop().then_inc(b6.h, 1)
            em("vector", _f)
            vv.n += 1
            b6.n += 1
            v1 = vv.n

            def _f(t, k=k, v1=v1, s1=s1, WCM=WCM, fb=fb):
                t.wait_ge(vv.h, v1)
                t.wait_ge(ss.h, s1)
                t.matmul(PS_DAX[:, :], sb_onesf[:], fb[:, 448:512],
                         start=True, stop=True)
                t.transpose(PS_WT[:, :], WCM[:, :],
                            sb_identf[0:64, 0:64]).then_inc(tm.h, 1)
            em("tensor", _f)
            tm.n += 1
            t2 = tm.n

            def _f(sc, k=k, t2=t2):
                sc.wait_ge(tm.h, t2)
                sc.activation(sb_dax[k % 2][:], PS_DAX[:, :], AF.Copy)
                sc.activation(sb_wtok[k % 2][:], PS_WT[:, :], AF.Copy).then_inc(ss.h, 1)
                sc.drain()
                sc.nop().then_inc(b6.h, 1)
            em("scalar", _f)
            ss.n += 1
            b6.n += 1

            def _f(sync, k=k, par=par, s1=s1, v1=v1, A_CM=A_CM, EAs=EAs,
                   WCM=WCM, CBTM=CBTM, sy0=sy.n):
                sync.wait_ge(ss.h, s1)
                sync.wait_ge(vv.h, v1)
                sync.dma_start(d_arep.ap()[par, :].rearrange("(h l) -> h l", h=64), A_CM[:, :]).then_inc(sy.h, 16)
                sync.dma_start(d_ea.ap()[par, :].rearrange("(h l) -> h l", h=64), EAs[:, :]).then_inc(sy.h, 16)
                sync.dma_start(d_cbm.ap()[par, :].rearrange("(s l) -> s l", s=64), CBTM[:, :]).then_inc(sy.h, 16)
            em("sync", _f)
            sy.n += 3 * 16
            rec[("flats", k)] = sy.n

            def _f(g, k=k, par=par, sy_at=rec[("flats", k)],
                   yf_at=rec.get(("yf", k - 2), 0)):
                g.wait_ge(sy.h, sy_at)
                g.wait_ge(vv.h, yf_at)
                g.dma_start(
                    sb_ea[par][0:64, :].rearrange("p (j l) -> p j l", l=64),
                    bass.AP(d_ea, par * DI, [[0, 64], [128, 32], [1, 64]])).then_inc(gp.h, 16)
                g.dma_start(
                    sb_ea[par][64:128, :].rearrange("p (j l) -> p j l", l=64),
                    bass.AP(d_ea, par * DI + 64, [[0, 64], [128, 32], [1, 64]])).then_inc(gp.h, 16)
            em("gpsimd", _f)
            gp.n += 2 * 16
            rec[("gp_front", k)] = gp.n

        def main(c):
            par = c % 2
            fb = sb_front[par]
            A2 = fb[:, 128:192]
            half = (c // 2) % 2
            rows = slice((c % 2) * 64, (c % 2) * 64 + 64)

            def _f(sync, c=c, fl_at=rec[("flats", c)],
                   dneg_at=rec.get(("dneg", c - 2), 0)):
                sync.wait_ge(sy.h, fl_at)
                sync.wait_ge(vv.h, dneg_at)
                sync.dma_start(
                    sb_arep[c % 2][:].rearrange("p (h l) -> p h l", h=64),
                    d_arep.ap()[c % 2, :].rearrange("(h l) -> h l", h=64)
                    .unsqueeze(0).broadcast_to((64, 64, 64))).then_inc(sy.h, 16)
            em("sync", _f)
            sy.n += 16
            rec[("sync_front", c)] = sy.n

            def _f(g, c=c, fl_at=rec[("flats", c)],
                   sct_at=rec.get(("sct", c - 1), 0)):
                g.wait_ge(sy.h, fl_at)
                g.wait_ge(vv.h, sct_at)
                g.dma_start(
                    sb_cbr[:].rearrange("p (h l) -> p h l", h=64),
                    d_cbm.ap()[c % 2, :].rearrange("(s l) -> s l", s=64)
                    .unsqueeze(1).broadcast_to((64, 64, 64))).then_inc(gp.h, 16)
            em("gpsimd", _f)
            gp.n += 16
            rec[("cbr", c)] = gp.n

            def _f(sc, c=c, half=half, rows=rows, xt_at=rec[("xtok", c)],
                   tyd_at=rec.get(("t_ydiag", c - 1), 0),
                   xdd_at=rec.get(("xdd", c - 1), 0)):
                sc.wait_ge(ss.h, xt_at)
                sc.wait_ge(tm.h, tyd_at)
                sc.wait_ge(vv.h, xdd_at)
                sc.dma_start(sb_xch[:], sb_xtok[half][rows, :]).then_inc(ss.h, 16)
            em("scalar", _f)
            ss.n += 16
            rec[("xch", c)] = ss.n

            def _f(v, c=c, A2=A2, sy_at=rec[("sync_front", c)],
                   tyd_at=rec.get(("t_ydiag", c - 1), 0)):
                v.wait_ge(sy.h, sy_at)
                v.wait_ge(tm.h, tyd_at)
                v.tensor_sub(sb_dneg[:].rearrange("p (h l) -> p h l", h=64),
                             A2[:, :].unsqueeze(2).broadcast_to((64, 64, 64)),
                             sb_arep[c % 2][:].rearrange("p (h l) -> p h l", h=64))
                v.tensor_scalar_max(sb_dneg[:], sb_dneg[:], -80.0).then_inc(vv.h, 1)
            em("vector", _f)
            vv.n += 1
            rec[("dneg", c)] = vv.n
            v_dneg = vv.n

            def _f(sc, v_dneg=v_dneg):
                sc.wait_ge(vv.h, v_dneg)
                sc.activation(sb_dneg[:], sb_dneg[:], AF.Exp, scale=-1.0).then_inc(ss.h, 1)
            em("scalar", _f)
            ss.n += 1
            s_E = ss.n

            def _f(v, c=c, s_E=s_E, gp_at=rec[("cbr", c)],
                   xch_at=rec[("xch", c)],
                   tyd_at=rec.get(("t_ydiag", c - 1), 0),
                   tst_at=rec.get(("stmm", c - 1, 7), 0)):
                v.wait_ge(ss.h, s_E)
                v.wait_ge(gp.h, gp_at)
                v.wait_ge(tm.h, max(tyd_at, tst_at))
                v.tensor_mul(sb_sct[:], sb_dneg[:], sb_cbr[:])
                v.tensor_add(sb_sct[:], sb_sct[:], sb_ddiag[:]).then_inc(vv.h, 1)
                v.wait_ge(ss.h, xch_at)
                v.tensor_mul(sb_xdd[:].rearrange("p (h q) -> p h q", h=64),
                             sb_xch[:].rearrange("p (h q) -> p h q", h=64),
                             sb_wtok[c % 2][:].unsqueeze(2)
                             .broadcast_to((64, 64, 64))).then_inc(vv.h, 1)
            em("vector", _f)
            vv.n += 2
            rec[("sct", c)] = vv.n - 1
            rec[("xdd", c)] = vv.n
            v_sct = vv.n - 1
            v_xdd = vv.n

            def _f(t, c=c, yf_at=yfree_base + c, sv=s_ready[0]):
                t.wait_ge(yfree.h, yf_at)
                t.wait_ge(vv.h, sv)
                mm = None
                for j in range(32):
                    mm = t.matmul(ps_a[:, j * 64:(j + 1) * 64],
                                  sb_S[:, j * 128:(j + 1) * 128],
                                  sb_C[:, c * 64:(c + 1) * 64], start=True, stop=True)
                mm.then_inc(tm.h, 1)
            em("tensor", _f)
            tm.n += 1
            t_yoff = tm.n

            def _f(sc, c=c, t_yoff=t_yoff, yrd_at=rec.get(("v_yadd", c - 1), 0)):
                sc.wait_ge(tm.h, t_yoff)
                sc.wait_ge(vv.h, yrd_at)
                sc.activation(sb_ev[:], ps_a[:], AF.Copy).then_inc(ss.h, 1)
            em("scalar", _f)
            ss.n += 1
            s_yoev = ss.n

            def _f(v, c=c, par=par, s_yoev=s_yoev, gp_at=rec[("gp_front", c)]):
                v.wait_ge(ss.h, s_yoev)
                v.wait_ge(gp.h, gp_at)
                v.tensor_mul(sb_yf[:], sb_ev[:], sb_ea[par][:]).then_inc(vv.h, 1)
            em("vector", _f)
            vv.n += 1
            v_yoff = vv.n

            def _f(t, c=c, v_sct=v_sct, s_yoev=s_yoev):
                t.wait_ge(vv.h, v_sct)
                t.wait_ge(ss.h, s_yoev)
                mm = None
                for j in range(32):
                    for hh in (2 * j, 2 * j + 1):
                        mm = t.matmul(ps_a[(hh % 2) * 64:(hh % 2) * 64 + 64,
                                           j * 64:(j + 1) * 64],
                                      sb_xch[:, hh * 64:(hh + 1) * 64],
                                      sb_sct[:, hh * 64:(hh + 1) * 64],
                                      start=True, stop=True)
                mm.then_inc(tm.h, 1)
            em("tensor", _f)
            tm.n += 1
            t_ydiag = tm.n
            rec[("t_ydiag", c)] = t_ydiag

            for sl in range(8):
                def _f(t, c=c, sl=sl, v_xdd=v_xdd,
                       st_at=stfree_base + max(0, c * 8 + sl - 1)):
                    if sl == 0:
                        t.wait_ge(vv.h, v_xdd)
                    t.wait_ge(stfree.h, st_at)
                    t.matmul(PS_ST[sl % 2][:, :], sb_btok[c % 2][:],
                             sb_xdd[:, sl * 512:(sl + 1) * 512],
                             start=True, stop=True).then_inc(tm.h, 1)
                em("tensor", _f)
                tm.n += 1
                rec[("stmm", c, sl)] = tm.n

            def _f(sc, c=c, t_ydiag=t_ydiag, yo_at=v_yoff):
                sc.wait_ge(tm.h, t_ydiag)
                sc.wait_ge(vv.h, yo_at)
                sc.activation(sb_ev[:], ps_a[:], AF.Copy).then_inc(yfree.h, 1)
            em("scalar", _f)
            yfree.n += 1
            s_ydev = yfree.n

            def _f(v, c=c, s_ydev=s_ydev):
                v.wait_ge(yfree.h, s_ydev)
                v.tensor_add(sb_yf[:], sb_yf[:], sb_ev[:]).then_inc(vv.h, 1)
            em("vector", _f)
            vv.n += 1
            v_yf = vv.n
            rec[("yf", c)] = v_yf
            rec[("v_yadd", c)] = v_yf

            def _f(sc, c=c, stmm={sl: rec[("stmm", c, sl)] for sl in range(8)},
                   vb=vv.n):
                for sl in range(8):
                    sc.wait_ge(tm.h, stmm[sl])
                    if sl >= 2:
                        sc.wait_ge(vv.h, vb + sl - 1)
                    sc.activation(sb_stb[sl % 2][:], PS_ST[sl % 2][:, :],
                                  AF.Copy).then_inc(stfree.h, 1)
            em("scalar", _f)
            sb0 = stfree.n
            stfree.n += 8

            def _f(v, c=c, par=par, sb0=sb0):
                for sl in range(8):
                    v.wait_ge(stfree.h, sb0 + sl + 1)
                    ssl = sb_S[:, sl * 512:(sl + 1) * 512]
                    v.tensor_mul(ssl.rearrange("p (h q) -> p h q", q=64), 
                                 ssl.rearrange("p (h q) -> p h q", q=64),
                                 sb_dax[par][:, sl * 8:(sl + 1) * 8].unsqueeze(2)
                                 .broadcast_to((128, 8, 64)))
                    v.tensor_add(ssl, ssl, sb_stb[sl % 2][:]).then_inc(vv.h, 1)
            em("vector", _f)
            supdv0 = vv.n
            vv.n += 8
            s_ready[0] = vv.n
            rec[("supd", c)] = vv.n
            rec[("supdv", c)] = supdv0

            zoff = (c % 2) * 64
            zgb = sb_zg[(c // 2) % 2]

            def _f(sc, c=c, zoff=zoff, zgb=zgb, z_at=rec[("xtok", (c // 2) * 2)]):
                sc.wait_ge(ss.h, z_at)
                sc.activation(sb_zs[:].rearrange("p (j l) -> p j l", l=64),
                              bass.AP(zgb, zoff, [[4096, 128], [128, 32], [1, 64]]),
                              AF.Silu).then_inc(ss.h, 1)
            em("scalar", _f)
            ss.n += 1
            s_sg = ss.n

            def _f(v, c=c, s_sg=s_sg, v_yf=v_yf):
                v.wait_ge(ss.h, s_sg)
                v.wait_ge(vv.h, v_yf)
                v.tensor_mul(sb_yf[:], sb_yf[:], sb_zs[:]).then_inc(vv.h, 1)
            em("vector", _f)
            vv.n += 1
            v_yf2 = vv.n
            rec[("zs_done", c)] = vv.n

            def _f(sc, c=c, v_yf2=v_yf2, trms_at=rec.get(("t_rms", c - 1), 0)):
                sc.wait_ge(vv.h, v_yf2)
                sc.wait_ge(tm.h, trms_at)
                sc.activation(sb_sq[:], sb_yf[:], AF.Square).then_inc(ss.h, 1)
            em("scalar", _f)
            ss.n += 1
            s_sq = ss.n

            def _f(t, c=c, s_sq=s_sq, b7_at=b7_base + c):
                t.wait_ge(ss.h, s_sq)
                t.wait_ge(b7.h, b7_at)
                mm = None
                for g4 in range(4):
                    mm = t.matmul(PS_RMS[:, :], sb_ones[:],
                                  sb_sq[:, g4 * 512:(g4 + 1) * 512],
                                  start=(g4 == 0), stop=(g4 == 3))
                mm.then_inc(tm.h, 1)
            em("tensor", _f)
            tm.n += 1
            t_rms = tm.n
            rec[("t_rms", c)] = t_rms

            def _f(sc, t_rms=t_rms):
                sc.wait_ge(tm.h, t_rms)
                sc.activation(sb_sev[:], PS_RMS[:, :], AF.Copy).then_inc(b7.h, 1)
            em("scalar", _f)
            b7.n += 1
            s_ev = b7.n

            def _f(v, s_ev=s_ev):
                v.wait_ge(b7.h, s_ev)
                v.tensor_reduce(sb_ms[:, 0:64].unsqueeze(2),
                                bass.AP(sb_sev, 0, [[512, 1], [1, 64], [64, 8]]),
                                axis=mybir.AxisListType.X, op=ALU.add)
                v.drain()
                v.nop().then_inc(vv.h, 1)
            em("vector", _f)
            vv.n += 1
            v_ms = vv.n

            def _f(sc, v_ms=v_ms):
                sc.wait_ge(vv.h, v_ms)
                sc.activation(sb_ms[:, 64:128], sb_ms[:, 0:64], AF.Sqrt,
                              scale=1.0 / 4096.0, bias=sb_eps[0:1, :]).then_inc(ss.h, 1)
            em("scalar", _f)
            ss.n += 1
            s_srt = ss.n

            def _f(v, c=c, s_srt=s_srt, rsp_at=rec.get(("rrspill", c - 1), 0)):
                v.wait_ge(ss.h, s_srt)
                v.wait_ge(gp.h, rsp_at)
                v.reciprocal(sb_ms[:, 128:192], sb_ms[:, 64:128])
                v.drain()
                v.nop().then_inc(vv.h, 1)
            em("vector", _f)
            vv.n += 1
            v_rr = vv.n

            def _f(g, c=c, v_rr=v_rr):
                g.wait_ge(vv.h, v_rr)
                g.dma_start(d_rr.ap()[c * 64:(c + 1) * 64].unsqueeze(0),
                            sb_ms[:, 128:192]).then_inc(gp.h, 16)
            em("gpsimd", _f)
            gp.n += 16
            rec[("rrspill", c)] = gp.n

            ygb = sb_yg[(c // 2) % 2]

            def _f(v, c=c, ygb=ygb, v_yf2=v_yf2,
                   sy_at=rec.get(("yspill", c // 2 - 2), 0)):
                v.wait_ge(sy.h, sy_at)
                v.tensor_mul(
                    bass.AP(ygb, (c % 2) * 64, [[4096, 128], [128, 32], [1, 64]]),
                    sb_yf[:].rearrange("p (j l) -> p j l", l=64),
                    sb_nw[:].unsqueeze(2).broadcast_to((128, 32, 64))).then_inc(vv.h, 1)
            em("vector", _f)
            vv.n += 1
            rec[("ynrm", c)] = vv.n

            if c % 2 == 1:
                def _f(sync, c=c, ygb=ygb, v_at=rec[("ynrm", c)]):
                    sync.wait_ge(vv.h, v_at)
                    sync.dma_start(
                        bass.AP(d_y, (c - 1) * 64,
                                [[T, 128], [128 * T, 32], [1, 128]]),
                        ygb[:].rearrange("p (j l) -> p j l", l=128)).then_inc(sy.h, 16)
                em("sync", _f)
                sy.n += 16
                rec[("yspill", c // 2)] = sy.n

        def wq_quant(j):
            def _f(g, j=j, sp_at=rec.get(("wqspill", j - 1), 0)):
                g.wait_ge(sy.h, sp_at)
                g.dma_start(sb_wrow[:],
                            d_wffn.ap()[j * 128:(j + 1) * 128, :]).then_inc(gp.h, 16)
            em("gpsimd", _f)
            gp.n += 16
            w_at = gp.n

            def _f(v, j=j, w_at=w_at):
                v.wait_ge(gp.h, w_at)
                v.tensor_reduce(sb_wst[:, 0:16].unsqueeze(2),
                                sb_wrow[:].rearrange("p (s f) -> p s f", s=16),
                                axis=mybir.AxisListType.X, op=ALU.max,
                                apply_absolute_value=True)
                v.drain()
                v.tensor_scalar_mul(sb_wst[:, 32:48], sb_wst[:, 0:16], 1.0 / 224.0)
                v.drain()
                v.reciprocal(sb_wst[:, 16:32], sb_wst[:, 32:48])
                v.drain()
                v.nop().then_inc(vv.h, 1)
            em("vector", _f)
            vv.n += 1
            v_w = vv.n

            def _f(sc, j=j, v_w=v_w):
                sc.wait_ge(vv.h, v_w)
                sop = None
                for sgi in range(16):
                    sc.activation(sb_wq8p[:],
                                  sb_wrow[:, sgi * 128:(sgi + 1) * 128],
                                  AF.Copy, scale=sb_wst[:, 16 + sgi:17 + sgi])
                    sop = sc.activation(sb_wrow[:, sgi * 128:(sgi + 1) * 128],
                                        sb_wq8p[:],
                                        AF.Copy, scale=sb_wst[:, 32 + sgi:33 + sgi])
                sop.then_inc(ss.h, 1)
            em("scalar", _f)
            ss.n += 1
            s_w = ss.n

            def _f(sync, j=j, s_w=s_w):
                sync.wait_ge(ss.h, s_w)
                sync.dma_start(d_wdq.ap()[j * 128:(j + 1) * 128, :],
                               sb_wrow[:]).then_inc(sy.h, 16)
            em("sync", _f)
            sy.n += 16
            rec[("wqspill", j)] = sy.n

        front(0)
        for c in range(NCH):
            if c + 1 < NCH:
                front(c + 1)
            main(c)
            if c % 2 == 0 and c // 2 < 16:
                wq_quant(c // 2)

        def _f(g, gp_at=gp.n):
            g.wait_ge(gp.h, gp_at)
        em("gpsimd", _f)
        barrier()
        p2.close()
        p12.close()

        if upto < 3:
            MUTE[0] = True

        p3 = contextlib.ExitStack()
        sb_wout = p3.enter_context(nc.sbuf_tensor("woutb", [128, 32 * 2048], BF16))
        sb_ygr = [p3.enter_context(nc.sbuf_tensor(f"ygr{i}", [128, 32 * 128], BF16))
                  for i in range(2)]
        sb_xin = p3.enter_context(nc.sbuf_tensor("xin", [128, 2048], BF16))
        sb_rr2 = p3.enter_context(nc.sbuf_tensor("rr2", [128, 16], F32))
        sb_r1 = p3.enter_context(nc.sbuf_tensor("r1", [128, 2048], F32))
        sb_x2f = p3.enter_context(nc.sbuf_tensor("x2f", [128, 2048], F32))
        sb_st1 = p3.enter_context(nc.sbuf_tensor("st1", [128, 72], F32))
        sb_q8 = p3.enter_context(nc.sbuf_tensor("q8t", [128, 128], FP8))
        sb_x2b = p3.enter_context(nc.sbuf_tensor("x2b", [128, 2048], BF16))
        sb_x2qb = p3.enter_context(nc.sbuf_tensor("x2qb", [128, 2048], BF16))
        sb_wqs = p3.enter_context(nc.sbuf_tensor("wqstage", [128, 2048], BF16))

        for j in range(32):
            def _f(g, j=j):
                g.dma_start(sb_wout[:, j * 2048:(j + 1) * 2048],
                            d_wout.ap()[j * 128:(j + 1) * 128, :]).then_inc(gp.h, 16)
            em("gpsimd", _f)
            gp.n += 16
        wout_at = gp.n

        def _f(sync):
            sync.dma_start(sb_rr2[:], bass.AP(d_rr, 0, [[1, 128], [128, 16]]), allow_slow_non_contiguous=True).then_inc(sy.h, 16)
        em("sync", _f)
        sy.n += 16
        rr_at = sy.n

        psf_base3 = psf.n
        tm_base3 = tm.n
        x2sp = {}
        for tg in range(16):
            ps = ps_a if tg % 2 == 0 else ps_b

            def _f(sync, tg=tg, tm_at=tm_base3 + max(0, tg - 1)):
                sync.wait_ge(tm.h, tm_at)
                for j in range(32):
                    sync.dma_start(
                        sb_ygr[tg % 2][:, j * 128:(j + 1) * 128],
                        d_y.ap()[j * 128:(j + 1) * 128,
                                 tg * 128:(tg + 1) * 128]).then_inc(sy.h, 16)
            em("sync", _f)
            sy.n += 32 * 16
            rec[("yrd", tg)] = sy.n

            def _f(g, tg=tg, v_at=rec.get(("p3r1", tg - 1), 0)):
                g.wait_ge(vv.h, v_at)
                g.dma_start(sb_xin[:],
                            d_u.ap()[tg * 128:(tg + 1) * 128, :]).then_inc(gp.h, 16)
            em("gpsimd", _f)
            gp.n += 16
            xin_at = gp.n

            def _f(sc, tg=tg, sp_at=rec.get(("wqTsp", tg - 1), 0)):
                sc.wait_ge(sy.h, sp_at)
                sc.dma_start_transpose(
                    sb_wqs[:],
                    d_wdq.ap()[:, tg * 128:(tg + 1) * 128]).then_inc(ss.h, 16)
            em("scalar", _f)
            ss.n += 16
            wqt_at = ss.n

            def _f(sync, tg=tg, wqt_at=wqt_at):
                sync.wait_ge(ss.h, wqt_at)
                sync.dma_start(d_wqT.ap()[tg * 128:(tg + 1) * 128, :],
                               sb_wqs[:]).then_inc(sy.h, 16)
            em("sync", _f)
            sy.n += 16
            rec[("wqTsp", tg)] = sy.n

            def _f(t, tg=tg, ps=ps, y_at=rec[("yrd", tg)], wo_at=wout_at,
                   psf_at=psf_base3 + max(0, tg - 1)):
                t.wait_ge(sy.h, y_at)
                if tg == 0:
                    t.wait_ge(gp.h, wo_at)
                t.wait_ge(psf.h, psf_at)
                mm = None
                for j in range(32):
                    lhsT = sb_ygr[tg % 2][:, j * 128:(j + 1) * 128]
                    for nb in range(4):
                        mm = t.matmul(ps[:, nb * 512:(nb + 1) * 512], lhsT,
                                      sb_wout[:, j * 2048 + nb * 512:j * 2048 + (nb + 1) * 512],
                                      start=(j == 0), stop=(j == 31))
                mm.then_inc(tm.h, 1)
            em("tensor", _f)
            tm.n += 1
            mm3 = tm.n

            def _f(sc, tg=tg, ps=ps, mm3=mm3, r1_at=rec.get(("p3r1", tg - 1), 0)):
                sc.wait_ge(tm.h, mm3)
                sc.wait_ge(vv.h, r1_at)
                sc.activation(sb_r1[:], ps[:], AF.Copy)
                sc.drain()
                sc.nop().then_inc(psf.h, 1)
            em("scalar", _f)
            psf.n += 1
            s_r1 = psf.n

            def _f(v, tg=tg, s_r1=s_r1, xin_at=xin_at, rr_at=rr_at):
                v.wait_ge(psf.h, s_r1)
                v.wait_ge(gp.h, xin_at)
                v.wait_ge(sy.h, rr_at)
                v.scalar_tensor_tensor(sb_r1[:], sb_r1[:], sb_rr2[:, tg:tg + 1],
                                       sb_xin[:], op0=ALU.mult, op1=ALU.add)
                v.reduce_sum(sb_st1[:, 0:1], sb_r1[:], axis=mybir.AxisListType.X)
                v.drain()
                v.tensor_scalar_mul(sb_st1[:, 1:2], sb_st1[:, 0:1], 1.0 / 2048.0)
                v.drain()
                v.nop().then_inc(vv.h, 1)
            em("vector", _f)
            vv.n += 1
            rec[("p3r1", tg)] = vv.n
            v_m = vv.n

            def _f(sc, v_m=v_m):
                sc.wait_ge(vv.h, v_m)
                sc.activation(sb_x2f[:], sb_r1[:], AF.Square,
                              bias=sb_st1[:, 1:2], scale=-1.0,
                              accum_out=sb_st1[:, 2:3]).then_inc(ss.h, 1)
            em("scalar", _f)
            ss.n += 1
            s_v = ss.n

            def _f(sc, s_v=s_v):
                sc.wait_ge(ss.h, s_v)
                sc.activation(sb_st1[:, 3:4], sb_st1[:, 2:3], AF.Sqrt,
                              scale=1.0 / 2048.0, bias=sb_eps[:, :])
                sc.drain()
                sc.nop().then_inc(ss.h, 1)
            em("scalar", _f)
            ss.n += 1
            s_sd = ss.n

            def _f(v, tg=tg, s_sd=s_sd, g_at=x2sp.get(tg - 1, 0),
                   sq_at=rec.get(("x2q_s", tg - 1), 0)):
                v.wait_ge(ss.h, max(s_sd, sq_at))
                v.wait_ge(gp.h, g_at)
                v.tensor_scalar(sb_x2f[:], sb_r1[:], sb_st1[:, 1:2], 1.0,
                                op0=ALU.subtract, op1=ALU.mult)
                v.tensor_reduce(sb_st1[:, 6:22].unsqueeze(2),
                                sb_x2f[:].rearrange("p (s f) -> p s f", s=16),
                                axis=mybir.AxisListType.X, op=ALU.max,
                                apply_absolute_value=True)
                v.drain()
                v.tensor_scalar_mul(sb_st1[:, 38:54], sb_st1[:, 6:22], 1.0 / 224.0)
                v.drain()
                v.reciprocal(sb_st1[:, 5:6], sb_st1[:, 3:4])
                v.reciprocal(sb_st1[:, 22:38], sb_st1[:, 38:54])
                v.drain()
                v.nop().then_inc(vv.h, 1)
            em("vector", _f)
            vv.n += 1
            v_q = vv.n

            def _f(sc, tg=tg, v_q=v_q):
                sc.wait_ge(vv.h, v_q)
                sc.activation(sb_st1[:, 54:70], sb_st1[:, 38:54], AF.Copy,
                              scale=sb_st1[:, 5:6])
                for sgi in range(16):
                    sc.activation(sb_q8[:],
                                  sb_x2f[:, sgi * 128:(sgi + 1) * 128], AF.Copy,
                                  scale=sb_st1[:, 22 + sgi:23 + sgi])
                    sc.activation(sb_x2qb[:, sgi * 128:(sgi + 1) * 128],
                                  sb_q8[:], AF.Copy,
                                  scale=sb_st1[:, 54 + sgi:55 + sgi])
                sc.activation(sb_x2b[:], sb_x2f[:], AF.Copy,
                              scale=sb_st1[:, 5:6])
                sc.drain()
                sc.nop().then_inc(ss.h, 1)
            em("scalar", _f)
            ss.n += 1
            s_x2b = ss.n
            rec[("x2q_s", tg)] = s_x2b

            def _f(g, tg=tg, s_x2b=s_x2b):
                g.wait_ge(ss.h, s_x2b)
                g.dma_start(d_x2.ap()[tg * 128:(tg + 1) * 128, :],
                            sb_x2b[:]).then_inc(gp.h, 16)
                g.dma_start(d_x2q.ap()[tg * 128:(tg + 1) * 128, :],
                            sb_x2qb[:]).then_inc(gp.h, 16)
            em("gpsimd", _f)
            gp.n += 32
            x2sp[tg] = gp.n

        barrier()
        p3.close()

        if upto < 4:
            MUTE[0] = True

        p4 = contextlib.ExitStack()
        sb_wq = p4.enter_context(nc.sbuf_tensor("wqT", [128, 16 * 2048], BF16))
        sb_st4 = p4.enter_context(nc.sbuf_tensor("st4", [128, 60], F32))
        sb_xq = [p4.enter_context(nc.sbuf_tensor(f"xqT{i}", [128, 2048], BF16))
                 for i in range(2)]
        sb_x2r = [p4.enter_context(nc.sbuf_tensor(f"x2r{i}", [128, 2048], BF16))
                  for i in range(2)]
        sb_r2 = p4.enter_context(nc.sbuf_tensor("r2", [128, 2048], F32))
        sb_sq4 = p4.enter_context(nc.sbuf_tensor("sq4", [128, 2048], F32))
        sb_o4 = [p4.enter_context(nc.sbuf_tensor(f"o4_{i}", [128, 2048], F32))
                 for i in range(2)]

        for kc in range(16):
            def _f(sync, kc=kc):
                sync.dma_start(sb_wq[:, kc * 2048:(kc + 1) * 2048],
                               d_wqT.ap()[kc * 128:(kc + 1) * 128, :]).then_inc(sy.h, 16)
            em("sync", _f)
            sy.n += 16
        wq_ld_at = sy.n

        psf_base4 = psf.n
        tm_base4 = tm.n
        outsp = {}
        for tg in range(16):
            ps = ps_a if tg % 2 == 0 else ps_b

            def _f(sync, tg=tg, tm_at=tm_base4 + max(0, tg - 1)):
                sync.wait_ge(tm.h, tm_at)
                for kc in range(16):
                    sync.dma_start_transpose(
                        sb_xq[tg % 2][:, kc * 128:(kc + 1) * 128],
                        d_x2q.ap()[tg * 128:(tg + 1) * 128,
                                   kc * 128:(kc + 1) * 128]).then_inc(sy.h, 16)
            em("sync", _f)
            sy.n += 16 * 16
            xq_at = sy.n

            def _f(sync, tg=tg, v_at=rec.get(("p4r2", tg - 1), 0)):
                sync.wait_ge(vv.h, v_at)
                sync.dma_start(sb_x2r[tg % 2][:],
                               d_x2.ap()[tg * 128:(tg + 1) * 128, :]).then_inc(sy.h, 16)
            em("sync", _f)
            sy.n += 16
            x2_at = sy.n

            def _f(t, tg=tg, ps=ps, xq_at=xq_at, wq_ld_at=wq_ld_at,
                   psf_at=psf_base4 + max(0, tg - 1)):
                t.wait_ge(sy.h, max(xq_at, wq_ld_at))
                t.wait_ge(psf.h, psf_at)
                mm = None
                for kc in range(16):
                    lhsT = sb_xq[tg % 2][:, kc * 128:(kc + 1) * 128]
                    for nb in range(4):
                        mm = t.matmul(ps[:, nb * 512:(nb + 1) * 512], lhsT,
                                      sb_wq[:, kc * 2048 + nb * 512:kc * 2048 + (nb + 1) * 512],
                                      start=(kc == 0), stop=(kc == 15))
                mm.then_inc(tm.h, 1)
            em("tensor", _f)
            tm.n += 1
            mm4 = tm.n

            def _f(sc, tg=tg, ps=ps, mm4=mm4, r2_at=rec.get(("p4r2", tg - 1), 0)):
                sc.wait_ge(tm.h, mm4)
                sc.wait_ge(vv.h, r2_at)
                sc.activation(sb_r2[:], ps[:], AF.Copy)
                sc.drain()
                sc.nop().then_inc(psf.h, 1)
            em("scalar", _f)
            psf.n += 1
            s_r2 = psf.n

            def _f(v, tg=tg, s_r2=s_r2, x2_at=x2_at):
                v.wait_ge(psf.h, s_r2)
                v.wait_ge(sy.h, x2_at)
                v.tensor_add(sb_r2[:], sb_r2[:], sb_x2r[tg % 2][:])
                v.reduce_sum(sb_st4[:, 32:33], sb_r2[:], axis=mybir.AxisListType.X)
                v.drain()
                v.tensor_scalar_mul(sb_st4[:, 33:34], sb_st4[:, 32:33], 1.0 / 2048.0)
                v.drain()
                v.nop().then_inc(vv.h, 1)
            em("vector", _f)
            vv.n += 1
            rec[("p4r2", tg)] = vv.n
            v_m4 = vv.n

            def _f(sc, v_m4=v_m4):
                sc.wait_ge(vv.h, v_m4)
                sc.activation(sb_sq4[:], sb_r2[:], AF.Square,
                              bias=sb_st4[:, 33:34], scale=-1.0,
                              accum_out=sb_st4[:, 34:35])
                sc.activation(sb_st4[:, 36:37], sb_st4[:, 34:35], AF.Sqrt,
                              scale=1.0 / 2048.0, bias=sb_eps[:, :])
                sc.drain()
                sc.nop().then_inc(ss.h, 1)
            em("scalar", _f)
            ss.n += 1
            s_sd4 = ss.n

            def _f(v, tg=tg, s_sd4=s_sd4, g_at=outsp.get(tg - 2, 0)):
                v.wait_ge(ss.h, s_sd4)
                v.wait_ge(gp.h, g_at)
                v.tensor_scalar(sb_sq4[:], sb_r2[:], sb_st4[:, 33:34], 1.0,
                                op0=ALU.subtract, op1=ALU.mult)
                v.reciprocal(sb_st4[:, 37:38], sb_st4[:, 36:37])
                v.drain()
                v.nop().then_inc(vv.h, 1)
            em("vector", _f)
            vv.n += 1
            v_o = vv.n

            def _f(sc, tg=tg, v_o=v_o):
                sc.wait_ge(vv.h, v_o)
                sc.activation(sb_o4[tg % 2][:], sb_sq4[:], AF.Copy,
                              scale=sb_st4[:, 37:38])
                sc.drain()
                sc.nop().then_inc(ss.h, 1)
            em("scalar", _f)
            ss.n += 1
            s_o4 = ss.n

            def _f(g, tg=tg, s_o4=s_o4):
                g.wait_ge(ss.h, s_o4)
                g.dma_start(d_out.ap()[tg * 128:(tg + 1) * 128, :],
                            sb_o4[tg % 2][:]).then_inc(gp.h, 16)
            em("gpsimd", _f)
            gp.n += 16
            outsp[tg] = gp.n

        def _f(g, gp_at=gp.n):
            g.wait_ge(gp.h, gp_at)
        em("gpsimd", _f)
        barrier()
        p4.close()

        global _DBG
        _DBG = {k: v.n for k, v in S.items()}
        with nc.Block() as block:
            @block.sync
            def _(sync):
                for fn in prog["sync"]:
                    fn(sync)

            @block.tensor
            def _(t):
                for fn in prog["tensor"]:
                    fn(t)

            @block.vector
            def _(v):
                for fn in prog["vector"]:
                    fn(v)

            @block.scalar
            def _(sc):
                for fn in prog["scalar"]:
                    fn(sc)

            @block.gpsimd
            def _(g):
                for fn in prog["gpsimd"]:
                    fn(g)

    return nc


_DBG = {}
_NC_CACHE = None


def _get_nc():
    global _NC_CACHE
    if _NC_CACHE is None:
        _NC_CACHE = build_program()
    return _NC_CACHE


def _host_constants(inputs):
    D = np.asarray(inputs["L0_attn_D_param"], np.float32)
    ddiag = np.zeros((64, DI), np.float32)
    for h in range(64):
        ddiag[np.arange(64), h * 64 + np.arange(64)] = D[h]
    conv_w = np.asarray(inputs["L0_attn_conv_w"], np.float32)
    conv_b = np.asarray(inputs["L0_attn_conv_b"], np.float32)
    cw = conv_w.reshape(34, 128, 4).transpose(1, 0, 2).reshape(128, 136)
    cb = conv_b.reshape(34, 128).T.copy()
    nw = np.asarray(inputs["L0_attn_norm_w"], np.float32).reshape(32, 128).T.copy()
    ut = np.triu(np.ones((64, 64), np.float32))
    return {
        "w_in": np.ascontiguousarray(np.asarray(inputs["L0_attn_W_in"], np.float32)),
        "w_out": np.ascontiguousarray(np.asarray(inputs["L0_attn_W_out"], np.float32)),
        "w_ffn": np.ascontiguousarray(np.asarray(inputs["L0_ffn_weight"], np.float32)),
        "cw": np.ascontiguousarray(cw),
        "cb": np.ascontiguousarray(cb),
        "dtb": np.asarray(inputs["L0_attn_dt_bias"], np.float32).reshape(64, 1),
        "alog": np.asarray(inputs["L0_attn_A_log"], np.float32).reshape(64, 1),
        "ddiag": ddiag.astype(BF),
        "nw": np.ascontiguousarray(nw),
        "identb": np.eye(128, dtype=np.float32).astype(BF),
        "identf": np.eye(128, dtype=np.float32),
        "ut": np.ascontiguousarray(ut),
        "ones128": np.ones((128, 1), np.float32).astype(BF),
        "onesrow": np.ones((1, 128), np.float32).astype(BF),
        "onesf": np.ones((64, 128), np.float32),
    }


def kernel(**inputs):
    nc = _get_nc()
    const = _host_constants(inputs)
    u = np.ascontiguousarray(np.asarray(inputs["L0_attn_u"], np.float32))
    in_maps = []
    for core in range(8):
        m = dict(const)
        m["u"] = u[core]
        in_maps.append(m)
    res = run_bass_kernel_spmd(nc, in_maps, core_ids=list(range(8)))
    out = np.stack([np.asarray(r["out"], np.float32) for r in res.results], axis=0)
    return out


# version 11
# speedup vs baseline: 1.0144x; 1.0144x over previous
import contextlib
import numpy as np
import ml_dtypes
import concourse.bass as bass
import concourse.mybir as mybir
from concourse.bass_utils import run_bass_kernel_spmd

F32 = mybir.dt.float32
BF16 = mybir.dt.bfloat16
FP8 = mybir.dt.float8e4
AF = mybir.ActivationFunctionType
ALU = mybir.AluOpType
BF = ml_dtypes.bfloat16

T = 2048
DM = 2048
DI = 4096
NCH = 32


class Ctr:
    def __init__(self, handle):
        self.h = handle
        self.n = 0


def build_program(upto=99, p1mode=1, p1n=67, dbg=False):
    nc = bass.Bass("TRN2", target_bir_lowering=False, detect_race_conditions=False)

    d_u = nc.dram_tensor("u", [T, DM], F32, kind="ExternalInput")
    d_win = nc.dram_tensor("w_in", [DM, 8512], F32, kind="ExternalInput")
    d_wout = nc.dram_tensor("w_out", [DI, DM], F32, kind="ExternalInput")
    d_wffn = nc.dram_tensor("w_ffn", [DM, DM], F32, kind="ExternalInput")
    d_cw = nc.dram_tensor("cw", [128, 136], F32, kind="ExternalInput")
    d_cb = nc.dram_tensor("cb", [128, 34], F32, kind="ExternalInput")
    d_dtb = nc.dram_tensor("dtb", [64, 1], F32, kind="ExternalInput")
    d_alog = nc.dram_tensor("alog", [64, 1], F32, kind="ExternalInput")
    d_ddiag = nc.dram_tensor("ddiag", [64, DI], BF16, kind="ExternalInput")
    d_nw = nc.dram_tensor("nw", [128, 32], F32, kind="ExternalInput")
    d_identb = nc.dram_tensor("identb", [128, 128], BF16, kind="ExternalInput")
    d_identf = nc.dram_tensor("identf", [128, 128], F32, kind="ExternalInput")
    d_ut = nc.dram_tensor("ut", [64, 64], F32, kind="ExternalInput")
    d_ones = nc.dram_tensor("ones128", [128, 1], BF16, kind="ExternalInput")
    d_onesr = nc.dram_tensor("onesrow", [1, 128], BF16, kind="ExternalInput")
    d_onesf = nc.dram_tensor("onesf", [64, 128], F32, kind="ExternalInput")
    d_out = nc.dram_tensor("out", [T, DM], F32, kind="ExternalOutput")

    KIND = "ExternalOutput" if dbg else "Internal"
    d_ubf = nc.dram_tensor("ubf_scr", [T, DM], BF16)
    d_z = nc.dram_tensor("z_scr", [DI, T], BF16, kind=KIND)
    d_xc = nc.dram_tensor("xc_scr", [DI, T], BF16, kind=KIND)
    d_y = nc.dram_tensor("y_scr", [DI, T], BF16, kind=KIND)
    d_rr = nc.dram_tensor("rr_scr", [T], F32, kind=KIND)
    d_x2 = nc.dram_tensor("x2_scr", [T, DM], BF16, kind=KIND)
    d_x2q = nc.dram_tensor("x2q_scr", [T, DM], BF16, kind=KIND)
    d_wdq = nc.dram_tensor("wdq_scr", [DM, DM], BF16, kind=KIND)
    d_wqT = nc.dram_tensor("wqT_scr", [DM, DM], BF16, kind=KIND)
    d_arep = nc.dram_tensor("arep_scr", [2, DI], F32)
    d_ea = nc.dram_tensor("ea_scr", [2, DI], F32)
    d_cbm = nc.dram_tensor("cbm_scr", [2, DI], F32)

    st = contextlib.ExitStack()
    with st:
        ps_a = st.enter_context(nc.psum_tensor("ps_a", [128, 2048], F32))
        ps_b = st.enter_context(nc.psum_tensor("ps_b", [128, 2048], F32))

        S = {}
        for name in ["sy", "gp", "tm", "vv", "ss", "ph", "psf", "yfree",
                     "stfree", "b6", "b7", "spl"]:
            S[name] = Ctr(st.enter_context(nc.semaphore(name)))
        sy, gp, tm, vv, ss = S["sy"], S["gp"], S["tm"], S["vv"], S["ss"]
        ph, psf, yfree = S["ph"], S["psf"], S["yfree"]
        stfree, b6, b7, spl = S["stfree"], S["b6"], S["b7"], S["spl"]

        sb_identb = st.enter_context(nc.sbuf_tensor("c_identb", [128, 128], BF16))
        sb_identf = st.enter_context(nc.sbuf_tensor("c_identf", [128, 128], F32))
        sb_ut = st.enter_context(nc.sbuf_tensor("c_ut", [64, 64], F32))
        sb_ones = st.enter_context(nc.sbuf_tensor("c_ones", [128, 1], BF16))
        sb_onesr = st.enter_context(nc.sbuf_tensor("c_onesr", [1, 128], BF16))
        sb_onesf = st.enter_context(nc.sbuf_tensor("c_onesf", [64, 128], F32))
        sb_nw = st.enter_context(nc.sbuf_tensor("c_nw", [128, 32], F32))
        sb_dtb = st.enter_context(nc.sbuf_tensor("c_dtb", [64, 1], F32))
        sb_alog = st.enter_context(nc.sbuf_tensor("c_alog", [64, 1], F32))
        sb_acol = st.enter_context(nc.sbuf_tensor("c_acol", [64, 1], F32))
        sb_cw = st.enter_context(nc.sbuf_tensor("c_cw", [128, 136], F32))
        sb_cb = st.enter_context(nc.sbuf_tensor("c_cb", [128, 34], F32))
        sb_eps = st.enter_context(nc.sbuf_tensor("c_eps", [128, 1], F32))
        sb_onef = st.enter_context(nc.sbuf_tensor("c_onef", [128, 1], F32))

        prog = {k: [] for k in ["sync", "tensor", "vector", "scalar", "gpsimd"]}
        MUTE = [False]

        def em(eng, fn):
            if not MUTE[0]:
                prog[eng].append(fn)

        def barrier():
            target = ph.n + 5
            for eng in ["sync", "tensor", "vector", "scalar", "gpsimd"]:
                def _f(e, _t=target):
                    e.drain()
                    e.nop().then_inc(ph.h, 1)
                    e.wait_ge(ph.h, _t)
                em(eng, _f)
            ph.n = target

        rec = {}

        p12 = contextlib.ExitStack()
        sb_dt = p12.enter_context(nc.sbuf_tensor("dt_cm", [64, T], F32))
        sb_lndt_tok = p12.enter_context(nc.sbuf_tensor("lndt_tok", [64, 2048], F32))
        sb_adt_tok = p12.enter_context(nc.sbuf_tensor("adt_tok", [64, 2048], F32))
        sb_B = p12.enter_context(nc.sbuf_tensor("B_cm", [128, T], BF16))
        sb_C = p12.enter_context(nc.sbuf_tensor("C_cm", [128, T], BF16))

        p1 = contextlib.ExitStack()
        sb_xT = p1.enter_context(nc.sbuf_tensor("xT", [128, 16 * T], BF16))
        sb_u0 = [p1.enter_context(nc.sbuf_tensor(f"u0_{i}", [128, DM], BF16))
                 for i in range(2)]
        sb_wsl = [p1.enter_context(nc.sbuf_tensor(f"wsl{i}", [128, 2048], BF16))
                  for i in range(2)]
        sb_acc = [p1.enter_context(nc.sbuf_tensor(f"acc{i}", [128, T], F32))
                  for i in range(2)]
        sb_cv = [p1.enter_context(nc.sbuf_tensor(f"cv{i}", [128, T], F32))
                 for i in range(2)]
        sb_sp = [p1.enter_context(nc.sbuf_tensor(f"spl{i}", [128, T], BF16))
                 for i in range(2)]
        sb_lncm = p1.enter_context(nc.sbuf_tensor("lndt_cm", [64, T], F32))
        sb_adcm = p1.enter_context(nc.sbuf_tensor("adt_cm", [64, T], F32))

        def _f(sync):
            for dst, src in [(sb_identb, d_identb), (sb_identf, d_identf),
                             (sb_ut, d_ut), (sb_ones, d_ones), (sb_onesr, d_onesr), (sb_onesf, d_onesf),
                             (sb_nw, d_nw), (sb_dtb, d_dtb),
                             (sb_alog, d_alog), (sb_cw, d_cw), (sb_cb, d_cb)]:
                sync.dma_start(dst[:], src.ap()).then_inc(sy.h, 16)
        em("sync", _f)
        sy.n += 11 * 16
        consts_at = sy.n

        gp0 = gp.n
        for i in range(16):
            def _f(g, i=i, gp0=gp0):
                if i >= 2:
                    g.wait_ge(gp.h, gp0 + (i - 2) * 32 + 32)
                g.dma_start(sb_u0[i % 2][:], d_u.ap()[i * 128:(i + 1) * 128, :]).then_inc(gp.h, 16)
                g.wait_ge(gp.h, gp0 + i * 32 + 16)
                g.dma_start(d_ubf.ap()[i * 128:(i + 1) * 128, :], sb_u0[i % 2][:]).then_inc(gp.h, 16)
            em("gpsimd", _f)
            gp.n += 32
        ubf_at = gp.n

        for dci in range(16):
            def _f(sync, dci=dci, ubf_at=ubf_at):
                if dci == 0:
                    sync.wait_ge(gp.h, ubf_at)
                sync.dma_start_transpose(
                    sb_xT[:, dci * T:(dci + 1) * T],
                    d_ubf.ap()[:, dci * 128:(dci + 1) * 128]).then_inc(sy.h, 16)
            em("sync", _f)
            sy.n += 16

        def _f(sync, sy_at=sy.n):
            sync.wait_ge(sy.h, sy_at)
        em("sync", _f)

        def _f(sc, consts_at=consts_at):
            sc.wait_ge(sy.h, consts_at)
            sc.activation(sb_acol[:], sb_alog[:], AF.Exp).then_inc(ss.h, 1)
        em("scalar", _f)
        ss.n += 1

        def _f(v):
            v.memset(sb_eps[:], 1e-5)
            v.memset(sb_onef[:], 1.0).then_inc(vv.h, 1)
        em("vector", _f)
        vv.n += 1

        barrier()
        MUTE[0] = upto < 1

        def _wld(g, c):
            w = 64 if c == 66 else 128
            src = bass.AP(d_win, c * 128, [[8512, 128], [128 * 8512, 16], [1, w]])
            g.dma_start(sb_wsl[c % 2][:, 0:16 * w].rearrange("p (d w) -> p d w", d=16),
                        src).then_inc(gp.h, 16)

        def _f(g):
            _wld(g, 0)
        em("gpsimd", _f)
        gp.n += 16
        wld_base = gp.n
        psf_base1 = psf.n
        last_spill = {0: 0, 1: 0}
        last_silu = {0: 0, 1: 0}

        for c in range(min(67, p1n)):
            w = 64 if c == 66 else 128
            ps = ps_a if c % 2 == 0 else ps_b
            if c + 1 < min(67, p1n):
                def _f(g, c=c, tm_at=tm.n):
                    g.wait_ge(tm.h, tm_at)
                    _wld(g, c + 1)
                em("gpsimd", _f)
                gp.n += 16
            wld_at = wld_base + c * 16

            def _f(t, c=c, w=w, ps=ps, wld_at=wld_at, psf_at=psf_base1 + max(0, c - 1)):
                t.wait_ge(gp.h, wld_at)
                t.wait_ge(psf.h, psf_at)
                if p1mode != 3:
                    mm = None
                    for d in range(16):
                        lhsT = sb_wsl[c % 2][:, d * w:(d + 1) * w]
                        for nb in range(4):
                            mm = t.matmul(ps[0:w, nb * 512:(nb + 1) * 512], lhsT,
                                          sb_xT[:, d * T + nb * 512:d * T + (nb + 1) * 512],
                                          start=(d == 0), stop=(d == 15))
                    mm.then_inc(tm.h, 1)
                else:
                    t.nop().then_inc(tm.h, 1)
            em("tensor", _f)
            tm.n += 1
            mm_done = tm.n

            if c < 32 or (p1mode == 0 and c < 66) or (p1mode in (2, 3) and c < 67):
                def _f(sc, c=c, ps=ps, mm_done=mm_done, spl_at=last_spill[c % 2]):
                    sc.wait_ge(tm.h, mm_done)
                    sc.wait_ge(spl.h, spl_at)
                    if p1mode in (2, 3):
                        sc.memzero(sb_sp[c % 2][:, 0:8]).then_inc(ss.h, 1)
                    else:
                        sc.activation(sb_sp[c % 2][:], ps[:], AF.Copy).then_inc(ss.h, 1)
                    sc.drain()
                    sc.nop().then_inc(psf.h, 1)
                em("scalar", _f)
                ss.n += 1
                psf.n += 1
                s_done = ss.n

                def _f(g, c=c, s_done=s_done):
                    g.wait_ge(ss.h, s_done)
                    if p1mode in (2, 3):
                        g.dma_start(d_z.ap()[0:128, 0:8], sb_sp[c % 2][:, 0:8]).then_inc(spl.h, 16)
                        return
                    dst = d_z if c < 32 else d_xc
                    coff = c if c < 32 else min(c - 32, 31)
                    g.dma_start(dst.ap()[coff * 128:(coff + 1) * 128, :],
                                sb_sp[c % 2][:]).then_inc(spl.h, 16)
                em("gpsimd", _f)
                spl.n += 16
                last_spill[c % 2] = spl.n
            elif c < 66:
                j2 = c - 32

                def _f(sc, c=c, ps=ps, mm_done=mm_done, acc_at=rec.get(("cvv", c - 2), 0)):
                    sc.wait_ge(tm.h, mm_done)
                    sc.wait_ge(vv.h, acc_at)
                    sc.activation(sb_acc[c % 2][:], ps[:], AF.Copy).then_inc(ss.h, 1)
                    sc.drain()
                    sc.nop().then_inc(psf.h, 1)
                em("scalar", _f)
                ss.n += 1
                psf.n += 1
                ev_done = ss.n

                def _f(v, c=c, j2=j2, ev_done=ev_done, silu_at=last_silu[c % 2]):
                    v.wait_ge(ss.h, max(ev_done, silu_at))
                    acc = sb_acc[c % 2]
                    cv = sb_cv[c % 2]
                    v.tensor_scalar(cv[:], acc[:], sb_cw[:, j2 * 4 + 3:j2 * 4 + 4],
                                    sb_cb[:, j2:j2 + 1], op0=ALU.mult, op1=ALU.add)
                    vop = None
                    for wt, off in [(2, 1), (1, 2), (0, 3)]:
                        vop = v.scalar_tensor_tensor(cv[:, off:T], acc[:, 0:T - off],
                                                     sb_cw[:, j2 * 4 + wt:j2 * 4 + wt + 1],
                                                     cv[:, off:T], op0=ALU.mult, op1=ALU.add)
                    vop.then_inc(vv.h, 1)
                em("vector", _f)
                vv.n += 1
                rec[("cvv", c)] = vv.n
                v_done = vv.n

                if c < 64:
                    def _f(sc, c=c, v_done=v_done, spl_at=last_spill[c % 2]):
                        sc.wait_ge(vv.h, v_done)
                        sc.wait_ge(spl.h, spl_at)
                        sc.activation(sb_sp[c % 2][:], sb_cv[c % 2][:], AF.Silu).then_inc(ss.h, 1)
                    em("scalar", _f)
                    ss.n += 1
                    last_silu[c % 2] = ss.n
                    s_done = ss.n

                    def _f(g, c=c, s_done=s_done):
                        g.wait_ge(ss.h, s_done)
                        g.dma_start(d_xc.ap()[(c - 32) * 128:(c - 31) * 128, :],
                                    sb_sp[c % 2][:]).then_inc(spl.h, 16)
                    em("gpsimd", _f)
                    spl.n += 16
                    last_spill[c % 2] = spl.n
                else:
                    dst = sb_B if c == 64 else sb_C

                    def _f(sc, c=c, v_done=v_done, dst=dst):
                        sc.wait_ge(vv.h, v_done)
                        sc.activation(dst[:], sb_cv[c % 2][:], AF.Silu).then_inc(ss.h, 1)
                    em("scalar", _f)
                    ss.n += 1
                    last_silu[c % 2] = ss.n
            elif p1mode == 0:
                def _f(sc, ps=ps, mm_done=mm_done):
                    sc.wait_ge(tm.h, mm_done)
                    sc.activation(sb_dt[:], ps[0:64, :], AF.Copy).then_inc(ss.h, 1)
                    sc.nop().then_inc(psf.h, 1)
                em("scalar", _f)
                ss.n += 1
                psf.n += 1
            else:
                def _f(sc, ps=ps, mm_done=mm_done):
                    sc.wait_ge(tm.h, mm_done)
                    sc.activation(sb_lncm[:], ps[0:64, :], AF.Copy).then_inc(ss.h, 1)
                    sc.drain()
                    sc.nop().then_inc(psf.h, 1)
                em("scalar", _f)
                ss.n += 1
                psf.n += 1
                s_ev66 = ss.n

                def _f(v, s_ev66=s_ev66):
                    v.wait_ge(ss.h, s_ev66)
                    v.tensor_scalar(sb_lncm[:], sb_lncm[:], sb_dtb[:], 80.0,
                                    op0=ALU.add, op1=ALU.min).then_inc(vv.h, 1)
                em("vector", _f)
                vv.n += 1
                v_t1 = vv.n

                def _f(sc, v_t1=v_t1):
                    sc.wait_ge(vv.h, v_t1)
                    if p1mode == 5:
                        sc.activation(sb_lncm[:], sb_lncm[:], AF.Copy)
                        sc.activation(sb_dt[:], sb_lncm[:], AF.Copy).then_inc(ss.h, 1)
                    else:
                        sc.activation(sb_lncm[:], sb_lncm[:], AF.Exp)
                        sc.activation(sb_dt[:], sb_lncm[:], AF.Ln,
                                      bias=sb_onef[0:64, :]).then_inc(ss.h, 1)
                em("scalar", _f)
                ss.n += 1
                sp_done = ss.n

                def _f(v, sp_done=sp_done):
                    v.wait_ge(ss.h, sp_done)
                    v.tensor_scalar_max(sb_lncm[:], sb_dt[:], 1e-38)
                    v.tensor_scalar(sb_adcm[:], sb_dt[:], sb_acol[0:64, :], -1.0,
                                    op0=ALU.mult, op1=ALU.mult).then_inc(vv.h, 1)
                em("vector", _f)
                vv.n += 1
                v_done = vv.n

                def _f(sc, v_done=v_done):
                    sc.wait_ge(vv.h, v_done)
                    if p1mode == 5:
                        sc.activation(sb_lncm[:], sb_lncm[:], AF.Copy).then_inc(ss.h, 1)
                    else:
                        sc.activation(sb_lncm[:], sb_lncm[:], AF.Ln).then_inc(ss.h, 1)
                em("scalar", _f)
                ss.n += 1
                ln_done = ss.n

                def _f(t, ln_done=ln_done, psf_at=psf.n):
                    t.wait_ge(ss.h, ln_done)
                    t.wait_ge(psf.h, psf_at)
                    if p1mode != 4:
                        tr = None
                        for i in range(32):
                            t.transpose(ps_a[0:64, i * 64:(i + 1) * 64],
                                        sb_lncm[:, i * 64:(i + 1) * 64],
                                        sb_identf[0:64, 0:64])
                            tr = t.transpose(ps_b[0:64, i * 64:(i + 1) * 64],
                                             sb_adcm[:, i * 64:(i + 1) * 64],
                                             sb_identf[0:64, 0:64])
                        tr.then_inc(tm.h, 1)
                    else:
                        t.nop().then_inc(tm.h, 1)
                em("tensor", _f)
                tm.n += 1
                tr_done = tm.n

                def _f(sc, tr_done=tr_done):
                    sc.wait_ge(tm.h, tr_done)
                    if p1mode == 4:
                        sc.nop().then_inc(ss.h, 1)
                        return
                    sc.activation(sb_lndt_tok[:], ps_a[0:64, 0:2048], AF.Copy)
                    sc.activation(sb_adt_tok[:], ps_b[0:64, 0:2048], AF.Copy).then_inc(ss.h, 1)
                em("scalar", _f)
                ss.n += 1

        def _f(g, spl_at=spl.n):
            g.wait_ge(spl.h, spl_at)
        em("gpsimd", _f)
        barrier()
        p1.close()

        MUTE[0] = upto < 2

        p2 = contextlib.ExitStack()
        sb_ddiag = p2.enter_context(nc.sbuf_tensor("c_ddiag", [64, DI], BF16))
        sb_S = p2.enter_context(nc.sbuf_tensor("S_state", [128, DI], BF16))
        sb_xtok = [p2.enter_context(nc.sbuf_tensor(f"xtok{i}", [128, DI], BF16))
                   for i in range(2)]
        sb_xch = p2.enter_context(nc.sbuf_tensor("xch", [64, DI], BF16))
        sb_zg = [p2.enter_context(nc.sbuf_tensor(f"zg{i}", [128, 4096], BF16))
                 for i in range(2)]
        sb_yg = [p2.enter_context(nc.sbuf_tensor(f"yg{i}", [128, 4096], BF16))
                 for i in range(2)]
        sb_arep = [p2.enter_context(nc.sbuf_tensor(f"arep{i}", [64, DI], F32))
                   for i in range(2)]
        sb_cbr = p2.enter_context(nc.sbuf_tensor("cbr", [64, DI], BF16))
        sb_dac = [p2.enter_context(nc.sbuf_tensor(f"dac{i}", [64, 1], F32))
                  for i in range(2)]
        sb_dax = [p2.enter_context(nc.sbuf_tensor(f"dax{i}", [128, 64], F32))
                  for i in range(2)]
        sb_wtok = [p2.enter_context(nc.sbuf_tensor(f"wtok{i}", [64, 64], F32))
                   for i in range(2)]
        sb_ea = [p2.enter_context(nc.sbuf_tensor(f"ea{i}", [128, 2048], BF16))
                 for i in range(2)]
        sb_btok = [p2.enter_context(nc.sbuf_tensor(f"btok{i}", [64, 128], BF16))
                   for i in range(2)]
        sb_front = [p2.enter_context(nc.sbuf_tensor(f"front{i}", [64, 576], F32))
                    for i in range(2)]
        sb_dneg = p2.enter_context(nc.sbuf_tensor("dneg", [64, DI], BF16))
        sb_sct = sb_dneg
        sb_xdd = p2.enter_context(nc.sbuf_tensor("xdd", [64, DI], BF16))
        sb_yf = p2.enter_context(nc.sbuf_tensor("yf", [128, 2048], BF16))
        sb_ev = p2.enter_context(nc.sbuf_tensor("yev", [128, 2048], BF16))
        sb_stb = [p2.enter_context(nc.sbuf_tensor(f"stb{i}", [128, 512], F32))
                  for i in range(2)]
        sb_zs = p2.enter_context(nc.sbuf_tensor("zs", [128, 2048], BF16))
        sb_sq = sb_zs
        sb_ms = p2.enter_context(nc.sbuf_tensor("msrow", [1, 192], F32))
        sb_sev = p2.enter_context(nc.sbuf_tensor("sevrow", [1, 512], F32))
        sb_wrow = p2.enter_context(nc.sbuf_tensor("wrow", [128, 2048], BF16))
        sb_wst = p2.enter_context(nc.sbuf_tensor("wst", [128, 48], F32))
        sb_wq8p = p2.enter_context(nc.sbuf_tensor("wq8p", [128, 128], FP8))

        def _f(sync):
            sync.dma_start(sb_ddiag[:], d_ddiag.ap()).then_inc(sy.h, 16)
        em("sync", _f)
        sy.n += 16

        PS_ST = [ps_b[:, 0:512], ps_b[:, 512:1024]]
        PS_ACT = ps_b[0:64, 1024:1088]
        PS_ACM = ps_b[0:64, 1088:1152]
        PS_CBT = ps_b[64:128, 1024:1088]
        PS_BT = ps_b.bitcast(BF16)[64:128, 2304:2432]
        PS_DAX = ps_b[:, 1152:1216]
        PS_WT = ps_b[0:64, 1216:1280]
        PS_RMS = ps_b[0:1, 1536:2048]

        def _f(v):
            v.memset(sb_S[:], 0.0).then_inc(vv.h, 1)
        em("vector", _f)
        vv.n += 1
        s_ready = [vv.n]
        stfree_base = stfree.n
        b6_base = b6.n
        b7_base = b7.n
        yfree_base = yfree.n

        def front(k):
            par = k % 2
            fb = sb_front[par]
            A_TOK, A_CM = fb[:, 0:64], fb[:, 64:128]
            A2, DEC = fb[:, 128:192], fb[:, 192:256]
            WCM, EAs, CBTM = fb[:, 256:320], fb[:, 320:384], fb[:, 384:448]

            rec[("xtok", k)] = ss.n
            if k % 2 == 1 and (k + 1) // 2 < NCH // 2:
                g0 = (k + 1) // 2
                def _f(sc, g0=g0, xch_at=rec.get(("xch", 2 * g0 - 3), 0)):
                    sc.wait_ge(ss.h, xch_at)
                    sc.dma_start_transpose(
                        sb_xtok[g0 % 2][:],
                        d_xc.ap()[:, g0 * 128:(g0 + 1) * 128]).then_inc(ss.h, 16)
                em("scalar", _f)
                ss.n += 16
                def _f(sc, g0=g0, vv_at=rec.get(("zs_done", 2 * g0 - 3), 0)):
                    sc.wait_ge(vv.h, vv_at)
                    for j in range(32):
                        sc.dma_start(
                            sb_zg[g0 % 2][:, j * 128:(j + 1) * 128],
                            d_z.ap()[j * 128:(j + 1) * 128,
                                     g0 * 128:(g0 + 1) * 128]).then_inc(ss.h, 16)
                em("scalar", _f)
                ss.n += 32 * 16

            def _f(t, k=k, b6_at=b6_base + 3 * k):
                t.wait_ge(b6.h, b6_at)
                la = sb_adt_tok[:, k * 64:(k + 1) * 64]
                t.matmul(PS_ACT[:, :], sb_ut[:], la, start=True, stop=True)
                t.matmul(PS_ACM[:, :], la, sb_ut[:], start=True, stop=True)
                t.matmul(PS_CBT[:, :], sb_B[:, k * 64:(k + 1) * 64],
                         sb_C[:, k * 64:(k + 1) * 64], start=True, stop=True)
                t.transpose(PS_BT[:, 0:128], sb_B[:, k * 64:(k + 1) * 64],
                            sb_identb[:]).then_inc(tm.h, 1)
            em("tensor", _f)
            tm.n += 1
            t1 = tm.n

            def _f(sc, k=k, t1=t1, A_TOK=A_TOK, A_CM=A_CM, DEC=DEC, EAs=EAs, fb=fb):
                sc.wait_ge(tm.h, t1)
                sc.activation(A_TOK[:, :], PS_ACT[:, :], AF.Copy)
                sc.activation(A_CM[:, :], PS_ACM[:, :], AF.Copy)
                sc.activation(fb[:, 512:576], PS_CBT[:, :], AF.Copy)
                sc.activation(sb_btok[k % 2][:], PS_BT[:, 0:128], AF.Copy)
                sc.activation(DEC[:, :], A_CM[:, :], AF.Exp, scale=-1.0,
                              bias=A_CM[:, 63:64])
                sc.activation(EAs[:, :], A_CM[:, :], AF.Exp)
                sc.activation(sb_dac[k % 2][:], A_CM[:, 63:64], AF.Exp).then_inc(ss.h, 1)
                sc.drain()
                sc.nop().then_inc(b6.h, 1)
            em("scalar", _f)
            ss.n += 1
            b6.n += 1
            s1 = ss.n

            def _f(v, k=k, s1=s1, A_TOK=A_TOK, A2=A2, DEC=DEC, WCM=WCM, CBTM=CBTM, fb=fb):
                v.wait_ge(ss.h, s1)
                lnd = sb_lndt_tok[:, k * 64:(k + 1) * 64]
                v.tensor_sub(A2[:, :], A_TOK[:, :], lnd)
                v.tensor_mul(WCM[:, :], sb_dt[:, k * 64:(k + 1) * 64], DEC[:, :])
                v.tensor_scalar_mul(fb[:, 448:512], sb_identf[0:64, 0:64],
                                    sb_dac[k % 2][:])
                v.tensor_mul(CBTM[:, :], fb[:, 512:576], sb_ut[:]).then_inc(vv.h, 1)
                v.drain()
                v.nop().then_inc(b6.h, 1)
            em("vector", _f)
            vv.n += 1
            b6.n += 1
            v1 = vv.n

            def _f(t, k=k, v1=v1, s1=s1, WCM=WCM, fb=fb):
                t.wait_ge(vv.h, v1)
                t.wait_ge(ss.h, s1)
                t.matmul(PS_DAX[:, :], sb_onesf[:], fb[:, 448:512],
                         start=True, stop=True)
                t.transpose(PS_WT[:, :], WCM[:, :],
                            sb_identf[0:64, 0:64]).then_inc(tm.h, 1)
            em("tensor", _f)
            tm.n += 1
            t2 = tm.n

            def _f(sc, k=k, t2=t2):
                sc.wait_ge(tm.h, t2)
                sc.activation(sb_dax[k % 2][:], PS_DAX[:, :], AF.Copy)
                sc.activation(sb_wtok[k % 2][:], PS_WT[:, :], AF.Copy).then_inc(ss.h, 1)
                sc.drain()
                sc.nop().then_inc(b6.h, 1)
            em("scalar", _f)
            ss.n += 1
            b6.n += 1

            def _f(sync, k=k, par=par, s1=s1, v1=v1, A_CM=A_CM, EAs=EAs,
                   WCM=WCM, CBTM=CBTM, sy0=sy.n):
                sync.wait_ge(ss.h, s1)
                sync.wait_ge(vv.h, v1)
                sync.dma_start(d_arep.ap()[par, :].rearrange("(h l) -> h l", h=64), A_CM[:, :]).then_inc(sy.h, 16)
                sync.dma_start(d_ea.ap()[par, :].rearrange("(h l) -> h l", h=64), EAs[:, :]).then_inc(sy.h, 16)
                sync.dma_start(d_cbm.ap()[par, :].rearrange("(s l) -> s l", s=64), CBTM[:, :]).then_inc(sy.h, 16)
            em("sync", _f)
            sy.n += 3 * 16
            rec[("flats", k)] = sy.n

            def _f(g, k=k, par=par, sy_at=rec[("flats", k)],
                   yf_at=rec.get(("yf", k - 2), 0)):
                g.wait_ge(sy.h, sy_at)
                g.wait_ge(vv.h, yf_at)
                g.dma_start(
                    sb_ea[par][0:64, :].rearrange("p (j l) -> p j l", l=64),
                    bass.AP(d_ea, par * DI, [[0, 64], [128, 32], [1, 64]])).then_inc(gp.h, 16)
                g.dma_start(
                    sb_ea[par][64:128, :].rearrange("p (j l) -> p j l", l=64),
                    bass.AP(d_ea, par * DI + 64, [[0, 64], [128, 32], [1, 64]])).then_inc(gp.h, 16)
            em("gpsimd", _f)
            gp.n += 2 * 16
            rec[("gp_front", k)] = gp.n

        def main(c):
            par = c % 2
            fb = sb_front[par]
            A2 = fb[:, 128:192]
            half = (c // 2) % 2
            rows = slice((c % 2) * 64, (c % 2) * 64 + 64)

            def _f(sync, c=c, fl_at=rec[("flats", c)],
                   dneg_at=rec.get(("dneg", c - 2), 0)):
                sync.wait_ge(sy.h, fl_at)
                sync.wait_ge(vv.h, dneg_at)
                sync.dma_start(
                    sb_arep[c % 2][:].rearrange("p (h l) -> p h l", h=64),
                    d_arep.ap()[c % 2, :].rearrange("(h l) -> h l", h=64)
                    .unsqueeze(0).broadcast_to((64, 64, 64))).then_inc(sy.h, 16)
            em("sync", _f)
            sy.n += 16
            rec[("sync_front", c)] = sy.n

            def _f(g, c=c, fl_at=rec[("flats", c)],
                   sct_at=rec.get(("sct", c - 1), 0)):
                g.wait_ge(sy.h, fl_at)
                g.wait_ge(vv.h, sct_at)
                g.dma_start(
                    sb_cbr[:].rearrange("p (h l) -> p h l", h=64),
                    d_cbm.ap()[c % 2, :].rearrange("(s l) -> s l", s=64)
                    .unsqueeze(1).broadcast_to((64, 64, 64))).then_inc(gp.h, 16)
            em("gpsimd", _f)
            gp.n += 16
            rec[("cbr", c)] = gp.n

            def _f(sc, c=c, half=half, rows=rows, xt_at=rec[("xtok", c)],
                   tyd_at=rec.get(("t_ydiag", c - 1), 0),
                   xdd_at=rec.get(("xdd", c - 1), 0)):
                sc.wait_ge(ss.h, xt_at)
                sc.wait_ge(tm.h, tyd_at)
                sc.wait_ge(vv.h, xdd_at)
                sc.dma_start(sb_xch[:], sb_xtok[half][rows, :]).then_inc(ss.h, 16)
            em("scalar", _f)
            ss.n += 16
            rec[("xch", c)] = ss.n

            def _f(v, c=c, A2=A2, sy_at=rec[("sync_front", c)],
                   tyd_at=rec.get(("t_ydiag", c - 1), 0)):
                v.wait_ge(sy.h, sy_at)
                v.wait_ge(tm.h, tyd_at)
                v.tensor_sub(sb_dneg[:].rearrange("p (h l) -> p h l", h=64),
                             A2[:, :].unsqueeze(2).broadcast_to((64, 64, 64)),
                             sb_arep[c % 2][:].rearrange("p (h l) -> p h l", h=64))
                v.tensor_scalar_max(sb_dneg[:], sb_dneg[:], -80.0).then_inc(vv.h, 1)
            em("vector", _f)
            vv.n += 1
            rec[("dneg", c)] = vv.n
            v_dneg = vv.n

            def _f(sc, v_dneg=v_dneg):
                sc.wait_ge(vv.h, v_dneg)
                sc.activation(sb_dneg[:], sb_dneg[:], AF.Exp, scale=-1.0).then_inc(ss.h, 1)
            em("scalar", _f)
            ss.n += 1
            s_E = ss.n

            def _f(v, c=c, s_E=s_E, gp_at=rec[("cbr", c)],
                   xch_at=rec[("xch", c)],
                   tyd_at=rec.get(("t_ydiag", c - 1), 0),
                   tst_at=rec.get(("stmm", c - 1, 7), 0)):
                v.wait_ge(ss.h, s_E)
                v.wait_ge(gp.h, gp_at)
                v.wait_ge(tm.h, max(tyd_at, tst_at))
                v.tensor_mul(sb_sct[:], sb_dneg[:], sb_cbr[:])
                v.tensor_add(sb_sct[:], sb_sct[:], sb_ddiag[:]).then_inc(vv.h, 1)
                v.wait_ge(ss.h, xch_at)
                v.tensor_mul(sb_xdd[:].rearrange("p (h q) -> p h q", h=64),
                             sb_xch[:].rearrange("p (h q) -> p h q", h=64),
                             sb_wtok[c % 2][:].unsqueeze(2)
                             .broadcast_to((64, 64, 64))).then_inc(vv.h, 1)
            em("vector", _f)
            vv.n += 2
            rec[("sct", c)] = vv.n - 1
            rec[("xdd", c)] = vv.n
            v_sct = vv.n - 1
            v_xdd = vv.n

            def _f(t, c=c, yf_at=yfree_base + c, sv=s_ready[0]):
                t.wait_ge(yfree.h, yf_at)
                t.wait_ge(vv.h, sv)
                mm = None
                for j in range(32):
                    mm = t.matmul(ps_a[:, j * 64:(j + 1) * 64],
                                  sb_S[:, j * 128:(j + 1) * 128],
                                  sb_C[:, c * 64:(c + 1) * 64], start=True, stop=True)
                mm.then_inc(tm.h, 1)
            em("tensor", _f)
            tm.n += 1
            t_yoff = tm.n

            def _f(sc, c=c, t_yoff=t_yoff, yrd_at=rec.get(("v_yadd", c - 1), 0)):
                sc.wait_ge(tm.h, t_yoff)
                sc.wait_ge(vv.h, yrd_at)
                sc.activation(sb_ev[:], ps_a[:], AF.Copy).then_inc(ss.h, 1)
            em("scalar", _f)
            ss.n += 1
            s_yoev = ss.n

            def _f(v, c=c, par=par, s_yoev=s_yoev, gp_at=rec[("gp_front", c)]):
                v.wait_ge(ss.h, s_yoev)
                v.wait_ge(gp.h, gp_at)
                v.tensor_mul(sb_yf[:], sb_ev[:], sb_ea[par][:]).then_inc(vv.h, 1)
            em("vector", _f)
            vv.n += 1
            v_yoff = vv.n

            def _f(t, c=c, v_sct=v_sct, s_yoev=s_yoev):
                t.wait_ge(vv.h, v_sct)
                t.wait_ge(ss.h, s_yoev)
                mm = None
                for j in range(32):
                    for hh in (2 * j, 2 * j + 1):
                        mm = t.matmul(ps_a[(hh % 2) * 64:(hh % 2) * 64 + 64,
                                           j * 64:(j + 1) * 64],
                                      sb_xch[:, hh * 64:(hh + 1) * 64],
                                      sb_sct[:, hh * 64:(hh + 1) * 64],
                                      start=True, stop=True)
                mm.then_inc(tm.h, 1)
            em("tensor", _f)
            tm.n += 1
            t_ydiag = tm.n
            rec[("t_ydiag", c)] = t_ydiag

            for sl in range(8):
                def _f(t, c=c, sl=sl, v_xdd=v_xdd,
                       st_at=stfree_base + max(0, c * 8 + sl - 1)):
                    if sl == 0:
                        t.wait_ge(vv.h, v_xdd)
                    t.wait_ge(stfree.h, st_at)
                    t.matmul(PS_ST[sl % 2][:, :], sb_btok[c % 2][:],
                             sb_xdd[:, sl * 512:(sl + 1) * 512],
                             start=True, stop=True).then_inc(tm.h, 1)
                em("tensor", _f)
                tm.n += 1
                rec[("stmm", c, sl)] = tm.n

            def _f(sc, c=c, t_ydiag=t_ydiag, yo_at=v_yoff):
                sc.wait_ge(tm.h, t_ydiag)
                sc.wait_ge(vv.h, yo_at)
                sc.activation(sb_ev[:], ps_a[:], AF.Copy).then_inc(yfree.h, 1)
            em("scalar", _f)
            yfree.n += 1
            s_ydev = yfree.n

            def _f(v, c=c, s_ydev=s_ydev):
                v.wait_ge(yfree.h, s_ydev)
                v.tensor_add(sb_yf[:], sb_yf[:], sb_ev[:]).then_inc(vv.h, 1)
            em("vector", _f)
            vv.n += 1
            v_yf = vv.n
            rec[("yf", c)] = v_yf
            rec[("v_yadd", c)] = v_yf

            def _f(sc, c=c, stmm={sl: rec[("stmm", c, sl)] for sl in range(8)},
                   vb=vv.n):
                for sl in range(8):
                    sc.wait_ge(tm.h, stmm[sl])
                    if sl >= 2:
                        sc.wait_ge(vv.h, vb + sl - 1)
                    sc.activation(sb_stb[sl % 2][:], PS_ST[sl % 2][:, :],
                                  AF.Copy).then_inc(stfree.h, 1)
            em("scalar", _f)
            sb0 = stfree.n
            stfree.n += 8

            def _f(v, c=c, par=par, sb0=sb0):
                for sl in range(8):
                    v.wait_ge(stfree.h, sb0 + sl + 1)
                    ssl = sb_S[:, sl * 512:(sl + 1) * 512]
                    v.tensor_mul(ssl.rearrange("p (h q) -> p h q", q=64), 
                                 ssl.rearrange("p (h q) -> p h q", q=64),
                                 sb_dax[par][:, sl * 8:(sl + 1) * 8].unsqueeze(2)
                                 .broadcast_to((128, 8, 64)))
                    v.tensor_add(ssl, ssl, sb_stb[sl % 2][:]).then_inc(vv.h, 1)
            em("vector", _f)
            supdv0 = vv.n
            vv.n += 8
            s_ready[0] = vv.n
            rec[("supd", c)] = vv.n
            rec[("supdv", c)] = supdv0

            zoff = (c % 2) * 64
            zgb = sb_zg[(c // 2) % 2]

            def _f(sc, c=c, zoff=zoff, zgb=zgb, z_at=rec[("xtok", (c // 2) * 2)]):
                sc.wait_ge(ss.h, z_at)
                sc.activation(sb_zs[:].rearrange("p (j l) -> p j l", l=64),
                              bass.AP(zgb, zoff, [[4096, 128], [128, 32], [1, 64]]),
                              AF.Silu).then_inc(ss.h, 1)
            em("scalar", _f)
            ss.n += 1
            s_sg = ss.n

            def _f(v, c=c, s_sg=s_sg, v_yf=v_yf):
                v.wait_ge(ss.h, s_sg)
                v.wait_ge(vv.h, v_yf)
                v.tensor_mul(sb_yf[:], sb_yf[:], sb_zs[:]).then_inc(vv.h, 1)
            em("vector", _f)
            vv.n += 1
            v_yf2 = vv.n
            rec[("zs_done", c)] = vv.n

            def _f(sc, c=c, v_yf2=v_yf2, trms_at=rec.get(("t_rms", c - 1), 0)):
                sc.wait_ge(vv.h, v_yf2)
                sc.wait_ge(tm.h, trms_at)
                sc.activation(sb_sq[:], sb_yf[:], AF.Square).then_inc(ss.h, 1)
            em("scalar", _f)
            ss.n += 1
            s_sq = ss.n

            def _f(t, c=c, s_sq=s_sq, b7_at=b7_base + c):
                t.wait_ge(ss.h, s_sq)
                t.wait_ge(b7.h, b7_at)
                mm = None
                for g4 in range(4):
                    mm = t.matmul(PS_RMS[:, :], sb_ones[:],
                                  sb_sq[:, g4 * 512:(g4 + 1) * 512],
                                  start=(g4 == 0), stop=(g4 == 3))
                mm.then_inc(tm.h, 1)
            em("tensor", _f)
            tm.n += 1
            t_rms = tm.n
            rec[("t_rms", c)] = t_rms

            def _f(sc, t_rms=t_rms):
                sc.wait_ge(tm.h, t_rms)
                sc.activation(sb_sev[:], PS_RMS[:, :], AF.Copy).then_inc(b7.h, 1)
            em("scalar", _f)
            b7.n += 1
            s_ev = b7.n

            def _f(v, s_ev=s_ev):
                v.wait_ge(b7.h, s_ev)
                v.tensor_reduce(sb_ms[:, 0:64].unsqueeze(2),
                                bass.AP(sb_sev, 0, [[512, 1], [1, 64], [64, 8]]),
                                axis=mybir.AxisListType.X, op=ALU.add)
                v.drain()
                v.nop().then_inc(vv.h, 1)
            em("vector", _f)
            vv.n += 1
            v_ms = vv.n

            def _f(sc, v_ms=v_ms):
                sc.wait_ge(vv.h, v_ms)
                sc.activation(sb_ms[:, 64:128], sb_ms[:, 0:64], AF.Sqrt,
                              scale=1.0 / 4096.0, bias=sb_eps[0:1, :]).then_inc(ss.h, 1)
            em("scalar", _f)
            ss.n += 1
            s_srt = ss.n

            def _f(v, c=c, s_srt=s_srt, rsp_at=rec.get(("rrspill", c - 1), 0)):
                v.wait_ge(ss.h, s_srt)
                v.wait_ge(gp.h, rsp_at)
                v.reciprocal(sb_ms[:, 128:192], sb_ms[:, 64:128])
                v.drain()
                v.nop().then_inc(vv.h, 1)
            em("vector", _f)
            vv.n += 1
            v_rr = vv.n

            def _f(g, c=c, v_rr=v_rr):
                g.wait_ge(vv.h, v_rr)
                g.dma_start(d_rr.ap()[c * 64:(c + 1) * 64].unsqueeze(0),
                            sb_ms[:, 128:192]).then_inc(gp.h, 16)
            em("gpsimd", _f)
            gp.n += 16
            rec[("rrspill", c)] = gp.n

            ygb = sb_yg[(c // 2) % 2]

            def _f(v, c=c, ygb=ygb, v_yf2=v_yf2,
                   sy_at=rec.get(("yspill", c // 2 - 2), 0)):
                v.wait_ge(sy.h, sy_at)
                v.tensor_mul(
                    bass.AP(ygb, (c % 2) * 64, [[4096, 128], [128, 32], [1, 64]]),
                    sb_yf[:].rearrange("p (j l) -> p j l", l=64),
                    sb_nw[:].unsqueeze(2).broadcast_to((128, 32, 64))).then_inc(vv.h, 1)
            em("vector", _f)
            vv.n += 1
            rec[("ynrm", c)] = vv.n

            if c % 2 == 1:
                def _f(sync, c=c, ygb=ygb, v_at=rec[("ynrm", c)]):
                    sync.wait_ge(vv.h, v_at)
                    sync.dma_start(
                        bass.AP(d_y, (c - 1) * 64,
                                [[T, 128], [128 * T, 32], [1, 128]]),
                        ygb[:].rearrange("p (j l) -> p j l", l=128)).then_inc(sy.h, 16)
                em("sync", _f)
                sy.n += 16
                rec[("yspill", c // 2)] = sy.n

        def wq_quant(j):
            def _f(g, j=j, sp_at=rec.get(("wqspill", j - 1), 0)):
                g.wait_ge(sy.h, sp_at)
                g.dma_start(sb_wrow[:],
                            d_wffn.ap()[j * 128:(j + 1) * 128, :]).then_inc(gp.h, 16)
            em("gpsimd", _f)
            gp.n += 16
            w_at = gp.n

            def _f(v, j=j, w_at=w_at):
                v.wait_ge(gp.h, w_at)
                v.tensor_reduce(sb_wst[:, 0:16].unsqueeze(2),
                                sb_wrow[:].rearrange("p (s f) -> p s f", s=16),
                                axis=mybir.AxisListType.X, op=ALU.max,
                                apply_absolute_value=True)
                v.drain()
                v.tensor_scalar_mul(sb_wst[:, 32:48], sb_wst[:, 0:16], 1.0 / 224.0)
                v.drain()
                v.reciprocal(sb_wst[:, 16:32], sb_wst[:, 32:48])
                v.drain()
                v.nop().then_inc(vv.h, 1)
            em("vector", _f)
            vv.n += 1
            v_w = vv.n

            def _f(sc, j=j, v_w=v_w):
                sc.wait_ge(vv.h, v_w)
                sop = None
                for sgi in range(8):
                    sc.activation(sb_wq8p[:],
                                  sb_wrow[:, sgi * 128:(sgi + 1) * 128],
                                  AF.Copy, scale=sb_wst[:, 16 + sgi:17 + sgi])
                    sop = sc.activation(sb_wrow[:, sgi * 128:(sgi + 1) * 128],
                                        sb_wq8p[:],
                                        AF.Copy, scale=sb_wst[:, 32 + sgi:33 + sgi])
                sop.then_inc(ss.h, 1)
            em("scalar", _f)
            ss.n += 1
            rec[("wqh", j)] = ss.n

        def wq_quant2(j):
            def _f(sc, j=j, h_at=rec[("wqh", j)]):
                sc.wait_ge(ss.h, h_at)
                sop = None
                for sgi in range(8, 16):
                    sc.activation(sb_wq8p[:],
                                  sb_wrow[:, sgi * 128:(sgi + 1) * 128],
                                  AF.Copy, scale=sb_wst[:, 16 + sgi:17 + sgi])
                    sop = sc.activation(sb_wrow[:, sgi * 128:(sgi + 1) * 128],
                                        sb_wq8p[:],
                                        AF.Copy, scale=sb_wst[:, 32 + sgi:33 + sgi])
                sop.then_inc(ss.h, 1)
            em("scalar", _f)
            ss.n += 1
            s_w = ss.n

            def _f(sync, j=j, s_w=s_w):
                sync.wait_ge(ss.h, s_w)
                sync.dma_start(d_wdq.ap()[j * 128:(j + 1) * 128, :],
                               sb_wrow[:]).then_inc(sy.h, 16)
            em("sync", _f)
            sy.n += 16
            rec[("wqspill", j)] = sy.n

        def _f(sc, gp_at=gp.n):
            sc.wait_ge(gp.h, gp_at)
            sc.dma_start_transpose(sb_xtok[0][:],
                                   d_xc.ap()[:, 0:128]).then_inc(ss.h, 16)
            for j in range(32):
                sc.dma_start(sb_zg[0][:, j * 128:(j + 1) * 128],
                             d_z.ap()[j * 128:(j + 1) * 128, 0:128]).then_inc(ss.h, 16)
        em("scalar", _f)
        ss.n += 33 * 16
        rec[("xtok", 0)] = ss.n

        front(0)
        for c in range(NCH):
            if c + 1 < NCH:
                front(c + 1)
            main(c)
            if c % 2 == 0 and c // 2 < 16:
                wq_quant(c // 2)
            if c % 2 == 1 and c // 2 < 16:
                wq_quant2(c // 2)

        def _f(g, gp_at=gp.n):
            g.wait_ge(gp.h, gp_at)
        em("gpsimd", _f)
        barrier()
        p2.close()
        p12.close()

        if upto < 3:
            MUTE[0] = True

        p3 = contextlib.ExitStack()
        sb_wout = p3.enter_context(nc.sbuf_tensor("woutb", [128, 32 * 2048], BF16))
        sb_ygr = [p3.enter_context(nc.sbuf_tensor(f"ygr{i}", [128, 32 * 128], BF16))
                  for i in range(2)]
        sb_xin = p3.enter_context(nc.sbuf_tensor("xin", [128, 2048], BF16))
        sb_rr2 = p3.enter_context(nc.sbuf_tensor("rr2", [128, 16], F32))
        sb_r1 = p3.enter_context(nc.sbuf_tensor("r1", [128, 2048], F32))
        sb_x2f = p3.enter_context(nc.sbuf_tensor("x2f", [128, 2048], F32))
        sb_st1 = p3.enter_context(nc.sbuf_tensor("st1", [128, 72], F32))
        sb_q8 = p3.enter_context(nc.sbuf_tensor("q8t", [128, 128], FP8))
        sb_x2b = p3.enter_context(nc.sbuf_tensor("x2b", [128, 2048], BF16))
        sb_x2qb = p3.enter_context(nc.sbuf_tensor("x2qb", [128, 2048], BF16))
        sb_wqs = p3.enter_context(nc.sbuf_tensor("wqstage", [128, 2048], BF16))

        for j in range(32):
            def _f(g, j=j):
                g.dma_start(sb_wout[:, j * 2048:(j + 1) * 2048],
                            d_wout.ap()[j * 128:(j + 1) * 128, :]).then_inc(gp.h, 16)
            em("gpsimd", _f)
            gp.n += 16
        wout_at = gp.n

        def _f(sync, rr_done=rec[("rrspill", 31)]):
            sync.wait_ge(gp.h, rr_done)
            sync.dma_start(sb_rr2[:], bass.AP(d_rr, 0, [[1, 128], [128, 16]]), allow_slow_non_contiguous=True).then_inc(sy.h, 16)
        em("sync", _f)
        sy.n += 16
        rr_at = sy.n

        psf_base3 = psf.n
        tm_base3 = tm.n
        x2sp = {}
        for tg in range(16):
            ps = ps_a if tg % 2 == 0 else ps_b

            def _f(sync, tg=tg, tm_at=tm_base3 + max(0, tg - 1)):
                sync.wait_ge(tm.h, tm_at)
                for j in range(32):
                    sync.dma_start(
                        sb_ygr[tg % 2][:, j * 128:(j + 1) * 128],
                        d_y.ap()[j * 128:(j + 1) * 128,
                                 tg * 128:(tg + 1) * 128]).then_inc(sy.h, 16)
            em("sync", _f)
            sy.n += 32 * 16
            rec[("yrd", tg)] = sy.n

            def _f(g, tg=tg, v_at=rec.get(("p3r1", tg - 1), 0)):
                g.wait_ge(vv.h, v_at)
                g.dma_start(sb_xin[:],
                            d_u.ap()[tg * 128:(tg + 1) * 128, :]).then_inc(gp.h, 16)
            em("gpsimd", _f)
            gp.n += 16
            xin_at = gp.n

            def _f(sc, tg=tg, sp_at=max(rec.get(("wqTsp", tg - 1), 0),
                                         rec[("wqspill", 15)] if tg == 0 else 0)):
                sc.wait_ge(sy.h, sp_at)
                sc.dma_start_transpose(
                    sb_wqs[:],
                    d_wdq.ap()[:, tg * 128:(tg + 1) * 128]).then_inc(ss.h, 16)
            em("scalar", _f)
            ss.n += 16
            wqt_at = ss.n

            def _f(sync, tg=tg, wqt_at=wqt_at):
                sync.wait_ge(ss.h, wqt_at)
                sync.dma_start(d_wqT.ap()[tg * 128:(tg + 1) * 128, :],
                               sb_wqs[:]).then_inc(sy.h, 16)
            em("sync", _f)
            sy.n += 16
            rec[("wqTsp", tg)] = sy.n

            def _f(t, tg=tg, ps=ps, y_at=rec[("yrd", tg)], wo_at=wout_at,
                   psf_at=psf_base3 + max(0, tg - 1)):
                t.wait_ge(sy.h, y_at)
                if tg == 0:
                    t.wait_ge(gp.h, wo_at)
                t.wait_ge(psf.h, psf_at)
                mm = None
                for j in range(32):
                    lhsT = sb_ygr[tg % 2][:, j * 128:(j + 1) * 128]
                    for nb in range(4):
                        mm = t.matmul(ps[:, nb * 512:(nb + 1) * 512], lhsT,
                                      sb_wout[:, j * 2048 + nb * 512:j * 2048 + (nb + 1) * 512],
                                      start=(j == 0), stop=(j == 31))
                mm.then_inc(tm.h, 1)
            em("tensor", _f)
            tm.n += 1
            mm3 = tm.n

            def _f(sc, tg=tg, ps=ps, mm3=mm3, r1_at=rec.get(("p3r1", tg - 1), 0)):
                sc.wait_ge(tm.h, mm3)
                sc.wait_ge(vv.h, r1_at)
                sc.activation(sb_r1[:], ps[:], AF.Copy)
                sc.drain()
                sc.nop().then_inc(psf.h, 1)
            em("scalar", _f)
            psf.n += 1
            s_r1 = psf.n

            def _f(v, tg=tg, s_r1=s_r1, xin_at=xin_at, rr_at=rr_at):
                v.wait_ge(psf.h, s_r1)
                v.wait_ge(gp.h, xin_at)
                v.wait_ge(sy.h, rr_at)
                v.scalar_tensor_tensor(sb_r1[:], sb_r1[:], sb_rr2[:, tg:tg + 1],
                                       sb_xin[:], op0=ALU.mult, op1=ALU.add)
                v.reduce_sum(sb_st1[:, 0:1], sb_r1[:], axis=mybir.AxisListType.X)
                v.drain()
                v.tensor_scalar_mul(sb_st1[:, 1:2], sb_st1[:, 0:1], 1.0 / 2048.0)
                v.drain()
                v.nop().then_inc(vv.h, 1)
            em("vector", _f)
            vv.n += 1
            rec[("p3r1", tg)] = vv.n
            v_m = vv.n

            def _f(sc, v_m=v_m):
                sc.wait_ge(vv.h, v_m)
                sc.activation(sb_x2f[:], sb_r1[:], AF.Square,
                              bias=sb_st1[:, 1:2], scale=-1.0,
                              accum_out=sb_st1[:, 2:3]).then_inc(ss.h, 1)
            em("scalar", _f)
            ss.n += 1
            s_v = ss.n

            def _f(sc, s_v=s_v):
                sc.wait_ge(ss.h, s_v)
                sc.activation(sb_st1[:, 3:4], sb_st1[:, 2:3], AF.Sqrt,
                              scale=1.0 / 2048.0, bias=sb_eps[:, :])
                sc.drain()
                sc.nop().then_inc(ss.h, 1)
            em("scalar", _f)
            ss.n += 1
            s_sd = ss.n

            def _f(v, tg=tg, s_sd=s_sd, g_at=x2sp.get(tg - 1, 0),
                   sq_at=rec.get(("x2q_s", tg - 1), 0)):
                v.wait_ge(ss.h, max(s_sd, sq_at))
                v.wait_ge(gp.h, g_at)
                v.tensor_scalar(sb_x2f[:], sb_r1[:], sb_st1[:, 1:2], 1.0,
                                op0=ALU.subtract, op1=ALU.mult)
                v.tensor_reduce(sb_st1[:, 6:22].unsqueeze(2),
                                sb_x2f[:].rearrange("p (s f) -> p s f", s=16),
                                axis=mybir.AxisListType.X, op=ALU.max,
                                apply_absolute_value=True)
                v.drain()
                v.tensor_scalar_mul(sb_st1[:, 38:54], sb_st1[:, 6:22], 1.0 / 224.0)
                v.drain()
                v.reciprocal(sb_st1[:, 5:6], sb_st1[:, 3:4])
                v.reciprocal(sb_st1[:, 22:38], sb_st1[:, 38:54])
                v.drain()
                v.nop().then_inc(vv.h, 1)
            em("vector", _f)
            vv.n += 1
            v_q = vv.n

            def _f(sc, tg=tg, v_q=v_q):
                sc.wait_ge(vv.h, v_q)
                sc.activation(sb_st1[:, 54:70], sb_st1[:, 38:54], AF.Copy,
                              scale=sb_st1[:, 5:6])
                for sgi in range(16):
                    sc.activation(sb_q8[:],
                                  sb_x2f[:, sgi * 128:(sgi + 1) * 128], AF.Copy,
                                  scale=sb_st1[:, 22 + sgi:23 + sgi])
                    sc.activation(sb_x2qb[:, sgi * 128:(sgi + 1) * 128],
                                  sb_q8[:], AF.Copy,
                                  scale=sb_st1[:, 54 + sgi:55 + sgi])
                sc.activation(sb_x2b[:], sb_x2f[:], AF.Copy,
                              scale=sb_st1[:, 5:6])
                sc.drain()
                sc.nop().then_inc(ss.h, 1)
            em("scalar", _f)
            ss.n += 1
            s_x2b = ss.n
            rec[("x2q_s", tg)] = s_x2b

            def _f(g, tg=tg, s_x2b=s_x2b):
                g.wait_ge(ss.h, s_x2b)
                g.dma_start(d_x2.ap()[tg * 128:(tg + 1) * 128, :],
                            sb_x2b[:]).then_inc(gp.h, 16)
                g.dma_start(d_x2q.ap()[tg * 128:(tg + 1) * 128, :],
                            sb_x2qb[:]).then_inc(gp.h, 16)
            em("gpsimd", _f)
            gp.n += 32
            x2sp[tg] = gp.n

        barrier()
        p3.close()

        if upto < 4:
            MUTE[0] = True

        p4 = contextlib.ExitStack()
        sb_wq = p4.enter_context(nc.sbuf_tensor("wqT", [128, 16 * 2048], BF16))
        sb_st4 = p4.enter_context(nc.sbuf_tensor("st4", [128, 60], F32))
        sb_xq = [p4.enter_context(nc.sbuf_tensor(f"xqT{i}", [128, 2048], BF16))
                 for i in range(2)]
        sb_x2r = [p4.enter_context(nc.sbuf_tensor(f"x2r{i}", [128, 2048], BF16))
                  for i in range(2)]
        sb_r2 = p4.enter_context(nc.sbuf_tensor("r2", [128, 2048], F32))
        sb_sq4 = p4.enter_context(nc.sbuf_tensor("sq4", [128, 2048], F32))
        sb_o4 = [p4.enter_context(nc.sbuf_tensor(f"o4_{i}", [128, 2048], F32))
                 for i in range(2)]

        for kc in range(16):
            def _f(sync, kc=kc, sp_at=rec[("wqTsp", 15)] if kc == 0 else 0):
                sync.wait_ge(sy.h, sp_at)
                sync.dma_start(sb_wq[:, kc * 2048:(kc + 1) * 2048],
                               d_wqT.ap()[kc * 128:(kc + 1) * 128, :]).then_inc(sy.h, 16)
            em("sync", _f)
            sy.n += 16
        wq_ld_at = sy.n

        psf_base4 = psf.n
        tm_base4 = tm.n
        outsp = {}
        for tg in range(16):
            ps = ps_a if tg % 2 == 0 else ps_b

            def _f(sync, tg=tg, tm_at=tm_base4 + max(0, tg - 1)):
                sync.wait_ge(tm.h, tm_at)
                for kc in range(16):
                    sync.dma_start_transpose(
                        sb_xq[tg % 2][:, kc * 128:(kc + 1) * 128],
                        d_x2q.ap()[tg * 128:(tg + 1) * 128,
                                   kc * 128:(kc + 1) * 128]).then_inc(sy.h, 16)
            em("sync", _f)
            sy.n += 16 * 16
            xq_at = sy.n

            def _f(sync, tg=tg, v_at=rec.get(("p4r2", tg - 1), 0)):
                sync.wait_ge(vv.h, v_at)
                sync.dma_start(sb_x2r[tg % 2][:],
                               d_x2.ap()[tg * 128:(tg + 1) * 128, :]).then_inc(sy.h, 16)
            em("sync", _f)
            sy.n += 16
            x2_at = sy.n

            def _f(t, tg=tg, ps=ps, xq_at=xq_at, wq_ld_at=wq_ld_at,
                   psf_at=psf_base4 + max(0, tg - 1)):
                t.wait_ge(sy.h, max(xq_at, wq_ld_at))
                t.wait_ge(psf.h, psf_at)
                mm = None
                for kc in range(16):
                    lhsT = sb_xq[tg % 2][:, kc * 128:(kc + 1) * 128]
                    for nb in range(4):
                        mm = t.matmul(ps[:, nb * 512:(nb + 1) * 512], lhsT,
                                      sb_wq[:, kc * 2048 + nb * 512:kc * 2048 + (nb + 1) * 512],
                                      start=(kc == 0), stop=(kc == 15))
                mm.then_inc(tm.h, 1)
            em("tensor", _f)
            tm.n += 1
            mm4 = tm.n

            def _f(sc, tg=tg, ps=ps, mm4=mm4, r2_at=rec.get(("p4r2", tg - 1), 0)):
                sc.wait_ge(tm.h, mm4)
                sc.wait_ge(vv.h, r2_at)
                sc.activation(sb_r2[:], ps[:], AF.Copy)
                sc.drain()
                sc.nop().then_inc(psf.h, 1)
            em("scalar", _f)
            psf.n += 1
            s_r2 = psf.n

            def _f(v, tg=tg, s_r2=s_r2, x2_at=x2_at):
                v.wait_ge(psf.h, s_r2)
                v.wait_ge(sy.h, x2_at)
                v.tensor_add(sb_r2[:], sb_r2[:], sb_x2r[tg % 2][:])
                v.reduce_sum(sb_st4[:, 32:33], sb_r2[:], axis=mybir.AxisListType.X)
                v.drain()
                v.tensor_scalar_mul(sb_st4[:, 33:34], sb_st4[:, 32:33], 1.0 / 2048.0)
                v.drain()
                v.nop().then_inc(vv.h, 1)
            em("vector", _f)
            vv.n += 1
            rec[("p4r2", tg)] = vv.n
            v_m4 = vv.n

            def _f(sc, v_m4=v_m4):
                sc.wait_ge(vv.h, v_m4)
                sc.activation(sb_sq4[:], sb_r2[:], AF.Square,
                              bias=sb_st4[:, 33:34], scale=-1.0,
                              accum_out=sb_st4[:, 34:35])
                sc.activation(sb_st4[:, 36:37], sb_st4[:, 34:35], AF.Sqrt,
                              scale=1.0 / 2048.0, bias=sb_eps[:, :])
                sc.drain()
                sc.nop().then_inc(ss.h, 1)
            em("scalar", _f)
            ss.n += 1
            s_sd4 = ss.n

            def _f(v, tg=tg, s_sd4=s_sd4, g_at=outsp.get(tg - 2, 0)):
                v.wait_ge(ss.h, s_sd4)
                v.wait_ge(gp.h, g_at)
                v.tensor_scalar(sb_sq4[:], sb_r2[:], sb_st4[:, 33:34], 1.0,
                                op0=ALU.subtract, op1=ALU.mult)
                v.reciprocal(sb_st4[:, 37:38], sb_st4[:, 36:37])
                v.drain()
                v.nop().then_inc(vv.h, 1)
            em("vector", _f)
            vv.n += 1
            v_o = vv.n

            def _f(sc, tg=tg, v_o=v_o):
                sc.wait_ge(vv.h, v_o)
                sc.activation(sb_o4[tg % 2][:], sb_sq4[:], AF.Copy,
                              scale=sb_st4[:, 37:38])
                sc.drain()
                sc.nop().then_inc(ss.h, 1)
            em("scalar", _f)
            ss.n += 1
            s_o4 = ss.n

            def _f(g, tg=tg, s_o4=s_o4):
                g.wait_ge(ss.h, s_o4)
                g.dma_start(d_out.ap()[tg * 128:(tg + 1) * 128, :],
                            sb_o4[tg % 2][:]).then_inc(gp.h, 16)
            em("gpsimd", _f)
            gp.n += 16
            outsp[tg] = gp.n

        def _f(g, gp_at=gp.n):
            g.wait_ge(gp.h, gp_at)
        em("gpsimd", _f)
        barrier()
        p4.close()

        global _DBG
        _DBG = {k: v.n for k, v in S.items()}
        with nc.Block() as block:
            @block.sync
            def _(sync):
                for fn in prog["sync"]:
                    fn(sync)

            @block.tensor
            def _(t):
                for fn in prog["tensor"]:
                    fn(t)

            @block.vector
            def _(v):
                for fn in prog["vector"]:
                    fn(v)

            @block.scalar
            def _(sc):
                for fn in prog["scalar"]:
                    fn(sc)

            @block.gpsimd
            def _(g):
                for fn in prog["gpsimd"]:
                    fn(g)

    return nc


_DBG = {}
_NC_CACHE = None


def _get_nc():
    global _NC_CACHE
    if _NC_CACHE is None:
        _NC_CACHE = build_program()
    return _NC_CACHE


def _host_constants(inputs):
    D = np.asarray(inputs["L0_attn_D_param"], np.float32)
    ddiag = np.zeros((64, DI), np.float32)
    for h in range(64):
        ddiag[np.arange(64), h * 64 + np.arange(64)] = D[h]
    conv_w = np.asarray(inputs["L0_attn_conv_w"], np.float32)
    conv_b = np.asarray(inputs["L0_attn_conv_b"], np.float32)
    cw = conv_w.reshape(34, 128, 4).transpose(1, 0, 2).reshape(128, 136)
    cb = conv_b.reshape(34, 128).T.copy()
    nw = np.asarray(inputs["L0_attn_norm_w"], np.float32).reshape(32, 128).T.copy()
    ut = np.triu(np.ones((64, 64), np.float32))
    return {
        "w_in": np.ascontiguousarray(np.asarray(inputs["L0_attn_W_in"], np.float32)),
        "w_out": np.ascontiguousarray(np.asarray(inputs["L0_attn_W_out"], np.float32)),
        "w_ffn": np.ascontiguousarray(np.asarray(inputs["L0_ffn_weight"], np.float32)),
        "cw": np.ascontiguousarray(cw),
        "cb": np.ascontiguousarray(cb),
        "dtb": np.asarray(inputs["L0_attn_dt_bias"], np.float32).reshape(64, 1),
        "alog": np.asarray(inputs["L0_attn_A_log"], np.float32).reshape(64, 1),
        "ddiag": ddiag.astype(BF),
        "nw": np.ascontiguousarray(nw),
        "identb": np.eye(128, dtype=np.float32).astype(BF),
        "identf": np.eye(128, dtype=np.float32),
        "ut": np.ascontiguousarray(ut),
        "ones128": np.ones((128, 1), np.float32).astype(BF),
        "onesrow": np.ones((1, 128), np.float32).astype(BF),
        "onesf": np.ones((64, 128), np.float32),
    }


def kernel(**inputs):
    nc = _get_nc()
    const = _host_constants(inputs)
    u = np.ascontiguousarray(np.asarray(inputs["L0_attn_u"], np.float32))
    in_maps = []
    for core in range(8):
        m = dict(const)
        m["u"] = u[core]
        in_maps.append(m)
    res = run_bass_kernel_spmd(nc, in_maps, core_ids=list(range(8)))
    out = np.stack([np.asarray(r["out"], np.float32) for r in res.results], axis=0)
    return out


# version 12
# speedup vs baseline: 1.2016x; 1.1845x over previous
import contextlib
import numpy as np
import ml_dtypes
import concourse.bass as bass
import concourse.mybir as mybir
from concourse.bass_utils import run_bass_kernel_spmd

F32 = mybir.dt.float32
BF16 = mybir.dt.bfloat16
FP8 = mybir.dt.float8e4
AF = mybir.ActivationFunctionType
ALU = mybir.AluOpType
BF = ml_dtypes.bfloat16

T = 2048
DM = 2048
DI = 4096
NCH = 32


class Ctr:
    def __init__(self, handle):
        self.h = handle
        self.n = 0


def build_program(upto=99, p1mode=1, p1n=67, dbg=False):
    nc = bass.Bass("TRN2", target_bir_lowering=False, detect_race_conditions=False)

    d_u = nc.dram_tensor("u", [T, DM], F32, kind="ExternalInput")
    d_win = nc.dram_tensor("w_in", [DM, 8512], F32, kind="ExternalInput")
    d_wout = nc.dram_tensor("w_out", [DI, DM], F32, kind="ExternalInput")
    d_wffn = nc.dram_tensor("w_ffn", [DM, DM], F32, kind="ExternalInput")
    d_cw = nc.dram_tensor("cw", [128, 136], F32, kind="ExternalInput")
    d_cb = nc.dram_tensor("cb", [128, 34], F32, kind="ExternalInput")
    d_dtb = nc.dram_tensor("dtb", [64, 1], F32, kind="ExternalInput")
    d_alog = nc.dram_tensor("alog", [64, 1], F32, kind="ExternalInput")
    d_ddiag = nc.dram_tensor("ddiag", [64, DI], BF16, kind="ExternalInput")
    d_nw = nc.dram_tensor("nw", [128, 32], F32, kind="ExternalInput")
    d_identb = nc.dram_tensor("identb", [128, 128], BF16, kind="ExternalInput")
    d_identf = nc.dram_tensor("identf", [128, 128], F32, kind="ExternalInput")
    d_ut = nc.dram_tensor("ut", [64, 64], F32, kind="ExternalInput")
    d_ones = nc.dram_tensor("ones128", [128, 1], BF16, kind="ExternalInput")
    d_onesr = nc.dram_tensor("onesrow", [1, 128], BF16, kind="ExternalInput")
    d_onesf = nc.dram_tensor("onesf", [64, 128], F32, kind="ExternalInput")
    d_out = nc.dram_tensor("out", [T, DM], F32, kind="ExternalOutput")

    KIND = "ExternalOutput" if dbg else "Internal"
    d_ubf = nc.dram_tensor("ubf_scr", [T, DM], BF16)
    d_z = nc.dram_tensor("z_scr", [DI, T], BF16, kind=KIND)
    d_xc = nc.dram_tensor("xc_scr", [DI, T], BF16, kind=KIND)
    d_y = nc.dram_tensor("y_scr", [DI, T], BF16, kind=KIND)
    d_rr = nc.dram_tensor("rr_scr", [T], F32, kind=KIND)
    d_x2 = nc.dram_tensor("x2_scr", [T, DM], BF16, kind=KIND)
    d_x2q = nc.dram_tensor("x2q_scr", [T, DM], BF16, kind=KIND)
    d_wdq = nc.dram_tensor("wdq_scr", [DM, DM], BF16, kind=KIND)
    d_wqT = nc.dram_tensor("wqT_scr", [DM, DM], BF16, kind=KIND)
    d_arep = nc.dram_tensor("arep_scr", [2, DI], F32)
    d_ea = nc.dram_tensor("ea_scr", [2, DI], F32)
    d_cbm = nc.dram_tensor("cbm_scr", [2, DI], F32)

    st = contextlib.ExitStack()
    with st:
        ps_a = st.enter_context(nc.psum_tensor("ps_a", [128, 2048], F32))
        ps_b = st.enter_context(nc.psum_tensor("ps_b", [128, 2048], F32))

        S = {}
        for name in ["sy", "gp", "tm", "vv", "ss", "ph", "psf", "yfree",
                     "stfree", "b6", "b7", "spl"]:
            S[name] = Ctr(st.enter_context(nc.semaphore(name)))
        sy, gp, tm, vv, ss = S["sy"], S["gp"], S["tm"], S["vv"], S["ss"]
        ph, psf, yfree = S["ph"], S["psf"], S["yfree"]
        stfree, b6, b7, spl = S["stfree"], S["b6"], S["b7"], S["spl"]

        sb_identb = st.enter_context(nc.sbuf_tensor("c_identb", [128, 128], BF16))
        sb_identf = st.enter_context(nc.sbuf_tensor("c_identf", [128, 128], F32))
        sb_ut = st.enter_context(nc.sbuf_tensor("c_ut", [64, 64], F32))
        sb_ones = st.enter_context(nc.sbuf_tensor("c_ones", [128, 1], BF16))
        sb_onesr = st.enter_context(nc.sbuf_tensor("c_onesr", [1, 128], BF16))
        sb_onesf = st.enter_context(nc.sbuf_tensor("c_onesf", [64, 128], F32))
        sb_nw = st.enter_context(nc.sbuf_tensor("c_nw", [128, 32], F32))
        sb_dtb = st.enter_context(nc.sbuf_tensor("c_dtb", [64, 1], F32))
        sb_alog = st.enter_context(nc.sbuf_tensor("c_alog", [64, 1], F32))
        sb_acol = st.enter_context(nc.sbuf_tensor("c_acol", [64, 1], F32))
        sb_cw = st.enter_context(nc.sbuf_tensor("c_cw", [128, 136], F32))
        sb_cb = st.enter_context(nc.sbuf_tensor("c_cb", [128, 34], F32))
        sb_eps = st.enter_context(nc.sbuf_tensor("c_eps", [128, 1], F32))
        sb_onef = st.enter_context(nc.sbuf_tensor("c_onef", [128, 1], F32))

        prog = {k: [] for k in ["sync", "tensor", "vector", "scalar", "gpsimd"]}
        MUTE = [False]

        def em(eng, fn):
            if not MUTE[0]:
                prog[eng].append(fn)

        def barrier():
            target = ph.n + 5
            for eng in ["sync", "tensor", "vector", "scalar", "gpsimd"]:
                def _f(e, _t=target):
                    e.drain()
                    e.nop().then_inc(ph.h, 1)
                    e.wait_ge(ph.h, _t)
                em(eng, _f)
            ph.n = target

        rec = {}

        p12 = contextlib.ExitStack()
        sb_dt = p12.enter_context(nc.sbuf_tensor("dt_cm", [64, T], F32))
        sb_lndt_tok = p12.enter_context(nc.sbuf_tensor("lndt_tok", [64, 2048], F32))
        sb_adt_tok = p12.enter_context(nc.sbuf_tensor("adt_tok", [64, 2048], F32))
        sb_B = p12.enter_context(nc.sbuf_tensor("B_cm", [128, T], BF16))
        sb_C = p12.enter_context(nc.sbuf_tensor("C_cm", [128, T], BF16))

        p1 = contextlib.ExitStack()
        sb_xT = p1.enter_context(nc.sbuf_tensor("xT", [128, 16 * T], BF16))
        sb_u0 = [p1.enter_context(nc.sbuf_tensor(f"u0_{i}", [128, DM], BF16))
                 for i in range(2)]
        sb_wsl = [p1.enter_context(nc.sbuf_tensor(f"wsl{i}", [128, 2048], BF16))
                  for i in range(2)]
        sb_acc = [p1.enter_context(nc.sbuf_tensor(f"acc{i}", [128, T], F32))
                  for i in range(2)]
        sb_cv = [p1.enter_context(nc.sbuf_tensor(f"cv{i}", [128, T], F32))
                 for i in range(2)]
        sb_sp = [p1.enter_context(nc.sbuf_tensor(f"spl{i}", [128, T], BF16))
                 for i in range(2)]
        sb_lncm = p1.enter_context(nc.sbuf_tensor("lndt_cm", [64, T], F32))
        sb_adcm = p1.enter_context(nc.sbuf_tensor("adt_cm", [64, T], F32))

        def _f(sync):
            for dst, src in [(sb_identb, d_identb), (sb_identf, d_identf),
                             (sb_ut, d_ut), (sb_ones, d_ones), (sb_onesr, d_onesr), (sb_onesf, d_onesf),
                             (sb_nw, d_nw), (sb_dtb, d_dtb),
                             (sb_alog, d_alog), (sb_cw, d_cw), (sb_cb, d_cb)]:
                sync.dma_start(dst[:], src.ap()).then_inc(sy.h, 16)
        em("sync", _f)
        sy.n += 11 * 16
        consts_at = sy.n

        gp0 = gp.n
        for i in range(16):
            def _f(g, i=i, gp0=gp0):
                if i >= 2:
                    g.wait_ge(gp.h, gp0 + (i - 2) * 32 + 32)
                g.dma_start(sb_u0[i % 2][:], d_u.ap()[i * 128:(i + 1) * 128, :]).then_inc(gp.h, 16)
                g.wait_ge(gp.h, gp0 + i * 32 + 16)
                g.dma_start(d_ubf.ap()[i * 128:(i + 1) * 128, :], sb_u0[i % 2][:]).then_inc(gp.h, 16)
            em("gpsimd", _f)
            gp.n += 32
        ubf_at = gp.n

        for dci in range(16):
            def _f(sync, dci=dci, ubf_at=ubf_at):
                if dci == 0:
                    sync.wait_ge(gp.h, ubf_at)
                sync.dma_start_transpose(
                    sb_xT[:, dci * T:(dci + 1) * T],
                    d_ubf.ap()[:, dci * 128:(dci + 1) * 128]).then_inc(sy.h, 16)
            em("sync", _f)
            sy.n += 16

        def _f(sync, sy_at=sy.n):
            sync.wait_ge(sy.h, sy_at)
        em("sync", _f)

        def _f(sc, consts_at=consts_at):
            sc.wait_ge(sy.h, consts_at)
            sc.activation(sb_acol[:], sb_alog[:], AF.Exp).then_inc(ss.h, 1)
        em("scalar", _f)
        ss.n += 1

        def _f(v):
            v.memset(sb_eps[:], 1e-5)
            v.memset(sb_onef[:], 1.0).then_inc(vv.h, 1)
        em("vector", _f)
        vv.n += 1

        barrier()
        MUTE[0] = upto < 1

        def _wld(g, c):
            w = 64 if c == 66 else 128
            src = bass.AP(d_win, c * 128, [[8512, 128], [128 * 8512, 16], [1, w]])
            g.dma_start(sb_wsl[c % 2][:, 0:16 * w].rearrange("p (d w) -> p d w", d=16),
                        src).then_inc(gp.h, 16)

        def _f(g):
            _wld(g, 0)
        em("gpsimd", _f)
        gp.n += 16
        wld_base = gp.n
        psf_base1 = psf.n
        last_spill = {0: 0, 1: 0}
        last_silu = {0: 0, 1: 0}

        for c in range(min(67, p1n)):
            w = 64 if c == 66 else 128
            ps = ps_a if c % 2 == 0 else ps_b
            if c + 1 < min(67, p1n):
                def _f(g, c=c, tm_at=tm.n):
                    g.wait_ge(tm.h, tm_at)
                    _wld(g, c + 1)
                em("gpsimd", _f)
                gp.n += 16
            wld_at = wld_base + c * 16

            def _f(t, c=c, w=w, ps=ps, wld_at=wld_at, psf_at=psf_base1 + max(0, c - 1)):
                t.wait_ge(gp.h, wld_at)
                t.wait_ge(psf.h, psf_at)
                if p1mode != 3:
                    mm = None
                    for d in range(16):
                        lhsT = sb_wsl[c % 2][:, d * w:(d + 1) * w]
                        for nb in range(4):
                            mm = t.matmul(ps[0:w, nb * 512:(nb + 1) * 512], lhsT,
                                          sb_xT[:, d * T + nb * 512:d * T + (nb + 1) * 512],
                                          start=(d == 0), stop=(d == 15))
                    mm.then_inc(tm.h, 1)
                else:
                    t.nop().then_inc(tm.h, 1)
            em("tensor", _f)
            tm.n += 1
            mm_done = tm.n

            if c < 32 or (p1mode == 0 and c < 66) or (p1mode in (2, 3) and c < 67):
                def _f(sc, c=c, ps=ps, mm_done=mm_done, spl_at=last_spill[c % 2]):
                    sc.wait_ge(tm.h, mm_done)
                    sc.wait_ge(spl.h, spl_at)
                    if p1mode in (2, 3):
                        sc.memzero(sb_sp[c % 2][:, 0:8]).then_inc(ss.h, 1)
                    else:
                        sc.activation(sb_sp[c % 2][:], ps[:], AF.Copy).then_inc(ss.h, 1)
                    sc.drain()
                    sc.nop().then_inc(psf.h, 1)
                em("scalar", _f)
                ss.n += 1
                psf.n += 1
                s_done = ss.n

                def _f(sync, c=c, s_done=s_done):
                    sync.wait_ge(ss.h, s_done)
                    if p1mode in (2, 3):
                        sync.dma_start(d_z.ap()[0:128, 0:8], sb_sp[c % 2][:, 0:8]).then_inc(spl.h, 16)
                        return
                    dst = d_z if c < 32 else d_xc
                    coff = c if c < 32 else min(c - 32, 31)
                    sync.dma_start(dst.ap()[coff * 128:(coff + 1) * 128, :],
                                   sb_sp[c % 2][:]).then_inc(spl.h, 16)
                em("sync", _f)
                spl.n += 16
                last_spill[c % 2] = spl.n
            elif c < 66:
                j2 = c - 32

                def _f(sc, c=c, ps=ps, mm_done=mm_done, acc_at=rec.get(("cvv", c - 2), 0)):
                    sc.wait_ge(tm.h, mm_done)
                    sc.wait_ge(vv.h, acc_at)
                    sc.activation(sb_acc[c % 2][:], ps[:], AF.Copy).then_inc(ss.h, 1)
                    sc.drain()
                    sc.nop().then_inc(psf.h, 1)
                em("scalar", _f)
                ss.n += 1
                psf.n += 1
                ev_done = ss.n

                def _f(v, c=c, j2=j2, ev_done=ev_done, silu_at=last_silu[c % 2]):
                    v.wait_ge(ss.h, max(ev_done, silu_at))
                    acc = sb_acc[c % 2]
                    cv = sb_cv[c % 2]
                    v.tensor_scalar(cv[:], acc[:], sb_cw[:, j2 * 4 + 3:j2 * 4 + 4],
                                    sb_cb[:, j2:j2 + 1], op0=ALU.mult, op1=ALU.add)
                    vop = None
                    for wt, off in [(2, 1), (1, 2), (0, 3)]:
                        vop = v.scalar_tensor_tensor(cv[:, off:T], acc[:, 0:T - off],
                                                     sb_cw[:, j2 * 4 + wt:j2 * 4 + wt + 1],
                                                     cv[:, off:T], op0=ALU.mult, op1=ALU.add)
                    vop.then_inc(vv.h, 1)
                em("vector", _f)
                vv.n += 1
                rec[("cvv", c)] = vv.n
                v_done = vv.n

                if c < 64:
                    def _f(sc, c=c, v_done=v_done, spl_at=last_spill[c % 2]):
                        sc.wait_ge(vv.h, v_done)
                        sc.wait_ge(spl.h, spl_at)
                        sc.activation(sb_sp[c % 2][:], sb_cv[c % 2][:], AF.Silu).then_inc(ss.h, 1)
                    em("scalar", _f)
                    ss.n += 1
                    last_silu[c % 2] = ss.n
                    s_done = ss.n

                    def _f(sync, c=c, s_done=s_done):
                        sync.wait_ge(ss.h, s_done)
                        sync.dma_start(d_xc.ap()[(c - 32) * 128:(c - 31) * 128, :],
                                       sb_sp[c % 2][:]).then_inc(spl.h, 16)
                    em("sync", _f)
                    spl.n += 16
                    last_spill[c % 2] = spl.n
                else:
                    dst = sb_B if c == 64 else sb_C

                    def _f(sc, c=c, v_done=v_done, dst=dst):
                        sc.wait_ge(vv.h, v_done)
                        sc.activation(dst[:], sb_cv[c % 2][:], AF.Silu).then_inc(ss.h, 1)
                    em("scalar", _f)
                    ss.n += 1
                    last_silu[c % 2] = ss.n
            elif p1mode == 0:
                def _f(sc, ps=ps, mm_done=mm_done):
                    sc.wait_ge(tm.h, mm_done)
                    sc.activation(sb_dt[:], ps[0:64, :], AF.Copy).then_inc(ss.h, 1)
                    sc.nop().then_inc(psf.h, 1)
                em("scalar", _f)
                ss.n += 1
                psf.n += 1
            else:
                def _f(sc, ps=ps, mm_done=mm_done):
                    sc.wait_ge(tm.h, mm_done)
                    sc.activation(sb_lncm[:], ps[0:64, :], AF.Copy).then_inc(ss.h, 1)
                    sc.drain()
                    sc.nop().then_inc(psf.h, 1)
                em("scalar", _f)
                ss.n += 1
                psf.n += 1
                s_ev66 = ss.n

                def _f(v, s_ev66=s_ev66):
                    v.wait_ge(ss.h, s_ev66)
                    v.tensor_scalar(sb_lncm[:], sb_lncm[:], sb_dtb[:], 80.0,
                                    op0=ALU.add, op1=ALU.min).then_inc(vv.h, 1)
                em("vector", _f)
                vv.n += 1
                v_t1 = vv.n

                def _f(sc, v_t1=v_t1):
                    sc.wait_ge(vv.h, v_t1)
                    if p1mode == 5:
                        sc.activation(sb_lncm[:], sb_lncm[:], AF.Copy)
                        sc.activation(sb_dt[:], sb_lncm[:], AF.Copy).then_inc(ss.h, 1)
                    else:
                        sc.activation(sb_lncm[:], sb_lncm[:], AF.Exp)
                        sc.activation(sb_dt[:], sb_lncm[:], AF.Ln,
                                      bias=sb_onef[0:64, :]).then_inc(ss.h, 1)
                em("scalar", _f)
                ss.n += 1
                sp_done = ss.n

                def _f(v, sp_done=sp_done):
                    v.wait_ge(ss.h, sp_done)
                    v.tensor_scalar_max(sb_lncm[:], sb_dt[:], 1e-38)
                    v.tensor_scalar(sb_adcm[:], sb_dt[:], sb_acol[0:64, :], -1.0,
                                    op0=ALU.mult, op1=ALU.mult).then_inc(vv.h, 1)
                em("vector", _f)
                vv.n += 1
                v_done = vv.n

                def _f(sc, v_done=v_done):
                    sc.wait_ge(vv.h, v_done)
                    if p1mode == 5:
                        sc.activation(sb_lncm[:], sb_lncm[:], AF.Copy).then_inc(ss.h, 1)
                    else:
                        sc.activation(sb_lncm[:], sb_lncm[:], AF.Ln).then_inc(ss.h, 1)
                em("scalar", _f)
                ss.n += 1
                ln_done = ss.n

                def _f(t, ln_done=ln_done, psf_at=psf.n):
                    t.wait_ge(ss.h, ln_done)
                    t.wait_ge(psf.h, psf_at)
                    if p1mode != 4:
                        tr = None
                        for i in range(32):
                            t.transpose(ps_a[0:64, i * 64:(i + 1) * 64],
                                        sb_lncm[:, i * 64:(i + 1) * 64],
                                        sb_identf[0:64, 0:64])
                            tr = t.transpose(ps_b[0:64, i * 64:(i + 1) * 64],
                                             sb_adcm[:, i * 64:(i + 1) * 64],
                                             sb_identf[0:64, 0:64])
                        tr.then_inc(tm.h, 1)
                    else:
                        t.nop().then_inc(tm.h, 1)
                em("tensor", _f)
                tm.n += 1
                tr_done = tm.n

                def _f(sc, tr_done=tr_done):
                    sc.wait_ge(tm.h, tr_done)
                    if p1mode == 4:
                        sc.nop().then_inc(ss.h, 1)
                        return
                    sc.activation(sb_lndt_tok[:], ps_a[0:64, 0:2048], AF.Copy)
                    sc.activation(sb_adt_tok[:], ps_b[0:64, 0:2048], AF.Copy).then_inc(ss.h, 1)
                em("scalar", _f)
                ss.n += 1

        def _f(g, spl_at=spl.n):
            g.wait_ge(spl.h, spl_at)
        em("gpsimd", _f)
        barrier()
        p1.close()

        MUTE[0] = upto < 2

        p2 = contextlib.ExitStack()
        sb_ddiag = p2.enter_context(nc.sbuf_tensor("c_ddiag", [64, DI], BF16))
        sb_S = p2.enter_context(nc.sbuf_tensor("S_state", [128, DI], BF16))
        sb_xtok = [p2.enter_context(nc.sbuf_tensor(f"xtok{i}", [128, DI], BF16))
                   for i in range(2)]
        sb_xch = p2.enter_context(nc.sbuf_tensor("xch", [64, DI], BF16))
        sb_zg = [p2.enter_context(nc.sbuf_tensor(f"zg{i}", [128, 4096], BF16))
                 for i in range(2)]
        sb_yg = [p2.enter_context(nc.sbuf_tensor(f"yg{i}", [128, 4096], BF16))
                 for i in range(2)]
        sb_arep = [p2.enter_context(nc.sbuf_tensor(f"arep{i}", [64, DI], F32))
                   for i in range(2)]
        sb_cbr = p2.enter_context(nc.sbuf_tensor("cbr", [64, DI], BF16))
        sb_dac = [p2.enter_context(nc.sbuf_tensor(f"dac{i}", [64, 1], F32))
                  for i in range(2)]
        sb_dax = [p2.enter_context(nc.sbuf_tensor(f"dax{i}", [128, 64], F32))
                  for i in range(2)]
        sb_wtok = [p2.enter_context(nc.sbuf_tensor(f"wtok{i}", [64, 64], F32))
                   for i in range(2)]
        sb_ea = [p2.enter_context(nc.sbuf_tensor(f"ea{i}", [128, 2048], BF16))
                 for i in range(2)]
        sb_btok = [p2.enter_context(nc.sbuf_tensor(f"btok{i}", [64, 128], BF16))
                   for i in range(2)]
        sb_front = [p2.enter_context(nc.sbuf_tensor(f"front{i}", [64, 576], F32))
                    for i in range(2)]
        sb_dneg = p2.enter_context(nc.sbuf_tensor("dneg", [64, DI], BF16))
        sb_sct = sb_dneg
        sb_xdd = p2.enter_context(nc.sbuf_tensor("xdd", [64, DI], BF16))
        sb_yf = p2.enter_context(nc.sbuf_tensor("yf", [128, 2048], BF16))
        sb_ev = p2.enter_context(nc.sbuf_tensor("yev", [128, 2048], BF16))
        sb_stb = [p2.enter_context(nc.sbuf_tensor(f"stb{i}", [128, 512], F32))
                  for i in range(2)]
        sb_zs = p2.enter_context(nc.sbuf_tensor("zs", [128, 2048], BF16))
        sb_sq = sb_zs
        sb_ms = p2.enter_context(nc.sbuf_tensor("msrow", [1, 192], F32))
        sb_sev = p2.enter_context(nc.sbuf_tensor("sevrow", [1, 512], F32))
        sb_wrow = p2.enter_context(nc.sbuf_tensor("wrow", [128, 2048], BF16))
        sb_wst = p2.enter_context(nc.sbuf_tensor("wst", [128, 48], F32))
        sb_wq8p = p2.enter_context(nc.sbuf_tensor("wq8p", [128, 128], FP8))

        def _f(sync):
            sync.dma_start(sb_ddiag[:], d_ddiag.ap()).then_inc(sy.h, 16)
        em("sync", _f)
        sy.n += 16

        PS_ST = [ps_b[:, 0:512], ps_b[:, 512:1024]]
        PS_ACT = ps_b[0:64, 1024:1088]
        PS_ACM = ps_b[0:64, 1088:1152]
        PS_CBT = ps_b[64:128, 1024:1088]
        PS_BT = ps_b.bitcast(BF16)[64:128, 2304:2432]
        PS_DAX = ps_b[:, 1152:1216]
        PS_WT = ps_b[0:64, 1216:1280]
        PS_RMS = ps_b[0:1, 1536:2048]

        def _f(v):
            v.memset(sb_S[:], 0.0).then_inc(vv.h, 1)
        em("vector", _f)
        vv.n += 1
        s_ready = [vv.n]
        stfree_base = stfree.n
        b6_base = b6.n
        b7_base = b7.n
        yfree_base = yfree.n

        def front(k):
            par = k % 2
            fb = sb_front[par]
            A_TOK, A_CM = fb[:, 0:64], fb[:, 64:128]
            A2, DEC = fb[:, 128:192], fb[:, 192:256]
            WCM, EAs, CBTM = fb[:, 256:320], fb[:, 320:384], fb[:, 384:448]

            rec[("xtok", k)] = ss.n
            if k % 2 == 1 and (k + 1) // 2 < NCH // 2:
                g0 = (k + 1) // 2
                def _f(sc, g0=g0, xch_at=rec.get(("xch", 2 * g0 - 3), 0)):
                    sc.wait_ge(ss.h, xch_at)
                    sc.dma_start_transpose(
                        sb_xtok[g0 % 2][:],
                        d_xc.ap()[:, g0 * 128:(g0 + 1) * 128]).then_inc(ss.h, 16)
                em("scalar", _f)
                ss.n += 16
                def _f(sc, g0=g0, vv_at=rec.get(("zs_done", 2 * g0 - 3), 0)):
                    sc.wait_ge(vv.h, vv_at)
                    for j in range(32):
                        sc.dma_start(
                            sb_zg[g0 % 2][:, j * 128:(j + 1) * 128],
                            d_z.ap()[j * 128:(j + 1) * 128,
                                     g0 * 128:(g0 + 1) * 128]).then_inc(ss.h, 16)
                em("scalar", _f)
                ss.n += 32 * 16

            def _f(t, k=k, b6_at=b6_base + 3 * k):
                t.wait_ge(b6.h, b6_at)
                la = sb_adt_tok[:, k * 64:(k + 1) * 64]
                t.matmul(PS_ACT[:, :], sb_ut[:], la, start=True, stop=True)
                t.matmul(PS_ACM[:, :], la, sb_ut[:], start=True, stop=True)
                t.matmul(PS_CBT[:, :], sb_B[:, k * 64:(k + 1) * 64],
                         sb_C[:, k * 64:(k + 1) * 64], start=True, stop=True)
                t.transpose(PS_BT[:, 0:128], sb_B[:, k * 64:(k + 1) * 64],
                            sb_identb[:]).then_inc(tm.h, 1)
            em("tensor", _f)
            tm.n += 1
            t1 = tm.n

            def _f(sc, k=k, t1=t1, A_TOK=A_TOK, A_CM=A_CM, DEC=DEC, EAs=EAs, fb=fb):
                sc.wait_ge(tm.h, t1)
                sc.activation(A_TOK[:, :], PS_ACT[:, :], AF.Copy)
                sc.activation(A_CM[:, :], PS_ACM[:, :], AF.Copy)
                sc.activation(fb[:, 512:576], PS_CBT[:, :], AF.Copy)
                sc.activation(sb_btok[k % 2][:], PS_BT[:, 0:128], AF.Copy)
                sc.activation(DEC[:, :], A_CM[:, :], AF.Exp, scale=-1.0,
                              bias=A_CM[:, 63:64])
                sc.activation(EAs[:, :], A_CM[:, :], AF.Exp)
                sc.activation(sb_dac[k % 2][:], A_CM[:, 63:64], AF.Exp).then_inc(ss.h, 1)
                sc.drain()
                sc.nop().then_inc(b6.h, 1)
            em("scalar", _f)
            ss.n += 1
            b6.n += 1
            s1 = ss.n

            def _f(v, k=k, s1=s1, A_TOK=A_TOK, A2=A2, DEC=DEC, WCM=WCM, CBTM=CBTM, fb=fb):
                v.wait_ge(ss.h, s1)
                lnd = sb_lndt_tok[:, k * 64:(k + 1) * 64]
                v.tensor_sub(A2[:, :], A_TOK[:, :], lnd)
                v.tensor_mul(WCM[:, :], sb_dt[:, k * 64:(k + 1) * 64], DEC[:, :])
                v.tensor_scalar_mul(fb[:, 448:512], sb_identf[0:64, 0:64],
                                    sb_dac[k % 2][:])
                v.tensor_mul(CBTM[:, :], fb[:, 512:576], sb_ut[:]).then_inc(vv.h, 1)
                v.drain()
                v.nop().then_inc(b6.h, 1)
            em("vector", _f)
            vv.n += 1
            b6.n += 1
            v1 = vv.n

            def _f(t, k=k, v1=v1, s1=s1, WCM=WCM, fb=fb):
                t.wait_ge(vv.h, v1)
                t.wait_ge(ss.h, s1)
                t.matmul(PS_DAX[:, :], sb_onesf[:], fb[:, 448:512],
                         start=True, stop=True)
                t.transpose(PS_WT[:, :], WCM[:, :],
                            sb_identf[0:64, 0:64]).then_inc(tm.h, 1)
            em("tensor", _f)
            tm.n += 1
            t2 = tm.n

            def _f(sc, k=k, t2=t2):
                sc.wait_ge(tm.h, t2)
                sc.activation(sb_dax[k % 2][:], PS_DAX[:, :], AF.Copy)
                sc.activation(sb_wtok[k % 2][:], PS_WT[:, :], AF.Copy).then_inc(ss.h, 1)
                sc.drain()
                sc.nop().then_inc(b6.h, 1)
            em("scalar", _f)
            ss.n += 1
            b6.n += 1

            def _f(sync, k=k, par=par, s1=s1, v1=v1, A_CM=A_CM, EAs=EAs,
                   WCM=WCM, CBTM=CBTM, sy0=sy.n):
                sync.wait_ge(ss.h, s1)
                sync.wait_ge(vv.h, v1)
                sync.dma_start(d_arep.ap()[par, :].rearrange("(h l) -> h l", h=64), A_CM[:, :]).then_inc(sy.h, 16)
                sync.dma_start(d_ea.ap()[par, :].rearrange("(h l) -> h l", h=64), EAs[:, :]).then_inc(sy.h, 16)
                sync.dma_start(d_cbm.ap()[par, :].rearrange("(s l) -> s l", s=64), CBTM[:, :]).then_inc(sy.h, 16)
            em("sync", _f)
            sy.n += 3 * 16
            rec[("flats", k)] = sy.n

            def _f(g, k=k, par=par, sy_at=rec[("flats", k)],
                   yf_at=rec.get(("yf", k - 2), 0)):
                g.wait_ge(sy.h, sy_at)
                g.wait_ge(vv.h, yf_at)
                g.dma_start(
                    sb_ea[par][0:64, :].rearrange("p (j l) -> p j l", l=64),
                    bass.AP(d_ea, par * DI, [[0, 64], [128, 32], [1, 64]])).then_inc(gp.h, 16)
                g.dma_start(
                    sb_ea[par][64:128, :].rearrange("p (j l) -> p j l", l=64),
                    bass.AP(d_ea, par * DI + 64, [[0, 64], [128, 32], [1, 64]])).then_inc(gp.h, 16)
            em("gpsimd", _f)
            gp.n += 2 * 16
            rec[("gp_front", k)] = gp.n

        def main(c):
            par = c % 2
            fb = sb_front[par]
            A2 = fb[:, 128:192]
            half = (c // 2) % 2
            rows = slice((c % 2) * 64, (c % 2) * 64 + 64)

            def _f(sync, c=c, fl_at=rec[("flats", c)],
                   dneg_at=rec.get(("dneg", c - 2), 0)):
                sync.wait_ge(sy.h, fl_at)
                sync.wait_ge(vv.h, dneg_at)
                sync.dma_start(
                    sb_arep[c % 2][:].rearrange("p (h l) -> p h l", h=64),
                    d_arep.ap()[c % 2, :].rearrange("(h l) -> h l", h=64)
                    .unsqueeze(0).broadcast_to((64, 64, 64))).then_inc(sy.h, 16)
            em("sync", _f)
            sy.n += 16
            rec[("sync_front", c)] = sy.n

            def _f(g, c=c, fl_at=rec[("flats", c)],
                   sct_at=rec.get(("sct", c - 1), 0)):
                g.wait_ge(sy.h, fl_at)
                g.wait_ge(vv.h, sct_at)
                g.dma_start(
                    sb_cbr[:].rearrange("p (h l) -> p h l", h=64),
                    d_cbm.ap()[c % 2, :].rearrange("(s l) -> s l", s=64)
                    .unsqueeze(1).broadcast_to((64, 64, 64))).then_inc(gp.h, 16)
            em("gpsimd", _f)
            gp.n += 16
            rec[("cbr", c)] = gp.n

            def _f(sc, c=c, half=half, rows=rows, xt_at=rec[("xtok", c)],
                   tyd_at=rec.get(("t_ydiag", c - 1), 0),
                   xdd_at=rec.get(("xdd", c - 1), 0)):
                sc.wait_ge(ss.h, xt_at)
                sc.wait_ge(tm.h, tyd_at)
                sc.wait_ge(vv.h, xdd_at)
                sc.dma_start(sb_xch[:], sb_xtok[half][rows, :]).then_inc(ss.h, 16)
            em("scalar", _f)
            ss.n += 16
            rec[("xch", c)] = ss.n

            def _f(v, c=c, A2=A2, sy_at=rec[("sync_front", c)],
                   tyd_at=rec.get(("t_ydiag", c - 1), 0)):
                v.wait_ge(sy.h, sy_at)
                v.wait_ge(tm.h, tyd_at)
                v.tensor_sub(sb_dneg[:].rearrange("p (h l) -> p h l", h=64),
                             A2[:, :].unsqueeze(2).broadcast_to((64, 64, 64)),
                             sb_arep[c % 2][:].rearrange("p (h l) -> p h l", h=64))
                v.tensor_scalar_max(sb_dneg[:], sb_dneg[:], -80.0).then_inc(vv.h, 1)
            em("vector", _f)
            vv.n += 1
            rec[("dneg", c)] = vv.n
            v_dneg = vv.n

            def _f(sc, v_dneg=v_dneg):
                sc.wait_ge(vv.h, v_dneg)
                sc.activation(sb_dneg[:], sb_dneg[:], AF.Exp, scale=-1.0).then_inc(ss.h, 1)
            em("scalar", _f)
            ss.n += 1
            s_E = ss.n

            def _f(v, c=c, s_E=s_E, gp_at=rec[("cbr", c)],
                   xch_at=rec[("xch", c)],
                   tyd_at=rec.get(("t_ydiag", c - 1), 0),
                   tst_at=rec.get(("stmm", c - 1, 7), 0)):
                v.wait_ge(ss.h, s_E)
                v.wait_ge(gp.h, gp_at)
                v.wait_ge(tm.h, max(tyd_at, tst_at))
                v.tensor_mul(sb_sct[:], sb_dneg[:], sb_cbr[:])
                v.tensor_add(sb_sct[:], sb_sct[:], sb_ddiag[:]).then_inc(vv.h, 1)
                v.wait_ge(ss.h, xch_at)
                v.tensor_mul(sb_xdd[:].rearrange("p (h q) -> p h q", h=64),
                             sb_xch[:].rearrange("p (h q) -> p h q", h=64),
                             sb_wtok[c % 2][:].unsqueeze(2)
                             .broadcast_to((64, 64, 64))).then_inc(vv.h, 1)
            em("vector", _f)
            vv.n += 2
            rec[("sct", c)] = vv.n - 1
            rec[("xdd", c)] = vv.n
            v_sct = vv.n - 1
            v_xdd = vv.n

            def _f(t, c=c, yf_at=yfree_base + c, sv=s_ready[0]):
                t.wait_ge(yfree.h, yf_at)
                t.wait_ge(vv.h, sv)
                mm = None
                for j in range(32):
                    mm = t.matmul(ps_a[:, j * 64:(j + 1) * 64],
                                  sb_S[:, j * 128:(j + 1) * 128],
                                  sb_C[:, c * 64:(c + 1) * 64], start=True, stop=True)
                mm.then_inc(tm.h, 1)
            em("tensor", _f)
            tm.n += 1
            t_yoff = tm.n

            def _f(sc, c=c, t_yoff=t_yoff, yrd_at=rec.get(("v_yadd", c - 1), 0)):
                sc.wait_ge(tm.h, t_yoff)
                sc.wait_ge(vv.h, yrd_at)
                sc.activation(sb_ev[:], ps_a[:], AF.Copy).then_inc(ss.h, 1)
            em("scalar", _f)
            ss.n += 1
            s_yoev = ss.n

            def _f(v, c=c, par=par, s_yoev=s_yoev, gp_at=rec[("gp_front", c)]):
                v.wait_ge(ss.h, s_yoev)
                v.wait_ge(gp.h, gp_at)
                v.tensor_mul(sb_yf[:], sb_ev[:], sb_ea[par][:]).then_inc(vv.h, 1)
            em("vector", _f)
            vv.n += 1
            v_yoff = vv.n

            def _f(t, c=c, v_sct=v_sct, s_yoev=s_yoev):
                t.wait_ge(vv.h, v_sct)
                t.wait_ge(ss.h, s_yoev)
                mm = None
                for j in range(32):
                    for hh in (2 * j, 2 * j + 1):
                        mm = t.matmul(ps_a[(hh % 2) * 64:(hh % 2) * 64 + 64,
                                           j * 64:(j + 1) * 64],
                                      sb_xch[:, hh * 64:(hh + 1) * 64],
                                      sb_sct[:, hh * 64:(hh + 1) * 64],
                                      start=True, stop=True)
                mm.then_inc(tm.h, 1)
            em("tensor", _f)
            tm.n += 1
            t_ydiag = tm.n
            rec[("t_ydiag", c)] = t_ydiag

            for sl in range(8):
                def _f(t, c=c, sl=sl, v_xdd=v_xdd,
                       st_at=stfree_base + max(0, c * 8 + sl - 1)):
                    if sl == 0:
                        t.wait_ge(vv.h, v_xdd)
                    t.wait_ge(stfree.h, st_at)
                    t.matmul(PS_ST[sl % 2][:, :], sb_btok[c % 2][:],
                             sb_xdd[:, sl * 512:(sl + 1) * 512],
                             start=True, stop=True).then_inc(tm.h, 1)
                em("tensor", _f)
                tm.n += 1
                rec[("stmm", c, sl)] = tm.n

            def _f(sc, c=c, t_ydiag=t_ydiag, yo_at=v_yoff):
                sc.wait_ge(tm.h, t_ydiag)
                sc.wait_ge(vv.h, yo_at)
                sc.activation(sb_ev[:], ps_a[:], AF.Copy).then_inc(yfree.h, 1)
            em("scalar", _f)
            yfree.n += 1
            s_ydev = yfree.n

            def _f(v, c=c, s_ydev=s_ydev):
                v.wait_ge(yfree.h, s_ydev)
                v.tensor_add(sb_yf[:], sb_yf[:], sb_ev[:]).then_inc(vv.h, 1)
            em("vector", _f)
            vv.n += 1
            v_yf = vv.n
            rec[("yf", c)] = v_yf
            rec[("v_yadd", c)] = v_yf

            def _f(sc, c=c, stmm={sl: rec[("stmm", c, sl)] for sl in range(8)},
                   vb=vv.n):
                for sl in range(8):
                    sc.wait_ge(tm.h, stmm[sl])
                    if sl >= 2:
                        sc.wait_ge(vv.h, vb + sl - 1)
                    sc.activation(sb_stb[sl % 2][:], PS_ST[sl % 2][:, :],
                                  AF.Copy).then_inc(stfree.h, 1)
            em("scalar", _f)
            sb0 = stfree.n
            stfree.n += 8

            def _f(v, c=c, par=par, sb0=sb0):
                for sl in range(8):
                    v.wait_ge(stfree.h, sb0 + sl + 1)
                    ssl = sb_S[:, sl * 512:(sl + 1) * 512]
                    v.tensor_mul(ssl.rearrange("p (h q) -> p h q", q=64), 
                                 ssl.rearrange("p (h q) -> p h q", q=64),
                                 sb_dax[par][:, sl * 8:(sl + 1) * 8].unsqueeze(2)
                                 .broadcast_to((128, 8, 64)))
                    v.tensor_add(ssl, ssl, sb_stb[sl % 2][:]).then_inc(vv.h, 1)
            em("vector", _f)
            supdv0 = vv.n
            vv.n += 8
            s_ready[0] = vv.n
            rec[("supd", c)] = vv.n
            rec[("supdv", c)] = supdv0

            zoff = (c % 2) * 64
            zgb = sb_zg[(c // 2) % 2]

            def _f(sc, c=c, zoff=zoff, zgb=zgb, z_at=rec[("xtok", (c // 2) * 2)]):
                sc.wait_ge(ss.h, z_at)
                sc.activation(sb_zs[:].rearrange("p (j l) -> p j l", l=64),
                              bass.AP(zgb, zoff, [[4096, 128], [128, 32], [1, 64]]),
                              AF.Silu).then_inc(ss.h, 1)
            em("scalar", _f)
            ss.n += 1
            s_sg = ss.n

            def _f(v, c=c, s_sg=s_sg, v_yf=v_yf):
                v.wait_ge(ss.h, s_sg)
                v.wait_ge(vv.h, v_yf)
                v.tensor_mul(sb_yf[:], sb_yf[:], sb_zs[:]).then_inc(vv.h, 1)
            em("vector", _f)
            vv.n += 1
            v_yf2 = vv.n
            rec[("zs_done", c)] = vv.n

            def _f(sc, c=c, v_yf2=v_yf2, trms_at=rec.get(("t_rms", c - 1), 0)):
                sc.wait_ge(vv.h, v_yf2)
                sc.wait_ge(tm.h, trms_at)
                sc.activation(sb_sq[:], sb_yf[:], AF.Square).then_inc(ss.h, 1)
            em("scalar", _f)
            ss.n += 1
            s_sq = ss.n

            def _f(t, c=c, s_sq=s_sq, b7_at=b7_base + c):
                t.wait_ge(ss.h, s_sq)
                t.wait_ge(b7.h, b7_at)
                mm = None
                for g4 in range(4):
                    mm = t.matmul(PS_RMS[:, :], sb_ones[:],
                                  sb_sq[:, g4 * 512:(g4 + 1) * 512],
                                  start=(g4 == 0), stop=(g4 == 3))
                mm.then_inc(tm.h, 1)
            em("tensor", _f)
            tm.n += 1
            t_rms = tm.n
            rec[("t_rms", c)] = t_rms

            def _f(sc, t_rms=t_rms):
                sc.wait_ge(tm.h, t_rms)
                sc.activation(sb_sev[:], PS_RMS[:, :], AF.Copy).then_inc(b7.h, 1)
            em("scalar", _f)
            b7.n += 1
            s_ev = b7.n

            def _f(v, s_ev=s_ev):
                v.wait_ge(b7.h, s_ev)
                v.tensor_reduce(sb_ms[:, 0:64].unsqueeze(2),
                                bass.AP(sb_sev, 0, [[512, 1], [1, 64], [64, 8]]),
                                axis=mybir.AxisListType.X, op=ALU.add)
                v.drain()
                v.nop().then_inc(vv.h, 1)
            em("vector", _f)
            vv.n += 1
            v_ms = vv.n

            def _f(sc, v_ms=v_ms):
                sc.wait_ge(vv.h, v_ms)
                sc.activation(sb_ms[:, 64:128], sb_ms[:, 0:64], AF.Sqrt,
                              scale=1.0 / 4096.0, bias=sb_eps[0:1, :]).then_inc(ss.h, 1)
            em("scalar", _f)
            ss.n += 1
            s_srt = ss.n

            def _f(v, c=c, s_srt=s_srt, rsp_at=rec.get(("rrspill", c - 1), 0)):
                v.wait_ge(ss.h, s_srt)
                v.wait_ge(gp.h, rsp_at)
                v.reciprocal(sb_ms[:, 128:192], sb_ms[:, 64:128])
                v.drain()
                v.nop().then_inc(vv.h, 1)
            em("vector", _f)
            vv.n += 1
            v_rr = vv.n

            def _f(g, c=c, v_rr=v_rr):
                g.wait_ge(vv.h, v_rr)
                g.dma_start(d_rr.ap()[c * 64:(c + 1) * 64].unsqueeze(0),
                            sb_ms[:, 128:192]).then_inc(gp.h, 16)
            em("gpsimd", _f)
            gp.n += 16
            rec[("rrspill", c)] = gp.n

            ygb = sb_yg[(c // 2) % 2]

            def _f(v, c=c, ygb=ygb, v_yf2=v_yf2,
                   sy_at=rec.get(("yspill", c // 2 - 2), 0)):
                v.wait_ge(sy.h, sy_at)
                v.tensor_mul(
                    bass.AP(ygb, (c % 2) * 64, [[4096, 128], [128, 32], [1, 64]]),
                    sb_yf[:].rearrange("p (j l) -> p j l", l=64),
                    sb_nw[:].unsqueeze(2).broadcast_to((128, 32, 64))).then_inc(vv.h, 1)
            em("vector", _f)
            vv.n += 1
            rec[("ynrm", c)] = vv.n

            if c % 2 == 1:
                def _f(sync, c=c, ygb=ygb, v_at=rec[("ynrm", c)]):
                    sync.wait_ge(vv.h, v_at)
                    sync.dma_start(
                        bass.AP(d_y, (c - 1) * 64,
                                [[T, 128], [128 * T, 32], [1, 128]]),
                        ygb[:].rearrange("p (j l) -> p j l", l=128)).then_inc(sy.h, 16)
                em("sync", _f)
                sy.n += 16
                rec[("yspill", c // 2)] = sy.n

        def wq_quant(j):
            def _f(g, j=j, sp_at=rec.get(("wqspill", j - 1), 0)):
                g.wait_ge(sy.h, sp_at)
                g.dma_start(sb_wrow[:],
                            d_wffn.ap()[j * 128:(j + 1) * 128, :]).then_inc(gp.h, 16)
            em("gpsimd", _f)
            gp.n += 16
            w_at = gp.n

            def _f(v, j=j, w_at=w_at):
                v.wait_ge(gp.h, w_at)
                v.tensor_reduce(sb_wst[:, 0:16].unsqueeze(2),
                                sb_wrow[:].rearrange("p (s f) -> p s f", s=16),
                                axis=mybir.AxisListType.X, op=ALU.max,
                                apply_absolute_value=True)
                v.drain()
                v.tensor_scalar_mul(sb_wst[:, 32:48], sb_wst[:, 0:16], 1.0 / 224.0)
                v.drain()
                v.reciprocal(sb_wst[:, 16:32], sb_wst[:, 32:48])
                v.drain()
                v.nop().then_inc(vv.h, 1)
            em("vector", _f)
            vv.n += 1
            v_w = vv.n

            def _f(sc, j=j, v_w=v_w):
                sc.wait_ge(vv.h, v_w)
                sop = None
                for sgi in range(8):
                    sc.activation(sb_wq8p[:],
                                  sb_wrow[:, sgi * 128:(sgi + 1) * 128],
                                  AF.Copy, scale=sb_wst[:, 16 + sgi:17 + sgi])
                    sop = sc.activation(sb_wrow[:, sgi * 128:(sgi + 1) * 128],
                                        sb_wq8p[:],
                                        AF.Copy, scale=sb_wst[:, 32 + sgi:33 + sgi])
                sop.then_inc(ss.h, 1)
            em("scalar", _f)
            ss.n += 1
            rec[("wqh", j)] = ss.n

        def wq_quant2(j):
            def _f(sc, j=j, h_at=rec[("wqh", j)]):
                sc.wait_ge(ss.h, h_at)
                sop = None
                for sgi in range(8, 16):
                    sc.activation(sb_wq8p[:],
                                  sb_wrow[:, sgi * 128:(sgi + 1) * 128],
                                  AF.Copy, scale=sb_wst[:, 16 + sgi:17 + sgi])
                    sop = sc.activation(sb_wrow[:, sgi * 128:(sgi + 1) * 128],
                                        sb_wq8p[:],
                                        AF.Copy, scale=sb_wst[:, 32 + sgi:33 + sgi])
                sop.then_inc(ss.h, 1)
            em("scalar", _f)
            ss.n += 1
            s_w = ss.n

            def _f(sync, j=j, s_w=s_w):
                sync.wait_ge(ss.h, s_w)
                sync.dma_start(d_wdq.ap()[j * 128:(j + 1) * 128, :],
                               sb_wrow[:]).then_inc(sy.h, 16)
            em("sync", _f)
            sy.n += 16
            rec[("wqspill", j)] = sy.n

        def _f(sc, gp_at=gp.n):
            sc.wait_ge(gp.h, gp_at)
            sc.dma_start_transpose(sb_xtok[0][:],
                                   d_xc.ap()[:, 0:128]).then_inc(ss.h, 16)
            for j in range(32):
                sc.dma_start(sb_zg[0][:, j * 128:(j + 1) * 128],
                             d_z.ap()[j * 128:(j + 1) * 128, 0:128]).then_inc(ss.h, 16)
        em("scalar", _f)
        ss.n += 33 * 16
        rec[("xtok", 0)] = ss.n

        front(0)
        for c in range(NCH):
            if c + 1 < NCH:
                front(c + 1)
            main(c)
            if c % 2 == 0 and c // 2 < 16:
                wq_quant(c // 2)
            if c % 2 == 1 and c // 2 < 16:
                wq_quant2(c // 2)

        def _f(g, gp_at=gp.n):
            g.wait_ge(gp.h, gp_at)
        em("gpsimd", _f)
        barrier()
        p2.close()
        p12.close()

        if upto < 3:
            MUTE[0] = True

        p3 = contextlib.ExitStack()
        sb_wout = p3.enter_context(nc.sbuf_tensor("woutb", [128, 32 * 2048], BF16))
        sb_ygr = [p3.enter_context(nc.sbuf_tensor(f"ygr{i}", [128, 32 * 128], BF16))
                  for i in range(2)]
        sb_xin = p3.enter_context(nc.sbuf_tensor("xin", [128, 2048], BF16))
        sb_rr2 = p3.enter_context(nc.sbuf_tensor("rr2", [128, 16], F32))
        sb_r1 = p3.enter_context(nc.sbuf_tensor("r1", [128, 2048], F32))
        sb_x2f = p3.enter_context(nc.sbuf_tensor("x2f", [128, 2048], F32))
        sb_st1 = p3.enter_context(nc.sbuf_tensor("st1", [128, 72], F32))
        sb_q8 = p3.enter_context(nc.sbuf_tensor("q8t", [128, 128], FP8))
        sb_x2b = p3.enter_context(nc.sbuf_tensor("x2b", [128, 2048], BF16))
        sb_x2qb = p3.enter_context(nc.sbuf_tensor("x2qb", [128, 2048], BF16))
        sb_wqs = p3.enter_context(nc.sbuf_tensor("wqstage", [128, 2048], BF16))

        for j in range(32):
            def _f(g, j=j):
                g.dma_start(sb_wout[:, j * 2048:(j + 1) * 2048],
                            d_wout.ap()[j * 128:(j + 1) * 128, :]).then_inc(gp.h, 16)
            em("gpsimd", _f)
            gp.n += 16
        wout_at = gp.n

        def _f(sync, rr_done=rec[("rrspill", 31)]):
            sync.wait_ge(gp.h, rr_done)
            sync.dma_start(sb_rr2[:], bass.AP(d_rr, 0, [[1, 128], [128, 16]]), allow_slow_non_contiguous=True).then_inc(sy.h, 16)
        em("sync", _f)
        sy.n += 16
        rr_at = sy.n

        psf_base3 = psf.n
        tm_base3 = tm.n
        x2sp = {}
        for tg in range(16):
            ps = ps_a if tg % 2 == 0 else ps_b

            def _f(sync, tg=tg, tm_at=tm_base3 + max(0, tg - 1)):
                sync.wait_ge(tm.h, tm_at)
                for j in range(32):
                    sync.dma_start(
                        sb_ygr[tg % 2][:, j * 128:(j + 1) * 128],
                        d_y.ap()[j * 128:(j + 1) * 128,
                                 tg * 128:(tg + 1) * 128]).then_inc(sy.h, 16)
            em("sync", _f)
            sy.n += 32 * 16
            rec[("yrd", tg)] = sy.n

            def _f(g, tg=tg, v_at=rec.get(("p3r1", tg - 1), 0)):
                g.wait_ge(vv.h, v_at)
                g.dma_start(sb_xin[:],
                            d_u.ap()[tg * 128:(tg + 1) * 128, :]).then_inc(gp.h, 16)
            em("gpsimd", _f)
            gp.n += 16
            xin_at = gp.n

            def _f(sc, tg=tg, sp_at=max(rec.get(("wqTsp", tg - 1), 0),
                                         rec[("wqspill", 15)] if tg == 0 else 0)):
                sc.wait_ge(sy.h, sp_at)
                sc.dma_start_transpose(
                    sb_wqs[:],
                    d_wdq.ap()[:, tg * 128:(tg + 1) * 128]).then_inc(ss.h, 16)
            em("scalar", _f)
            ss.n += 16
            wqt_at = ss.n

            def _f(sync, tg=tg, wqt_at=wqt_at):
                sync.wait_ge(ss.h, wqt_at)
                sync.dma_start(d_wqT.ap()[tg * 128:(tg + 1) * 128, :],
                               sb_wqs[:]).then_inc(sy.h, 16)
            em("sync", _f)
            sy.n += 16
            rec[("wqTsp", tg)] = sy.n

            def _f(t, tg=tg, ps=ps, y_at=rec[("yrd", tg)], wo_at=wout_at,
                   psf_at=psf_base3 + max(0, tg - 1)):
                t.wait_ge(sy.h, y_at)
                if tg == 0:
                    t.wait_ge(gp.h, wo_at)
                t.wait_ge(psf.h, psf_at)
                mm = None
                for j in range(32):
                    lhsT = sb_ygr[tg % 2][:, j * 128:(j + 1) * 128]
                    for nb in range(4):
                        mm = t.matmul(ps[:, nb * 512:(nb + 1) * 512], lhsT,
                                      sb_wout[:, j * 2048 + nb * 512:j * 2048 + (nb + 1) * 512],
                                      start=(j == 0), stop=(j == 31))
                mm.then_inc(tm.h, 1)
            em("tensor", _f)
            tm.n += 1
            mm3 = tm.n

            def _f(sc, tg=tg, ps=ps, mm3=mm3, r1_at=rec.get(("p3r1", tg - 1), 0)):
                sc.wait_ge(tm.h, mm3)
                sc.wait_ge(vv.h, r1_at)
                sc.activation(sb_r1[:], ps[:], AF.Copy)
                sc.drain()
                sc.nop().then_inc(psf.h, 1)
            em("scalar", _f)
            psf.n += 1
            s_r1 = psf.n

            def _f(v, tg=tg, s_r1=s_r1, xin_at=xin_at, rr_at=rr_at):
                v.wait_ge(psf.h, s_r1)
                v.wait_ge(gp.h, xin_at)
                v.wait_ge(sy.h, rr_at)
                v.scalar_tensor_tensor(sb_r1[:], sb_r1[:], sb_rr2[:, tg:tg + 1],
                                       sb_xin[:], op0=ALU.mult, op1=ALU.add)
                v.reduce_sum(sb_st1[:, 0:1], sb_r1[:], axis=mybir.AxisListType.X)
                v.drain()
                v.tensor_scalar_mul(sb_st1[:, 1:2], sb_st1[:, 0:1], 1.0 / 2048.0)
                v.drain()
                v.nop().then_inc(vv.h, 1)
            em("vector", _f)
            vv.n += 1
            rec[("p3r1", tg)] = vv.n
            v_m = vv.n

            def _f(sc, v_m=v_m):
                sc.wait_ge(vv.h, v_m)
                sc.activation(sb_x2f[:], sb_r1[:], AF.Square,
                              bias=sb_st1[:, 1:2], scale=-1.0,
                              accum_out=sb_st1[:, 2:3]).then_inc(ss.h, 1)
            em("scalar", _f)
            ss.n += 1
            s_v = ss.n

            def _f(sc, s_v=s_v):
                sc.wait_ge(ss.h, s_v)
                sc.activation(sb_st1[:, 3:4], sb_st1[:, 2:3], AF.Sqrt,
                              scale=1.0 / 2048.0, bias=sb_eps[:, :])
                sc.drain()
                sc.nop().then_inc(ss.h, 1)
            em("scalar", _f)
            ss.n += 1
            s_sd = ss.n

            def _f(v, tg=tg, s_sd=s_sd, g_at=x2sp.get(tg - 1, 0),
                   sq_at=rec.get(("x2q_s", tg - 1), 0)):
                v.wait_ge(ss.h, max(s_sd, sq_at))
                v.wait_ge(gp.h, g_at)
                v.tensor_scalar(sb_x2f[:], sb_r1[:], sb_st1[:, 1:2], 1.0,
                                op0=ALU.subtract, op1=ALU.mult)
                v.tensor_reduce(sb_st1[:, 6:22].unsqueeze(2),
                                sb_x2f[:].rearrange("p (s f) -> p s f", s=16),
                                axis=mybir.AxisListType.X, op=ALU.max,
                                apply_absolute_value=True)
                v.drain()
                v.tensor_scalar_mul(sb_st1[:, 38:54], sb_st1[:, 6:22], 1.0 / 224.0)
                v.drain()
                v.reciprocal(sb_st1[:, 5:6], sb_st1[:, 3:4])
                v.reciprocal(sb_st1[:, 22:38], sb_st1[:, 38:54])
                v.drain()
                v.nop().then_inc(vv.h, 1)
            em("vector", _f)
            vv.n += 1
            v_q = vv.n

            def _f(sc, tg=tg, v_q=v_q):
                sc.wait_ge(vv.h, v_q)
                sc.activation(sb_st1[:, 54:70], sb_st1[:, 38:54], AF.Copy,
                              scale=sb_st1[:, 5:6])
                for sgi in range(16):
                    sc.activation(sb_q8[:],
                                  sb_x2f[:, sgi * 128:(sgi + 1) * 128], AF.Copy,
                                  scale=sb_st1[:, 22 + sgi:23 + sgi])
                    sc.activation(sb_x2qb[:, sgi * 128:(sgi + 1) * 128],
                                  sb_q8[:], AF.Copy,
                                  scale=sb_st1[:, 54 + sgi:55 + sgi])
                sc.activation(sb_x2b[:], sb_x2f[:], AF.Copy,
                              scale=sb_st1[:, 5:6])
                sc.drain()
                sc.nop().then_inc(ss.h, 1)
            em("scalar", _f)
            ss.n += 1
            s_x2b = ss.n
            rec[("x2q_s", tg)] = s_x2b

            def _f(g, tg=tg, s_x2b=s_x2b):
                g.wait_ge(ss.h, s_x2b)
                g.dma_start(d_x2.ap()[tg * 128:(tg + 1) * 128, :],
                            sb_x2b[:]).then_inc(gp.h, 16)
                g.dma_start(d_x2q.ap()[tg * 128:(tg + 1) * 128, :],
                            sb_x2qb[:]).then_inc(gp.h, 16)
            em("gpsimd", _f)
            gp.n += 32
            x2sp[tg] = gp.n

        barrier()
        p3.close()

        if upto < 4:
            MUTE[0] = True

        p4 = contextlib.ExitStack()
        sb_wq = p4.enter_context(nc.sbuf_tensor("wqT", [128, 16 * 2048], BF16))
        sb_st4 = p4.enter_context(nc.sbuf_tensor("st4", [128, 60], F32))
        sb_xq = [p4.enter_context(nc.sbuf_tensor(f"xqT{i}", [128, 2048], BF16))
                 for i in range(2)]
        sb_x2r = [p4.enter_context(nc.sbuf_tensor(f"x2r{i}", [128, 2048], BF16))
                  for i in range(2)]
        sb_r2 = p4.enter_context(nc.sbuf_tensor("r2", [128, 2048], F32))
        sb_sq4 = p4.enter_context(nc.sbuf_tensor("sq4", [128, 2048], F32))
        sb_o4 = [p4.enter_context(nc.sbuf_tensor(f"o4_{i}", [128, 2048], F32))
                 for i in range(2)]

        for kc in range(16):
            def _f(sync, kc=kc, sp_at=rec[("wqTsp", 15)] if kc == 0 else 0):
                sync.wait_ge(sy.h, sp_at)
                sync.dma_start(sb_wq[:, kc * 2048:(kc + 1) * 2048],
                               d_wqT.ap()[kc * 128:(kc + 1) * 128, :]).then_inc(sy.h, 16)
            em("sync", _f)
            sy.n += 16
        wq_ld_at = sy.n

        psf_base4 = psf.n
        tm_base4 = tm.n
        outsp = {}
        for tg in range(16):
            ps = ps_a if tg % 2 == 0 else ps_b

            def _f(sync, tg=tg, tm_at=tm_base4 + max(0, tg - 1)):
                sync.wait_ge(tm.h, tm_at)
                for kc in range(16):
                    sync.dma_start_transpose(
                        sb_xq[tg % 2][:, kc * 128:(kc + 1) * 128],
                        d_x2q.ap()[tg * 128:(tg + 1) * 128,
                                   kc * 128:(kc + 1) * 128]).then_inc(sy.h, 16)
            em("sync", _f)
            sy.n += 16 * 16
            xq_at = sy.n

            def _f(sync, tg=tg, v_at=rec.get(("p4r2", tg - 1), 0)):
                sync.wait_ge(vv.h, v_at)
                sync.dma_start(sb_x2r[tg % 2][:],
                               d_x2.ap()[tg * 128:(tg + 1) * 128, :]).then_inc(sy.h, 16)
            em("sync", _f)
            sy.n += 16
            x2_at = sy.n

            def _f(t, tg=tg, ps=ps, xq_at=xq_at, wq_ld_at=wq_ld_at,
                   psf_at=psf_base4 + max(0, tg - 1)):
                t.wait_ge(sy.h, max(xq_at, wq_ld_at))
                t.wait_ge(psf.h, psf_at)
                mm = None
                for kc in range(16):
                    lhsT = sb_xq[tg % 2][:, kc * 128:(kc + 1) * 128]
                    for nb in range(4):
                        mm = t.matmul(ps[:, nb * 512:(nb + 1) * 512], lhsT,
                                      sb_wq[:, kc * 2048 + nb * 512:kc * 2048 + (nb + 1) * 512],
                                      start=(kc == 0), stop=(kc == 15))
                mm.then_inc(tm.h, 1)
            em("tensor", _f)
            tm.n += 1
            mm4 = tm.n

            def _f(sc, tg=tg, ps=ps, mm4=mm4, r2_at=rec.get(("p4r2", tg - 1), 0)):
                sc.wait_ge(tm.h, mm4)
                sc.wait_ge(vv.h, r2_at)
                sc.activation(sb_r2[:], ps[:], AF.Copy)
                sc.drain()
                sc.nop().then_inc(psf.h, 1)
            em("scalar", _f)
            psf.n += 1
            s_r2 = psf.n

            def _f(v, tg=tg, s_r2=s_r2, x2_at=x2_at):
                v.wait_ge(psf.h, s_r2)
                v.wait_ge(sy.h, x2_at)
                v.tensor_add(sb_r2[:], sb_r2[:], sb_x2r[tg % 2][:])
                v.reduce_sum(sb_st4[:, 32:33], sb_r2[:], axis=mybir.AxisListType.X)
                v.drain()
                v.tensor_scalar_mul(sb_st4[:, 33:34], sb_st4[:, 32:33], 1.0 / 2048.0)
                v.drain()
                v.nop().then_inc(vv.h, 1)
            em("vector", _f)
            vv.n += 1
            rec[("p4r2", tg)] = vv.n
            v_m4 = vv.n

            def _f(sc, v_m4=v_m4):
                sc.wait_ge(vv.h, v_m4)
                sc.activation(sb_sq4[:], sb_r2[:], AF.Square,
                              bias=sb_st4[:, 33:34], scale=-1.0,
                              accum_out=sb_st4[:, 34:35])
                sc.activation(sb_st4[:, 36:37], sb_st4[:, 34:35], AF.Sqrt,
                              scale=1.0 / 2048.0, bias=sb_eps[:, :])
                sc.drain()
                sc.nop().then_inc(ss.h, 1)
            em("scalar", _f)
            ss.n += 1
            s_sd4 = ss.n

            def _f(v, tg=tg, s_sd4=s_sd4, g_at=outsp.get(tg - 2, 0)):
                v.wait_ge(ss.h, s_sd4)
                v.wait_ge(gp.h, g_at)
                v.tensor_scalar(sb_sq4[:], sb_r2[:], sb_st4[:, 33:34], 1.0,
                                op0=ALU.subtract, op1=ALU.mult)
                v.reciprocal(sb_st4[:, 37:38], sb_st4[:, 36:37])
                v.drain()
                v.nop().then_inc(vv.h, 1)
            em("vector", _f)
            vv.n += 1
            v_o = vv.n

            def _f(sc, tg=tg, v_o=v_o):
                sc.wait_ge(vv.h, v_o)
                sc.activation(sb_o4[tg % 2][:], sb_sq4[:], AF.Copy,
                              scale=sb_st4[:, 37:38])
                sc.drain()
                sc.nop().then_inc(ss.h, 1)
            em("scalar", _f)
            ss.n += 1
            s_o4 = ss.n

            def _f(g, tg=tg, s_o4=s_o4):
                g.wait_ge(ss.h, s_o4)
                g.dma_start(d_out.ap()[tg * 128:(tg + 1) * 128, :],
                            sb_o4[tg % 2][:]).then_inc(gp.h, 16)
            em("gpsimd", _f)
            gp.n += 16
            outsp[tg] = gp.n

        def _f(g, gp_at=gp.n):
            g.wait_ge(gp.h, gp_at)
        em("gpsimd", _f)
        barrier()
        p4.close()

        global _DBG
        _DBG = {k: v.n for k, v in S.items()}
        with nc.Block() as block:
            @block.sync
            def _(sync):
                for fn in prog["sync"]:
                    fn(sync)

            @block.tensor
            def _(t):
                for fn in prog["tensor"]:
                    fn(t)

            @block.vector
            def _(v):
                for fn in prog["vector"]:
                    fn(v)

            @block.scalar
            def _(sc):
                for fn in prog["scalar"]:
                    fn(sc)

            @block.gpsimd
            def _(g):
                for fn in prog["gpsimd"]:
                    fn(g)

    return nc


_DBG = {}
_NC_CACHE = None


def _get_nc():
    global _NC_CACHE
    if _NC_CACHE is None:
        _NC_CACHE = build_program()
    return _NC_CACHE


def _host_constants(inputs):
    D = np.asarray(inputs["L0_attn_D_param"], np.float32)
    ddiag = np.zeros((64, DI), np.float32)
    for h in range(64):
        ddiag[np.arange(64), h * 64 + np.arange(64)] = D[h]
    conv_w = np.asarray(inputs["L0_attn_conv_w"], np.float32)
    conv_b = np.asarray(inputs["L0_attn_conv_b"], np.float32)
    cw = conv_w.reshape(34, 128, 4).transpose(1, 0, 2).reshape(128, 136)
    cb = conv_b.reshape(34, 128).T.copy()
    nw = np.asarray(inputs["L0_attn_norm_w"], np.float32).reshape(32, 128).T.copy()
    ut = np.triu(np.ones((64, 64), np.float32))
    return {
        "w_in": np.ascontiguousarray(np.asarray(inputs["L0_attn_W_in"], np.float32)),
        "w_out": np.ascontiguousarray(np.asarray(inputs["L0_attn_W_out"], np.float32)),
        "w_ffn": np.ascontiguousarray(np.asarray(inputs["L0_ffn_weight"], np.float32)),
        "cw": np.ascontiguousarray(cw),
        "cb": np.ascontiguousarray(cb),
        "dtb": np.asarray(inputs["L0_attn_dt_bias"], np.float32).reshape(64, 1),
        "alog": np.asarray(inputs["L0_attn_A_log"], np.float32).reshape(64, 1),
        "ddiag": ddiag.astype(BF),
        "nw": np.ascontiguousarray(nw),
        "identb": np.eye(128, dtype=np.float32).astype(BF),
        "identf": np.eye(128, dtype=np.float32),
        "ut": np.ascontiguousarray(ut),
        "ones128": np.ones((128, 1), np.float32).astype(BF),
        "onesrow": np.ones((1, 128), np.float32).astype(BF),
        "onesf": np.ones((64, 128), np.float32),
    }


def kernel(**inputs):
    nc = _get_nc()
    const = _host_constants(inputs)
    u = np.ascontiguousarray(np.asarray(inputs["L0_attn_u"], np.float32))
    in_maps = []
    for core in range(8):
        m = dict(const)
        m["u"] = u[core]
        in_maps.append(m)
    res = run_bass_kernel_spmd(nc, in_maps, core_ids=list(range(8)))
    out = np.stack([np.asarray(r["out"], np.float32) for r in res.results], axis=0)
    return out
